# version 07
# baseline (speedup 1.0000x reference)
import numpy as np
import ml_dtypes
from contextlib import ExitStack

import concourse.bass as bass
import concourse.mybir as mybir
from concourse.bass_utils import run_bass_kernel_spmd

F32 = mybir.dt.float32
BF16 = mybir.dt.bfloat16
AF = mybir.ActivationFunctionType
ALU = mybir.AluOpType
AX = mybir.AxisListType

NCORES = 8
T = 2048
NT = 16
D = 1024
NK = 8
H = 16
DH = 64
NE = 20
HEPS = 1e-6
LNEPS = 1e-5
STOP_AFTER = 3
STOP_MICRO = None
DEBUG_DUMPS = False
DBG = {}


class StopEmission(Exception):
    pass


class Cnt:
    def __init__(self):
        self.v = 0

    def inc(self, n=1):
        self.v += n
        return self.v


def build_program():
    nc = bass.Bass("TRN2", target_bir_lowering=False, debug=False)

    xin = nc.dram_tensor("xin", [T, D], F32, kind="ExternalInput").ap()
    qkvw = nc.dram_tensor("qkvw", [3 * D, D], F32, kind="ExternalInput").ap()
    qkvb = nc.dram_tensor("qkvb", [1, 3 * D], F32, kind="ExternalInput").ap()
    outw = nc.dram_tensor("outw", [D, D], F32, kind="ExternalInput").ap()
    outb = nc.dram_tensor("outb", [1, D], F32, kind="ExternalInput").ap()
    wg1d = nc.dram_tensor("wg1", [D, 256], F32, kind="ExternalInput").ap()
    bg1d = nc.dram_tensor("bg1", [1, 256], F32, kind="ExternalInput").ap()
    wg2d = nc.dram_tensor("wg2", [256, 40], F32, kind="ExternalInput").ap()
    bg2d = nc.dram_tensor("bg2", [1, 40], F32, kind="ExternalInput").ap()
    wed = nc.dram_tensor("wed", [NE * D, D], F32, kind="ExternalInput").ap()
    bed = nc.dram_tensor("bed", [NE, D], F32, kind="ExternalInput").ap()
    identf_d = nc.dram_tensor("identf", [128, 128], F32, kind="ExternalInput").ap()
    identb_d = nc.dram_tensor("identb", [128, 128], BF16, kind="ExternalInput").ap()
    triu_d = nc.dram_tensor("triu", [128, 128], BF16, kind="ExternalInput").ap()
    maskrow_d = nc.dram_tensor("maskrow", [1, 8], F32, kind="ExternalInput").ap()
    ydram = nc.dram_tensor("y", [T, D], F32, kind="ExternalOutput").ap()
    if DEBUG_DUMPS:
        dbg_lnT = nc.dram_tensor("dbg_lnT", [128, NK * T], F32, kind="ExternalOutput").ap()
        dbg_hT = nc.dram_tensor("dbg_hT", [128, 2 * T], F32, kind="ExternalOutput").ap()
        dbg_gates = nc.dram_tensor("dbg_gates", [128, NT * NE], F32, kind="ExternalOutput").ap()
        dbg_misc = nc.dram_tensor("dbg_misc", [128, 256], F32, kind="ExternalOutput").ap()
        dbg_att = nc.dram_tensor("dbg_att", [128, NT * D], F32, kind="ExternalOutput").ap()
        dbg_den = nc.dram_tensor("dbg_den", [128, 256], F32, kind="ExternalOutput").ap()
        dbg_st = nc.dram_tensor("dbg_st", [64, 4 * H * 65], F32, kind="ExternalOutput").ap()

    qkvT_hbm = nc.dram_tensor("qkvT_hbm", [NK, 128, 3 * D], BF16)
    xspill = nc.dram_tensor("xspill", [128, NT * D], F32)
    contrib_dram = nc.dram_tensor("contrib_dram", [64, H * 65], F32)
    gathered_dram = nc.dram_tensor("gathered_dram", [NCORES * 64, H * 65], F32)

    B0 = (nc.sbuf_base + 31) // 32 * 32
    A = B0
    B = A + 66560
    C = B + 32768
    Dr = C + 49152
    E = Dr + 16384

    at = lambda name, shape, dt, offset: nc.alloc_sbuf_tensor_at(name, shape, dt, offset=offset).ap()

    xsb = at("xsb", [128, NT * D], F32, A)
    attnum = at("attnum", [128, NT * D], BF16, A + 32768)
    U_sb = at("U_sb", [64, 2048], F32, A)
    contrib = at("contrib", [64, H * 65], F32, A + 8192)
    den_all = at("den_all", [128, 256], F32, A + 12416)
    rec = at("rec", [128, 32], F32, A + 13440)
    gbuf = [at(f"gbuf{i}", [64, H * 65], F32, A + i * 4192) for i in range(2)]
    qT2 = at("qT2", [64, T], BF16, A + 13568)
    kT2 = at("kT2", [64, T], BF16, A + 17664)
    vT2 = at("vT2", [64, T], BF16, A + 21760)
    ktok2 = at("ktok2", [128, 16 * 64], BF16, A + 25856)
    v_aug2 = at("v_aug2", [128, 16 * 65], BF16, A + 27904)
    sown2 = at("sown2", [64, 16 * 65], BF16, A + 30016)

    lnT = at("lnT", [128, NK * T], BF16, B)
    asm = [at(f"asm{i}", [128, 3 * D], BF16, B + i * 6144) for i in range(2)]
    qkvb_row = at("qkvb_row", [1, 3 * D], F32, B + 12288)
    bg1row = at("bg1row", [1, 256], F32, B + 24576)

    stage = at("stage", [128, 24 * D], BF16, C)
    web = [at(f"web{i}", [128, NK * D], BF16, C + i * 16384) for i in range(2)]
    hT = at("hT", [128, 2 * T], BF16, C + 32768)
    qwt = at("qwt", [128, NK * 3 * D], BF16, C)
    xstage = [at(f"xstage{i}", [128, D], F32, C + i * 4096) for i in range(2)]
    attT = at("attT", [128, D], BF16, C + 8192)

    owt = at("owt", [128, NK * D], BF16, Dr)

    off = [E]

    def tal(name, shape, dt):
        per = int(np.prod(shape[1:])) * mybir.dt.size(dt)
        o = (off[0] + 31) // 32 * 32
        off[0] = o + per
        return at(name, shape, dt, o)

    identf = tal("identfs", [128, 128], F32)
    identb = tal("identbs", [128, 128], BF16)
    triu = tal("trius", [128, 128], BF16)
    wg1b = tal("wg1b", [128, NK * 256], BF16)
    wg2b = tal("wg2b", [128, 2 * 40], BF16)
    beb = tal("beb", [NE, D], BF16)
    qkvb64 = tal("qkvb64", [64, 48], F32)
    outbbf = tal("outbbf", [1, D], BF16)
    bg2row = tal("bg2row", [1, 40], BF16)
    onesbf = tal("onesbf", [1, 512], BF16)
    onesf = tal("onesf", [1, 64], F32)
    maskrow = tal("maskrow_sb", [1, 8], F32)
    maskcol = tal("maskcol", [64, 8], F32)
    gates_all = tal("gates_all", [128, NT * NE], F32)
    gatesT = tal("gatesT", [NE, NT * 128], BF16)
    stats = tal("stats", [128, 64], F32)
    lnstage = tal("lnstage", [128, D], BF16)
    probs = tal("probs", [128, 64], F32)
    m8sb = tal("m8sb", [128, 8], F32)
    bg1cols = tal("bg1cols", [128, 2], F32)
    qT_h = tal("qT_h", [64, T], BF16)
    kT_h = tal("kT_h", [64, T], BF16)
    vT_h = tal("vT_h", [64, T], BF16)
    ktok = tal("ktok", [128, 16 * 64], BF16)
    v_aug = tal("v_aug", [128, 16 * 65], BF16)
    sown = tal("sown", [64, 16 * 65], BF16)
    S0 = tal("S0", [64, H * 65], F32)
    S0bf = tal("S0bf", [64, H * 65], BF16)
    sacc = tal("sacc", [64, 72], F32)
    numtot = tal("numtot", [128, 512], F32)
    epscol = tal("epscol", [128, 1], F32)
    atm2 = tal("atm2", [128, 256], BF16)
    assert off[0] <= 229340, f"tail overflow {off[0]}"

    with ExitStack() as ctx:
        pbig0 = ctx.enter_context(nc.psum_tensor("pbig0", [128, 1024], F32))
        pbig1 = ctx.enter_context(nc.psum_tensor("pbig1", [128, 1024], F32))
        psm0 = ctx.enter_context(nc.psum_tensor("psm0", [128, 512], F32))
        psm1 = ctx.enter_context(nc.psum_tensor("psm1", [128, 512], F32))
        pbf0 = ctx.enter_context(nc.psum_tensor("pbf0", [128, 1024], BF16))
        pbf1 = ctx.enter_context(nc.psum_tensor("pbf1", [128, 1024], BF16))
        pbig = [pbig0, pbig1]
        psm = [psm0, psm1]
        pbf = [pbf0, pbf1]

        semPE = ctx.enter_context(nc.semaphore("semPE"))
        semDVE = ctx.enter_context(nc.semaphore("semDVE"))
        semACT = ctx.enter_context(nc.semaphore("semACT"))
        semIN = ctx.enter_context(nc.semaphore("semIN"))
        semW = ctx.enter_context(nc.semaphore("semW"))
        semSP = ctx.enter_context(nc.semaphore("semSP"))
        semGP = ctx.enter_context(nc.semaphore("semGP"))
        semCC = ctx.enter_context(nc.semaphore("semCC"))
        block = ctx.enter_context(nc.Block())

        cPE, cDVE, cACT = Cnt(), Cnt(), Cnt()
        cIN, cW, cSP, cGP, cCC = Cnt(), Cnt(), Cnt(), Cnt(), Cnt()

        t_, v_, a_, sy, gp = nc.tensor, nc.vector, nc.scalar, nc.sync, nc.gpsimd

        def mpe(i):
            i.then_inc(semPE, 1)
            return cPE.inc()

        def mdve(i):
            i.then_inc(semDVE, 1)
            return cDVE.inc()

        def mact(i):
            i.then_inc(semACT, 1)
            return cACT.inc()

        def win(i):
            i.then_inc(semIN, 16)
            return cIN.inc(16)

        def ww(i):
            i.then_inc(semW, 16)
            return cW.inc(16)

        def wsp(i):
            i.then_inc(semSP, 16)
            return cSP.inc(16)

        def wgp(i):
            i.then_inc(semGP, 16)
            return cGP.inc(16)

        def wall(eng):
            for sem, c in ((semPE, cPE), (semDVE, cDVE), (semACT, cACT), (semIN, cIN),
                           (semW, cW), (semSP, cSP), (semGP, cGP), (semCC, cCC)):
                if c.v:
                    eng.wait_ge(sem, c.v)

        @block.sync
        def _(sync):
          try:
            def micro(n):
                if STOP_MICRO is not None and STOP_MICRO == n:
                    raise StopEmission()

            win(sy.dma_start(out=xsb.rearrange("p (t d) -> p t d", d=D),
                             in_=xin.rearrange("(t p) d -> p t d", p=128)))
            micro(1)
            win(sy.dma_start(out=identf, in_=identf_d))
            win(sy.dma_start(out=identb, in_=identb_d))
            win(sy.dma_start(out=triu, in_=triu_d))
            win(sy.dma_start(out=maskrow, in_=maskrow_d))
            win(sy.dma_start(out=qkvb_row, in_=qkvb))
            win(sy.dma_start(out=bg1row, in_=bg1d))
            in_all = cIN.v

            ww(gp.dma_start(out=wg1b.rearrange("p (k f) -> p k f", k=NK),
                            in_=wg1d.rearrange("(k p) f -> p k f", p=128)))
            ww(gp.dma_start(out=wg2b.rearrange("p (k f) -> p k f", k=2),
                            in_=wg2d.rearrange("(k p) f -> p k f", p=128)))
            ww(gp.dma_start(out=beb, in_=bed))
            ww(gp.dma_start(out=outbbf, in_=outb))
            ww(gp.dma_start(out=bg2row, in_=bg2d))
            w_outw = ww(gp.dma_start(out=stage[:, 0:8 * D].rearrange("p (i d) -> p i d", i=8),
                                     in_=outw.rearrange("(i p) d -> p i d", p=128)))
            micro(2)

            v_.memset(onesbf, 1.0)
            v_.memset(onesf, 1.0)
            m_ones = mdve(v_.memset(v_aug.rearrange("p (c s) -> p c s", s=65)[:, :, 64:65], 1.0))
            micro(3)

            t_.wait_ge(semIN, in_all)
            t_.wait_ge(semDVE, m_ones)
            for c in range(3):
                for h in range(H):
                    j = c * 16 + h
                    t_.matmul(psm0[0:64, j:j + 1],
                              qkvb_row[0:1, c * D + h * DH: c * D + (h + 1) * DH],
                              onesf[0:1, 0:1], start=True, stop=True)
            t_.matmul(psm1[0:128, 0:1], bg1row[0:1, 0:128], onesf[0:1, 0:1],
                      start=True, stop=True)
            t_.matmul(psm1[0:128, 1:2], bg1row[0:1, 128:256], onesf[0:1, 0:1],
                      start=True, stop=True)
            m_cols = mpe(t_.matmul(psm1[0:64, 8:16], onesf[0:1, 0:64], maskrow[0:1, 0:8],
                                   start=True, stop=True))
            v_.wait_ge(semPE, m_cols)
            m_cp1 = mdve(v_.tensor_copy(qkvb64, psm0[0:64, 0:48]))
            v_.tensor_copy(bg1cols, psm1[0:128, 0:2])
            v_.tensor_copy(maskcol, psm1[0:64, 8:16])
            v_.wait_ge(semDVE, m_cp1)
            m_bias = mdve(v_.tensor_scalar(qkvb64[:, 0:32], qkvb64[:, 0:32], HEPS, None, ALU.add))
            micro(4)

            t_.wait_ge(semW, w_outw)
            owt_packs = []
            for k in range(NK):
                pb = pbf[k % 2]
                if len(owt_packs) >= 2:
                    t_.wait_ge(semDVE, owt_packs[-2])
                for i in range(8):
                    mm = t_.transpose(pb[:, i * 128:(i + 1) * 128],
                                      stage[:, i * D + k * 128: i * D + (k + 1) * 128], identb)
                mk = mpe(mm)
                v_.wait_ge(semPE, mk)
                owt_packs.append(mdve(v_.tensor_copy(owt[:, k * D:(k + 1) * D], pb[:, :])))
            m_owt_done_pe = cPE.v
            micro(5)

            gp.wait_ge(semPE, m_owt_done_pe)
            for part in range(3):
                w_qkvw = ww(gp.dma_start(
                    out=stage[:, part * 8 * D:(part + 1) * 8 * D].rearrange("p (i d) -> p i d", i=8),
                    in_=qkvw[part * 8 * 128:(part + 1) * 8 * 128, :].rearrange("(i p) d -> p i d", p=128)))

            micro(51)
            t_.wait_ge(semW, w_qkvw)
            asm_store = [0, 0]
            pack_ms = []
            for k in range(NK):
                ab = asm[k % 2]
                if asm_store[k % 2]:
                    t_.wait_ge(semSP, asm_store[k % 2])
                for ip in range(3):
                    pb = pbf[(k * 3 + ip) % 2]
                    if len(pack_ms) >= 2:
                        t_.wait_ge(semDVE, pack_ms[-2])
                    for i2 in range(8):
                        i = ip * 8 + i2
                        mm = t_.transpose(pb[:, i2 * 128:(i2 + 1) * 128],
                                          stage[:, i * D + k * 128: i * D + (k + 1) * 128], identb)
                    mk = mpe(mm)
                    v_.wait_ge(semPE, mk)
                    pack_ms.append(mdve(v_.tensor_copy(ab[:, ip * 1024:(ip + 1) * 1024], pb[:, :])))
                if STOP_MICRO == 52 and k == 0:
                    raise StopEmission()
                sy.wait_ge(semDVE, pack_ms[-1])
                asm_store[k % 2] = wsp(sy.dma_start(out=qkvT_hbm.ap()[k], in_=ab))
                if STOP_MICRO == 53 and k == 0:
                    raise StopEmission()
            micro(6)

            def ln_emit():
                pbf_free = [0, 0]
                lnT_done = 0
                for t in range(NT):
                    xt = xsb[:, t * D:(t + 1) * D]
                    sc = stats[:, (t % 2) * 8:(t % 2) * 8 + 8]
                    negm, var, sd, rstd = sc[:, 2:3], sc[:, 3:4], sc[:, 4:5], sc[:, 5:6]
                    bn = stats[:, 16 + (t % 2) * 16: 16 + (t % 2) * 16 + 12]
                    v_.bn_stats(bn[:, 0:6], xt[:, 0:512])
                    m_b1 = mdve(v_.bn_stats(bn[:, 6:12], xt[:, 512:1024]))
                    v_.wait_ge(semDVE, m_b1)
                    m_b2 = mdve(v_.bn_aggr(sc[:, 0:2], bn[:, 0:12]))
                    v_.wait_ge(semDVE, m_b2)
                    v_.tensor_scalar_mul(negm, sc[:, 0:1], -1.0)
                    m_var = mdve(v_.tensor_scalar_add(var, sc[:, 1:2], LNEPS))
                    a_.wait_ge(semDVE, m_var)
                    m_sd = mact(a_.activation(sd, var, AF.Sqrt))
                    v_.wait_ge(semACT, m_sd)
                    m_rc = mdve(v_.reciprocal(rstd, sd))
                    v_.wait_ge(semDVE, m_rc)
                    m_ln = mdve(v_.tensor_scalar(lnstage, xt, negm, rstd, ALU.add, ALU.mult))
                    pb = pbf[t % 2]
                    t_.wait_ge(semDVE, m_ln)
                    if pbf_free[t % 2]:
                        t_.wait_ge(semDVE, pbf_free[t % 2])
                    for k in range(NK):
                        mm = t_.transpose(pb[:, k * 128:(k + 1) * 128],
                                          lnstage[:, k * 128:(k + 1) * 128], identb)
                    m_tp = mpe(mm)
                    t_.matmul(pbig1[:, 0:512], identb, lnstage[:, 0:512], start=True, stop=True)
                    t_.matmul(pbig1[:, 0:512], identb, lnstage[:, 0:512], start=True, stop=True)
                    v_.wait_ge(semPE, m_tp)
                    lnT_done = mdve(v_.tensor_copy(
                        lnT.rearrange("p (k s) -> p k s", k=NK)[:, :, t * 128:(t + 1) * 128],
                        pb.rearrange("p (k s) -> p k s", k=NK)))
                    pbf_free[t % 2] = lnT_done
                return lnT_done

            def ffn_emit():
                wall(t_); wall(v_); wall(a_); wall(gp)

                web_loaded = {}
                for e in range(2):
                    web_loaded[e] = ww(gp.dma_start(
                        out=web[e].rearrange("p (k f) -> p k f", k=NK),
                        in_=wed[e * D:(e + 1) * D, :].rearrange("(k p) f -> p k f", p=128)))

                lnT_done = ln_emit()

                t_.wait_ge(semDVE, lnT_done)
                h_acts = []
                for fc in range(2):
                    for nb in range(4):
                        ps = psm[nb % 2]
                        if len(h_acts) >= 2:
                            t_.wait_ge(semACT, h_acts[-2])
                        for k in range(NK):
                            mm = t_.matmul(ps[:, :],
                                           wg1b[:, k * 256 + fc * 128: k * 256 + fc * 128 + 128],
                                           lnT[:, k * T + nb * 512: k * T + (nb + 1) * 512],
                                           start=(k == 0), stop=(k == NK - 1))
                        mk = mpe(mm)
                        a_.wait_ge(semPE, mk)
                        h_acts.append(mact(a_.activation(
                            hT[:, fc * T + nb * 512: fc * T + (nb + 1) * 512], ps[:, :],
                            AF.Gelu, bias=bg1cols[:, fc:fc + 1])))
                h_all = cACT.v

                t_.wait_ge(semACT, h_all)
                exp_read = 0
                m8_read = 0
                gT_evac = 0
                for t in range(NT):
                    if exp_read:
                        t_.wait_ge(semACT, exp_read)
                    for fc in range(2):
                        t_.matmul(psm0[:, 0:20],
                                  hT[:, fc * T + t * 128: fc * T + (t + 1) * 128],
                                  wg2b[:, fc * 40: fc * 40 + 20],
                                  start=(fc == 0), stop=False)
                    m_lg = mpe(t_.matmul(psm0[:, 0:20], onesbf[0:1, 0:128], bg2row[0:1, 0:20],
                                         start=False, stop=True))
                    v_.wait_ge(semPE, m_lg)
                    m_nm = mdve(v_.tensor_reduce(stats[:, 48:49], psm0[:, 0:20], AX.X, ALU.max,
                                                 negate=True))
                    a_.wait_ge(semDVE, m_nm)
                    exp_read = mact(a_.activation(probs[:, 0:20], psm0[:, 0:20], AF.Exp,
                                                  bias=stats[:, 48:49]))
                    v_.wait_ge(semACT, exp_read)
                    m_rs = mdve(v_.tensor_reduce(stats[:, 49:50], probs[:, 0:20], AX.X, ALU.add))
                    v_.wait_ge(semDVE, m_rs)
                    m_rc = mdve(v_.reciprocal(stats[:, 50:51], stats[:, 49:50]))
                    v_.wait_ge(semDVE, m_rc)
                    m_ps = mdve(v_.tensor_scalar(probs[:, 32:52], probs[:, 0:20],
                                                 stats[:, 50:51], 0.5, ALU.mult, ALU.mult))
                    v_.wait_ge(semDVE, m_ps)
                    m_m8 = mdve(v_.max(m8sb[:, 0:8], probs[:, 32:52]))
                    v_.wait_ge(semDVE, m_m8)
                    m8_read = mdve(v_.scalar_tensor_tensor(
                        gates_all[:, t * NE:(t + 1) * NE], probs[:, 32:52], m8sb[:, 1:2],
                        probs[:, 32:52], ALU.is_ge, ALU.mult))
                    t_.wait_ge(semDVE, m8_read)
                    if gT_evac:
                        t_.wait_ge(semDVE, gT_evac)
                    m_gt = mpe(t_.transpose(psm1[0:20, 0:128],
                                            gates_all[:, t * NE:(t + 1) * NE], identf))
                    v_.wait_ge(semPE, m_gt)
                    gT_evac = mdve(v_.tensor_copy(gatesT[:, t * 128:(t + 1) * 128],
                                                  psm1[0:20, 0:128]))

                micro(10)
                bank_free = [0, 0]
                t_.wait_ge(semDVE, gT_evac)
                for t in range(NT):
                    pb = pbig[t % 2]
                    if bank_free[t % 2]:
                        t_.wait_ge(semDVE, bank_free[t % 2])
                    t_.matmul(pb[:, 0:512], gatesT[:, t * 128:(t + 1) * 128], beb[:, 0:512],
                              start=True, stop=True)
                    m_be = mpe(t_.matmul(pb[:, 512:1024], gatesT[:, t * 128:(t + 1) * 128],
                                         beb[:, 512:1024], start=True, stop=True))
                    v_.wait_ge(semPE, m_be)
                    v_.tensor_add(xsb[:, t * D: t * D + 512], xsb[:, t * D: t * D + 512],
                                  pb[:, 0:512])
                    bank_free[t % 2] = mdve(v_.tensor_add(
                        xsb[:, t * D + 512: t * D + 1024], xsb[:, t * D + 512: t * D + 1024],
                        pb[:, 512:1024]))

                micro(11)
                stt_m = {}
                t_.wait_ge(semACT, exp_read)
                ebanks = [(pbig0[:, 0:512], pbig0[:, 512:1024]),
                          (pbig1[:, 0:512], pbig1[:, 512:1024]),
                          (psm0[:, 0:512], psm1[:, 0:512])]
                for e in range(NE):
                    t_.wait_ge(semW, web_loaded[e])
                    mm_last = 0
                    for t in range(NT):
                        idx = e * NT + t
                        bh = ebanks[idx % 3]
                        if idx - 3 >= 0:
                            t_.wait_ge(semDVE, stt_m[idx - 3])
                        elif bank_free[idx % 2]:
                            t_.wait_ge(semDVE, bank_free[idx % 2])
                        for k in range(NK):
                            for nb in range(2):
                                mm = t_.matmul(bh[nb],
                                               lnT[:, k * T + t * 128: k * T + (t + 1) * 128],
                                               web[e % 2][:, k * D + nb * 512: k * D + (nb + 1) * 512],
                                               start=(k == 0), stop=(k == NK - 1))
                        mm_last = mpe(mm)
                        v_.wait_ge(semPE, mm_last)
                        stt_m[idx] = mdve(v_.scalar_tensor_tensor(
                            xsb[:, t * D: t * D + 512], bh[0],
                            gates_all[:, t * NE + e: t * NE + e + 1],
                            xsb[:, t * D: t * D + 512], ALU.mult, ALU.add))
                        stt_m[idx] = mdve(v_.scalar_tensor_tensor(
                            xsb[:, t * D + 512:(t + 1) * D], bh[1],
                            gates_all[:, t * NE + e: t * NE + e + 1],
                            xsb[:, t * D + 512:(t + 1) * D], ALU.mult, ALU.add))
                    if e + 2 < NE:
                        gp.wait_ge(semPE, mm_last)
                        web_loaded[e + 2] = ww(gp.dma_start(
                            out=web[e % 2].rearrange("p (k f) -> p k f", k=NK),
                            in_=wed[(e + 2) * D:(e + 3) * D, :].rearrange("(k p) f -> p k f", p=128)))

            if STOP_AFTER >= 1:
                ffn_emit()

            if STOP_AFTER < 2:
                raise StopEmission()
            wall(t_); wall(v_); wall(a_); wall(gp); wall(sy)

            lnT_done = ln_emit()

            sy.wait_ge(semDVE, lnT_done)
            sp_x = wsp(sy.dma_start(out=xspill.ap(), in_=xsb))
            w_qwt = wsp(sy.dma_start(out=qwt.rearrange("p (k o) -> p k o", k=NK),
                                     in_=qkvT_hbm.ap().rearrange("k p o -> p k o")))

            t_.wait_ge(semSP, w_qwt)
            v_.wait_ge(semSP, w_qwt)
            mdve(v_.memset(v_aug2.rearrange("p (c s) -> p c s", s=65)[:, :, 64:65], 1.0))

            def qkv_comp(h, comp, dst, with_relu, guard=0):
                biascol = qkvb64[:, comp * 16 + h: comp * 16 + h + 1]
                ms = []
                for nb in range(4):
                    ps = psm[nb % 2]
                    if len(ms) >= 2:
                        t_.wait_ge(semDVE, ms[-2])
                    elif guard:
                        t_.wait_ge(semDVE, guard)
                    for k in range(NK):
                        mm = t_.matmul(ps[0:64, :],
                                       qwt[:, k * 3072 + comp * D + h * DH: k * 3072 + comp * D + (h + 1) * DH],
                                       lnT[:, k * T + nb * 512: k * T + (nb + 1) * 512],
                                       start=(k == 0), stop=(k == NK - 1))
                    mk = mpe(mm)
                    v_.wait_ge(semPE, mk)
                    if with_relu:
                        ms.append(mdve(v_.tensor_scalar(dst[:, nb * 512:(nb + 1) * 512],
                                                        ps[0:64, :], biascol, HEPS,
                                                        ALU.add, ALU.max)))
                    else:
                        ms.append(mdve(v_.tensor_scalar(dst[:, nb * 512:(nb + 1) * 512],
                                                        ps[0:64, :], biascol, None, ALU.add)))
                return ms[-1]

            BUF = [dict(qT=qT_h, kT=kT_h, vT=vT_h, kt=ktok, va=v_aug, so=sown),
                   dict(qT=qT2, kT=kT2, vT=vT2, kt=ktok2, va=v_aug2, so=sown2)]
            vaug_m = {}
            scan_m = {}
            scan_steps = {}

            def make_A_units(h, guard0):
                B = BUF[h % 2]
                st = {"ms": []}
                dsts = [B["qT"], B["kT"], B["vT"]]
                units = []

                def qunit(comp, nb):
                    def go():
                        ps = psm[(comp * 4 + nb) % 2]
                        if len(st["ms"]) >= 2:
                            t_.wait_ge(semDVE, st["ms"][-2])
                        elif guard0:
                            t_.wait_ge(semDVE, guard0)
                        for k in range(NK):
                            mm = t_.matmul(
                                ps[0:64, :],
                                qwt[:, k * 3072 + comp * D + h * DH: k * 3072 + comp * D + (h + 1) * DH],
                                lnT[:, k * T + nb * 512: k * T + (nb + 1) * 512],
                                start=(k == 0), stop=(k == NK - 1))
                        mk = mpe(mm)
                        v_.wait_ge(semPE, mk)
                        biascol = qkvb64[:, comp * 16 + h: comp * 16 + h + 1]
                        if comp < 2:
                            st["ms"].append(mdve(v_.tensor_scalar(
                                dsts[comp][:, nb * 512:(nb + 1) * 512], ps[0:64, :],
                                biascol, HEPS, ALU.add, ALU.max)))
                        else:
                            st["ms"].append(mdve(v_.tensor_scalar(
                                dsts[comp][:, nb * 512:(nb + 1) * 512], ps[0:64, :],
                                biascol, None, ALU.add)))
                    return go

                for comp in range(3):
                    for nb in range(4):
                        units.append(qunit(comp, nb))

                def ktrans():
                    t_.wait_ge(semDVE, st["ms"][-1])
                    for c in range(16):
                        mmk = t_.transpose(pbf0[:, c * 64:(c + 1) * 64],
                                           B["kT"][:, c * 128:(c + 1) * 128], identb[0:64, 0:64])
                    mpe(mmk)

                def vtrans():
                    for c in range(16):
                        mmv = t_.transpose(pbf1[:, c * 64:(c + 1) * 64],
                                           B["vT"][:, c * 128:(c + 1) * 128], identb[0:64, 0:64])
                    m_tv = mpe(mmv)
                    v_.wait_ge(semPE, m_tv)
                    v_.tensor_copy(B["kt"], pbf0[:, :])
                    vaug_m[h] = mdve(v_.tensor_copy(
                        B["va"].rearrange("p (c s) -> p c s", s=65)[:, :, 0:64],
                        pbf1.rearrange("p (c s) -> p c s", s=64)))

                units += [ktrans, vtrans]
                return units

            def stage_U(h, pbig_free_act):
                B = BUF[h % 2]
                if pbig_free_act:
                    t_.wait_ge(semACT, pbig_free_act)
                t_.wait_ge(semDVE, vaug_m[h])
                for c in range(16):
                    pbU = pbig[c // 8]
                    mm = t_.matmul(pbU[0:64, (c % 8) * 128:(c % 8) * 128 + 65],
                                   B["kt"][:, c * 64:(c + 1) * 64],
                                   B["va"][:, c * 65:(c + 1) * 65], start=True, stop=True)
                m_U = mpe(mm)
                v_.wait_ge(semPE, m_U)
                v_.tensor_copy(U_sb[:, 0:1024], pbig0[0:64, :])
                mdve(v_.tensor_copy(U_sb[:, 1024:2048], pbig1[0:64, :]))
                Us = U_sb.rearrange("p (c s) -> p c s", s=128)
                steps = {}
                steps[0] = mdve(v_.memset(B["so"][:, 0:65], 0.0))
                m_sc = mdve(v_.tensor_copy(sacc[:, 0:65], Us[:, 0, 0:65]))
                for c in range(1, 16):
                    v_.wait_ge(semDVE, m_sc)
                    steps[c] = mdve(v_.tensor_copy(B["so"][:, c * 65:(c + 1) * 65],
                                                   sacc[:, 0:65]))
                    if c < 15:
                        m_sc = mdve(v_.tensor_add(sacc[:, 0:65], sacc[:, 0:65],
                                                  Us[:, c, 0:65]))
                scan_m[h] = mdve(v_.tensor_add(contrib[:, h * 65:(h + 1) * 65],
                                               sacc[:, 0:65], Us[:, 15, 0:65]))
                scan_steps[h] = steps

            def stage_D(h, aunits):
                B = BUF[h % 2]
                t_.wait_ge(semDVE, scan_steps[h][1])
                atm_m = {}
                nd_act = {}
                nd_pe = {}
                for c in range(16):
                    atv = pbig0[:, (c % 2) * 512:(c % 2) * 512 + 128]
                    if c >= 2:
                        t_.wait_ge(semDVE, atm_m[c - 2])
                    m_at = mpe(t_.matmul(atv, B["kT"][:, c * 128:(c + 1) * 128],
                                         B["qT"][:, c * 128:(c + 1) * 128],
                                         start=True, stop=True))
                    v_.wait_ge(semPE, m_at)
                    if c >= 2:
                        v_.wait_ge(semPE, nd_pe[c - 2])
                    atm_m[c] = mdve(v_.tensor_mul(atm2[:, (c % 2) * 128:(c % 2) * 128 + 128],
                                                  atv, triu))
                    if aunits:
                        aunits.pop(0)()
                    t_.wait_ge(semDVE, atm_m[c])
                    t_.wait_ge(semDVE, scan_steps[h][c])
                    if c >= 2:
                        t_.wait_ge(semACT, nd_act[c - 2])
                    ndv = pbig1[:, (c % 2) * 512:(c % 2) * 512 + 65]
                    t_.matmul(ndv, atm2[:, (c % 2) * 128:(c % 2) * 128 + 128],
                              B["va"][:, c * 65:(c + 1) * 65], start=True, stop=False)
                    nd_pe[c] = mpe(t_.matmul(ndv, B["qT"][:, c * 128:(c + 1) * 128],
                                             B["so"][:, c * 65:(c + 1) * 65],
                                             start=False, stop=True))
                    a_.wait_ge(semPE, nd_pe[c])
                    a_.copy(attnum[:, c * D + h * DH: c * D + (h + 1) * DH], ndv[:, 0:64])
                    nd_act[c] = mact(a_.copy(den_all[:, h * 16 + c: h * 16 + c + 1],
                                             ndv[:, 64:65]))
                return nd_act[15]

            for u in make_A_units(0, 0):
                u()
            stage_U(0, 0)
            nd_read_act = 0
            aunits = make_A_units(1, vaug_m[0]) if H > 1 else []
            for h in range(H):
                nd_read_act = stage_D(h, aunits)
                if h + 1 < H:
                    for u in aunits:
                        u()
                    aunits = []
                    stage_U(h + 1, nd_read_act)
                    if h + 2 < H:
                        aunits = make_A_units(h + 2, vaug_m[h + 1])
            scan_done = scan_m[H - 1]
            nd_read = cDVE.v

            micro(19)
            gp.wait_ge(semDVE, scan_done)
            c_dma = wgp(gp.dma_start(out=contrib_dram.ap(), in_=contrib))
            gp.wait_ge(semGP, c_dma)
            gp.collective_compute(
                "AllGather", ALU.bypass,
                replica_groups=[list(range(NCORES))],
                ins=[contrib_dram.ap().opt()],
                outs=[gathered_dram.ap().opt()],
            ).then_inc(semCC, 1)
            cCC.inc()
            gp.wait_ge(semCC, cCC.v)
            g_reads = {}
            for r in range(NCORES):
                if r >= 2:
                    gp.wait_ge(semDVE, g_reads[r - 2])
                g_dma = wgp(gp.dma_start(out=gbuf[r % 2],
                                         in_=gathered_dram.ap()[r * 64:(r + 1) * 64, :]))
                v_.wait_ge(semGP, g_dma)
                if r == 0:
                    g_reads[0] = mdve(v_.tensor_scalar_mul(S0, gbuf[0], maskcol[:, 0:1]))
                else:
                    v_.wait_ge(semDVE, g_reads[r - 1])
                    g_reads[r] = mdve(v_.scalar_tensor_tensor(S0, gbuf[r % 2],
                                                              maskcol[:, r:r + 1], S0,
                                                              ALU.mult, ALU.add))
            v_.wait_ge(semDVE, g_reads[NCORES - 1])
            m_s0 = mdve(v_.tensor_copy(S0bf, S0))

            t_.wait_ge(semACT, nd_read_act)
            v_.wait_ge(semACT, nd_read_act)
            CQ = [qT_h, qT2]
            m_q = {0: qkv_comp(0, 0, qT_h, True, guard=nd_read)}
            m_q[1] = qkv_comp(1, 0, qT2, True, guard=m_q[0])
            t_.wait_ge(semDVE, m_s0)
            num_read = 0
            m_d1h = {}
            for h in range(H):
                qb = CQ[h % 2]
                t_.wait_ge(semDVE, m_q[h])
                if h > 0:
                    t_.wait_ge(semDVE, m_d1h[h - 1])
                for c in range(16):
                    mm = t_.matmul(pbig1[:, c:c + 1], qb[:, c * 128:(c + 1) * 128],
                                   S0bf[:, h * 65 + 64: h * 65 + 65], start=True, stop=True)
                m_dc = mpe(mm)
                if num_read:
                    t_.wait_ge(semACT, num_read)
                for c in range(16):
                    mm = t_.matmul(pbig0[:, c * 64:(c + 1) * 64],
                                   qb[:, c * 128:(c + 1) * 128],
                                   S0bf[:, h * 65: h * 65 + 64], start=True, stop=True)
                m_nc = mpe(mm)
                if h + 2 < H:
                    m_q[h + 2] = qkv_comp(h + 2, 0, CQ[(h + 2) % 2], True, guard=m_q[h + 1])
                v_.wait_ge(semPE, m_dc)
                if num_read:
                    v_.wait_ge(semACT, num_read)
                m_d1 = mdve(v_.tensor_add(rec[:, 0:16], den_all[:, h * 16:(h + 1) * 16],
                                          pbig1[:, 0:16]))
                m_d1h[h] = m_d1
                v_.wait_ge(semDVE, m_d1)
                m_d2 = mdve(v_.tensor_scalar_max(rec[:, 16:32], rec[:, 0:16], HEPS))
                v_.wait_ge(semDVE, m_d2)
                m_rec = mdve(v_.reciprocal(rec[:, 0:16], rec[:, 16:32]))
                v_.wait_ge(semDVE, m_rec)
                v_.wait_ge(semPE, m_nc)
                for half in range(2):
                    c0 = half * 8
                    if num_read:
                        v_.wait_ge(semACT, num_read)
                    m_nt = mdve(v_.tensor_add(
                        numtot.rearrange("p (c f) -> p c f", f=64),
                        pbig0[:, half * 512:(half + 1) * 512].rearrange("p (c f) -> p c f", f=64),
                        attnum.rearrange("p (c f) -> p c f", f=D)[:, c0:c0 + 8, h * DH:(h + 1) * DH]))
                    a_.wait_ge(semDVE, m_nt)
                    a_.wait_ge(semDVE, m_rec)
                    for cc in range(8):
                        c = c0 + cc
                        mm2 = a_.activation(
                            attnum[:, c * D + h * DH: c * D + (h + 1) * DH],
                            numtot[:, cc * 64:(cc + 1) * 64], AF.Copy,
                            scale=rec[:, c:c + 1])
                    num_read_act = mact(mm2)
                    num_read = num_read_act
            corr_done_act = num_read
            corr_done = cDVE.v
            qwt_free = cPE.v

            micro(20)
            t_.wait_ge(semDVE, corr_done)
            t_.wait_ge(semACT, corr_done_act)
            sy.wait_ge(semPE, qwt_free)
            sy.wait_ge(semDVE, corr_done)
            xs_dmas = {}
            xs_dmas[0] = wsp(sy.dma_start(out=xstage[0], in_=xspill.ap()[:, 0:D]))
            xs_dmas[1] = wsp(sy.dma_start(out=xstage[1], in_=xspill.ap()[:, D:2 * D]))
            attT_m = {}
            xup = {}
            attT_free = 0
            for t in range(NT):
                pb = pbf[t % 2]
                pbg = pbig[t % 2]
                if t >= 2:
                    t_.wait_ge(semDVE, attT_m[t - 2])
                for k in range(NK):
                    mm = t_.transpose(pb[:, k * 128:(k + 1) * 128],
                                      attnum[:, t * D + k * 128: t * D + (k + 1) * 128], identb)
                m_tt = mpe(mm)
                v_.wait_ge(semPE, m_tt)
                if attT_free:
                    v_.wait_ge(semPE, attT_free)
                m_aT = mdve(v_.tensor_copy(attT, pb[:, :]))
                attT_m[t] = m_aT
                t_.wait_ge(semDVE, m_aT)
                if t >= 2:
                    t_.wait_ge(semDVE, xup[t - 2])
                for nb in range(2):
                    for k in range(NK):
                        t_.matmul(pbg[:, nb * 512:(nb + 1) * 512],
                                  attT[:, k * 128:(k + 1) * 128],
                                  owt[:, k * D + nb * 512: k * D + (nb + 1) * 512],
                                  start=(k == 0), stop=False)
                    mm = t_.matmul(pbg[:, nb * 512:(nb + 1) * 512], onesbf[0:1, 0:128],
                                   outbbf[0:1, nb * 512:(nb + 1) * 512], start=False, stop=True)
                m_op = mpe(mm)
                attT_free = m_op
                v_.wait_ge(semPE, m_op)
                v_.wait_ge(semSP, xs_dmas[t])
                v_.tensor_add(xsb[:, t * D: t * D + 512], xstage[t % 2][:, 0:512],
                              pbg[:, 0:512])
                xup[t] = mdve(v_.tensor_add(xsb[:, t * D + 512: t * D + 1024],
                                            xstage[t % 2][:, 512:1024], pbg[:, 512:1024]))
                if t + 2 < NT:
                    sy.wait_ge(semDVE, xup[t])
                    xs_dmas[t + 2] = wsp(sy.dma_start(
                        out=xstage[t % 2], in_=xspill.ap()[:, (t + 2) * D:(t + 3) * D]))

            if STOP_AFTER >= 3:
                ffn_emit()

          except StopEmission:
            pass
          if DEBUG_DUMPS:
            wall(gp)
            cGP.inc(16); gp.dma_start(out=dbg_lnT, in_=lnT).then_inc(semGP, 16)
            cGP.inc(16); gp.dma_start(out=dbg_hT, in_=hT).then_inc(semGP, 16)
            cGP.inc(16); gp.dma_start(out=dbg_gates, in_=gates_all).then_inc(semGP, 16)
            cGP.inc(16); gp.dma_start(out=dbg_misc[0:64, 0:48], in_=qkvb64).then_inc(semGP, 16)
            cGP.inc(16); gp.dma_start(out=dbg_misc[:, 48:112], in_=probs).then_inc(semGP, 16)
            cGP.inc(16); gp.dma_start(out=dbg_misc[:, 112:120], in_=m8sb).then_inc(semGP, 16)
            cGP.inc(16); gp.dma_start(out=dbg_misc[:, 120:136], in_=stats[:, 0:16]).then_inc(semGP, 16)
            if STOP_AFTER >= 2:
                cGP.inc(16); gp.dma_start(out=dbg_misc[:, 136:168], in_=rec).then_inc(semGP, 16)
                cGP.inc(16); gp.dma_start(out=dbg_misc[:, 168:232], in_=numtot[:, 448:512]).then_inc(semGP, 16)
                cGP.inc(16); gp.dma_start(out=dbg_att, in_=attnum).then_inc(semGP, 16)
                cGP.inc(16); gp.dma_start(out=dbg_den, in_=den_all).then_inc(semGP, 16)
                cGP.inc(16); gp.dma_start(out=dbg_st[:, 0:H * 65], in_=contrib).then_inc(semGP, 16)
                cGP.inc(16); gp.dma_start(out=dbg_st[:, H * 65:2 * H * 65], in_=S0).then_inc(semGP, 16)
                cGP.inc(16); gp.dma_start(out=dbg_st[:, 2 * H * 65:3 * H * 65], in_=S0bf).then_inc(semGP, 16)
                cGP.inc(16); gp.dma_start(out=dbg_st[:, 3 * H * 65:3 * H * 65 + 1040], in_=sown).then_inc(semGP, 16)
            gp.wait_ge(semGP, cGP.v)
          wall(sy)
          m_out = wsp(sy.dma_start(out=ydram.rearrange("(t p) d -> p t d", p=128),
                                   in_=xsb.rearrange("p (t d) -> p t d", d=D)))
          sy.wait_ge(semSP, m_out)

    return nc


def kernel(L0_attn_x, L0_attn_qkv_w, L0_attn_qkv_b, L0_attn_out_w, L0_attn_out_b,
           L0_ffn_wg1, L0_ffn_bg1, L0_ffn_wg2, L0_ffn_bg2, L0_ffn_We, L0_ffn_be):
    x = np.ascontiguousarray(np.asarray(L0_attn_x, dtype=np.float32))
    qkv_w = np.ascontiguousarray(np.asarray(L0_attn_qkv_w, dtype=np.float32))
    qkv_b = np.ascontiguousarray(np.asarray(L0_attn_qkv_b, dtype=np.float32)).reshape(1, 3 * D)
    out_w = np.ascontiguousarray(np.asarray(L0_attn_out_w, dtype=np.float32))
    out_b = np.ascontiguousarray(np.asarray(L0_attn_out_b, dtype=np.float32)).reshape(1, D)
    wg1 = np.ascontiguousarray(np.asarray(L0_ffn_wg1, dtype=np.float32))
    bg1 = np.ascontiguousarray(np.asarray(L0_ffn_bg1, dtype=np.float32)).reshape(1, 256)
    wg2 = np.ascontiguousarray(np.asarray(L0_ffn_wg2, dtype=np.float32))
    bg2 = np.ascontiguousarray(np.asarray(L0_ffn_bg2, dtype=np.float32)).reshape(1, 40)
    We = np.ascontiguousarray(np.asarray(L0_ffn_We, dtype=np.float32)).reshape(NE * D, D)
    be = np.ascontiguousarray(np.asarray(L0_ffn_be, dtype=np.float32))

    identf = np.eye(128, dtype=np.float32)
    identb = np.eye(128, dtype=ml_dtypes.bfloat16)
    triu = np.triu(np.ones((128, 128), dtype=np.float32)).astype(ml_dtypes.bfloat16)

    nc = build_program()
    in_maps = []
    for c in range(NCORES):
        b, half = c // 2, c % 2
        msk = np.zeros((1, 8), dtype=np.float32)
        if half == 1:
            msk[0, c - 1] = 1.0
        in_maps.append({
            "xin": np.ascontiguousarray(x[b, half * T:(half + 1) * T]),
            "qkvw": qkv_w, "qkvb": qkv_b, "outw": out_w, "outb": out_b,
            "wg1": wg1, "bg1": bg1, "wg2": wg2, "bg2": bg2,
            "wed": We, "bed": be,
            "identf": identf, "identb": identb, "triu": triu,
            "maskrow": msk,
        })
    res = run_bass_kernel_spmd(nc, in_maps, core_ids=list(range(NCORES)))
    if DEBUG_DUMPS:
        DBG.clear()
        DBG.update({k: [res.results[c][k] for c in range(NCORES)]
                    for k in res.results[0] if k.startswith("dbg_")})
    out = np.empty((4, 4096, D), dtype=np.float32)
    for c in range(NCORES):
        b, half = c // 2, c % 2
        out[b, half * T:(half + 1) * T] = res.results[c]["y"]
    return out


# version 08
# speedup vs baseline: 1.0265x; 1.0265x over previous
import numpy as np
import ml_dtypes
from contextlib import ExitStack

import concourse.bass as bass
import concourse.mybir as mybir
from concourse.bass_utils import run_bass_kernel_spmd

F32 = mybir.dt.float32
BF16 = mybir.dt.bfloat16
AF = mybir.ActivationFunctionType
ALU = mybir.AluOpType
AX = mybir.AxisListType

NCORES = 8
T = 2048
NT = 16
D = 1024
NK = 8
H = 16
DH = 64
NE = 20
HEPS = 1e-6
LNEPS = 1e-5
STOP_AFTER = 3
STOP_MICRO = None
DEBUG_DUMPS = False
DBG = {}


class StopEmission(Exception):
    pass


class Cnt:
    def __init__(self):
        self.v = 0

    def inc(self, n=1):
        self.v += n
        return self.v


def build_program():
    nc = bass.Bass("TRN2", target_bir_lowering=False, debug=False)

    xin = nc.dram_tensor("xin", [T, D], F32, kind="ExternalInput").ap()
    qkvw = nc.dram_tensor("qkvw", [3 * D, D], F32, kind="ExternalInput").ap()
    qkvb = nc.dram_tensor("qkvb", [1, 3 * D], F32, kind="ExternalInput").ap()
    outw = nc.dram_tensor("outw", [D, D], F32, kind="ExternalInput").ap()
    outb = nc.dram_tensor("outb", [1, D], F32, kind="ExternalInput").ap()
    wg1d = nc.dram_tensor("wg1", [D, 256], F32, kind="ExternalInput").ap()
    bg1d = nc.dram_tensor("bg1", [1, 256], F32, kind="ExternalInput").ap()
    wg2d = nc.dram_tensor("wg2", [256, 40], F32, kind="ExternalInput").ap()
    bg2d = nc.dram_tensor("bg2", [1, 40], F32, kind="ExternalInput").ap()
    wed = nc.dram_tensor("wed", [NE * D, D], F32, kind="ExternalInput").ap()
    bed = nc.dram_tensor("bed", [NE, D], F32, kind="ExternalInput").ap()
    identf_d = nc.dram_tensor("identf", [128, 128], F32, kind="ExternalInput").ap()
    identb_d = nc.dram_tensor("identb", [128, 128], BF16, kind="ExternalInput").ap()
    triu_d = nc.dram_tensor("triu", [128, 128], BF16, kind="ExternalInput").ap()
    maskrow_d = nc.dram_tensor("maskrow", [1, 8], F32, kind="ExternalInput").ap()
    ydram = nc.dram_tensor("y", [T, D], F32, kind="ExternalOutput").ap()
    if DEBUG_DUMPS:
        dbg_lnT = nc.dram_tensor("dbg_lnT", [128, NK * T], F32, kind="ExternalOutput").ap()
        dbg_hT = nc.dram_tensor("dbg_hT", [128, 2 * T], F32, kind="ExternalOutput").ap()
        dbg_gates = nc.dram_tensor("dbg_gates", [128, NT * NE], F32, kind="ExternalOutput").ap()
        dbg_misc = nc.dram_tensor("dbg_misc", [128, 256], F32, kind="ExternalOutput").ap()
        dbg_att = nc.dram_tensor("dbg_att", [128, NT * D], F32, kind="ExternalOutput").ap()
        dbg_den = nc.dram_tensor("dbg_den", [128, 256], F32, kind="ExternalOutput").ap()
        dbg_st = nc.dram_tensor("dbg_st", [64, 4 * H * 65], F32, kind="ExternalOutput").ap()

    qkvT_hbm = nc.dram_tensor("qkvT_hbm", [NK, 128, 3 * D], BF16)
    xspill = nc.dram_tensor("xspill", [128, NT * D], F32)
    contrib_dram = nc.dram_tensor("contrib_dram", [64, H * 65], F32)
    gathered_dram = nc.dram_tensor("gathered_dram", [NCORES * 64, H * 65], F32)

    B0 = (nc.sbuf_base + 31) // 32 * 32
    A = B0
    B = A + 66560
    C = B + 32768
    Dr = C + 49152
    E = Dr + 16384

    at = lambda name, shape, dt, offset: nc.alloc_sbuf_tensor_at(name, shape, dt, offset=offset).ap()

    xsb = at("xsb", [128, NT * D], F32, A)
    attnum = at("attnum", [128, NT * D], BF16, A + 32768)
    U_sb = at("U_sb", [64, 2048], F32, A)
    contrib = at("contrib", [64, H * 65], F32, A + 8192)
    den_all = at("den_all", [128, 256], F32, A + 12416)
    rec = at("rec", [128, 32], F32, A + 13440)
    gbuf = [at(f"gbuf{i}", [64, H * 65], F32, A + i * 4192) for i in range(2)]
    qT2 = at("qT2", [64, T], BF16, A + 13568)
    kT2 = at("kT2", [64, T], BF16, A + 17664)
    vT2 = at("vT2", [64, T], BF16, A + 21760)
    ktok2 = at("ktok2", [128, 16 * 64], BF16, A + 25856)
    v_aug2 = at("v_aug2", [128, 16 * 65], BF16, A + 27904)
    sown2 = at("sown2", [64, 16 * 65], BF16, A + 30016)

    lnT = at("lnT", [128, NK * T], BF16, B)
    asm = [at(f"asm{i}", [128, 3 * D], BF16, B + i * 6144) for i in range(2)]
    qkvb_row = at("qkvb_row", [1, 3 * D], F32, B + 12288)
    bg1row = at("bg1row", [1, 256], F32, B + 24576)

    stage = at("stage", [128, 24 * D], BF16, C)
    web = [at(f"web{i}", [128, NK * D], BF16, C + i * 16384) for i in range(2)]
    hT = at("hT", [128, 2 * T], BF16, C + 32768)
    qwt = at("qwt", [128, NK * 3 * D], BF16, C)
    xstage = [at(f"xstage{i}", [128, D], F32, C + i * 4096) for i in range(2)]
    attT = at("attT", [128, D], BF16, C + 8192)

    owt = at("owt", [128, NK * D], BF16, Dr)

    off = [E]

    def tal(name, shape, dt):
        per = int(np.prod(shape[1:])) * mybir.dt.size(dt)
        o = (off[0] + 31) // 32 * 32
        off[0] = o + per
        return at(name, shape, dt, o)

    identf = tal("identfs", [128, 128], F32)
    identb = tal("identbs", [128, 128], BF16)
    triu = tal("trius", [128, 128], BF16)
    wg1b = tal("wg1b", [128, NK * 256], BF16)
    wg2b = tal("wg2b", [128, 2 * 40], BF16)
    beb = tal("beb", [NE, D], BF16)
    qkvb64 = tal("qkvb64", [64, 48], F32)
    outbbf = tal("outbbf", [1, D], BF16)
    bg2row = tal("bg2row", [1, 40], BF16)
    onesbf = tal("onesbf", [1, 512], BF16)
    onesf = tal("onesf", [1, 64], F32)
    maskrow = tal("maskrow_sb", [1, 8], F32)
    maskcol = tal("maskcol", [64, 8], F32)
    gates_all = tal("gates_all", [128, NT * NE], F32)
    gatesT = tal("gatesT", [NE, NT * 128], BF16)
    stats = tal("stats", [128, 64], F32)
    lnstage = tal("lnstage", [128, D], BF16)
    probs = tal("probs", [128, 64], F32)
    m8sb = tal("m8sb", [128, 8], F32)
    bg1cols = tal("bg1cols", [128, 2], F32)
    qT_h = tal("qT_h", [64, T], BF16)
    kT_h = tal("kT_h", [64, T], BF16)
    vT_h = tal("vT_h", [64, T], BF16)
    ktok = tal("ktok", [128, 16 * 64], BF16)
    v_aug = tal("v_aug", [128, 16 * 65], BF16)
    sown = tal("sown", [64, 16 * 65], BF16)
    S0 = tal("S0", [64, H * 65], F32)
    S0bf = tal("S0bf", [64, H * 65], BF16)
    sacc = tal("sacc", [64, 72], F32)
    numtot = tal("numtot", [128, 512], F32)
    epscol = tal("epscol", [128, 1], F32)
    atm2 = tal("atm2", [128, 256], BF16)
    assert off[0] <= 229340, f"tail overflow {off[0]}"

    with ExitStack() as ctx:
        pbig0 = ctx.enter_context(nc.psum_tensor("pbig0", [128, 1024], F32))
        pbig1 = ctx.enter_context(nc.psum_tensor("pbig1", [128, 1024], F32))
        psm0 = ctx.enter_context(nc.psum_tensor("psm0", [128, 512], F32))
        psm1 = ctx.enter_context(nc.psum_tensor("psm1", [128, 512], F32))
        pbf0 = ctx.enter_context(nc.psum_tensor("pbf0", [128, 1024], BF16))
        pbf1 = ctx.enter_context(nc.psum_tensor("pbf1", [128, 1024], BF16))
        pbig = [pbig0, pbig1]
        psm = [psm0, psm1]
        pbf = [pbf0, pbf1]

        semPE = ctx.enter_context(nc.semaphore("semPE"))
        semDVE = ctx.enter_context(nc.semaphore("semDVE"))
        semACT = ctx.enter_context(nc.semaphore("semACT"))
        semIN = ctx.enter_context(nc.semaphore("semIN"))
        semW = ctx.enter_context(nc.semaphore("semW"))
        semSP = ctx.enter_context(nc.semaphore("semSP"))
        semGP = ctx.enter_context(nc.semaphore("semGP"))
        semCC = ctx.enter_context(nc.semaphore("semCC"))
        block = ctx.enter_context(nc.Block())

        cPE, cDVE, cACT = Cnt(), Cnt(), Cnt()
        cIN, cW, cSP, cGP, cCC = Cnt(), Cnt(), Cnt(), Cnt(), Cnt()

        t_, v_, a_, sy, gp = nc.tensor, nc.vector, nc.scalar, nc.sync, nc.gpsimd

        def mpe(i):
            i.then_inc(semPE, 1)
            return cPE.inc()

        def mdve(i):
            i.then_inc(semDVE, 1)
            return cDVE.inc()

        def mact(i):
            i.then_inc(semACT, 1)
            return cACT.inc()

        def win(i):
            i.then_inc(semIN, 16)
            return cIN.inc(16)

        def ww(i):
            i.then_inc(semW, 16)
            return cW.inc(16)

        def wsp(i):
            i.then_inc(semSP, 16)
            return cSP.inc(16)

        def wgp(i):
            i.then_inc(semGP, 16)
            return cGP.inc(16)

        def wall(eng):
            for sem, c in ((semPE, cPE), (semDVE, cDVE), (semACT, cACT), (semIN, cIN),
                           (semW, cW), (semSP, cSP), (semGP, cGP), (semCC, cCC)):
                if c.v:
                    eng.wait_ge(sem, c.v)

        @block.sync
        def _(sync):
          try:
            def micro(n):
                if STOP_MICRO is not None and STOP_MICRO == n:
                    raise StopEmission()

            win(sy.dma_start(out=xsb.rearrange("p (t d) -> p t d", d=D),
                             in_=xin.rearrange("(t p) d -> p t d", p=128)))
            micro(1)
            win(sy.dma_start(out=identf, in_=identf_d))
            win(sy.dma_start(out=identb, in_=identb_d))
            win(sy.dma_start(out=triu, in_=triu_d))
            win(sy.dma_start(out=maskrow, in_=maskrow_d))
            win(sy.dma_start(out=qkvb_row, in_=qkvb))
            win(sy.dma_start(out=bg1row, in_=bg1d))
            in_all = cIN.v

            ww(gp.dma_start(out=wg1b.rearrange("p (k f) -> p k f", k=NK),
                            in_=wg1d.rearrange("(k p) f -> p k f", p=128)))
            ww(gp.dma_start(out=wg2b.rearrange("p (k f) -> p k f", k=2),
                            in_=wg2d.rearrange("(k p) f -> p k f", p=128)))
            ww(gp.dma_start(out=beb, in_=bed))
            ww(gp.dma_start(out=outbbf, in_=outb))
            ww(gp.dma_start(out=bg2row, in_=bg2d))
            w_outw = ww(gp.dma_start(out=stage[:, 0:8 * D].rearrange("p (i d) -> p i d", i=8),
                                     in_=outw.rearrange("(i p) d -> p i d", p=128)))
            micro(2)

            v_.memset(onesbf, 1.0)
            v_.memset(onesf, 1.0)
            m_ones = mdve(v_.memset(v_aug.rearrange("p (c s) -> p c s", s=65)[:, :, 64:65], 1.0))
            micro(3)

            t_.wait_ge(semIN, in_all)
            t_.wait_ge(semDVE, m_ones)
            for c in range(3):
                for h in range(H):
                    j = c * 16 + h
                    t_.matmul(psm0[0:64, j:j + 1],
                              qkvb_row[0:1, c * D + h * DH: c * D + (h + 1) * DH],
                              onesf[0:1, 0:1], start=True, stop=True)
            t_.matmul(psm1[0:128, 0:1], bg1row[0:1, 0:128], onesf[0:1, 0:1],
                      start=True, stop=True)
            t_.matmul(psm1[0:128, 1:2], bg1row[0:1, 128:256], onesf[0:1, 0:1],
                      start=True, stop=True)
            m_cols = mpe(t_.matmul(psm1[0:64, 8:16], onesf[0:1, 0:64], maskrow[0:1, 0:8],
                                   start=True, stop=True))
            v_.wait_ge(semPE, m_cols)
            m_cp1 = mdve(v_.tensor_copy(qkvb64, psm0[0:64, 0:48]))
            v_.tensor_copy(bg1cols, psm1[0:128, 0:2])
            v_.tensor_copy(maskcol, psm1[0:64, 8:16])
            v_.wait_ge(semDVE, m_cp1)
            m_bias = mdve(v_.tensor_scalar(qkvb64[:, 0:32], qkvb64[:, 0:32], HEPS, None, ALU.add))
            micro(4)

            t_.wait_ge(semW, w_outw)
            owt_packs = []
            for k in range(NK):
                pb = pbf[k % 2]
                if len(owt_packs) >= 2:
                    t_.wait_ge(semDVE, owt_packs[-2])
                for i in range(8):
                    mm = t_.transpose(pb[:, i * 128:(i + 1) * 128],
                                      stage[:, i * D + k * 128: i * D + (k + 1) * 128], identb)
                mk = mpe(mm)
                v_.wait_ge(semPE, mk)
                owt_packs.append(mdve(v_.tensor_copy(owt[:, k * D:(k + 1) * D], pb[:, :])))
            m_owt_done_pe = cPE.v
            micro(5)

            gp.wait_ge(semPE, m_owt_done_pe)
            for part in range(3):
                w_qkvw = ww(gp.dma_start(
                    out=stage[:, part * 8 * D:(part + 1) * 8 * D].rearrange("p (i d) -> p i d", i=8),
                    in_=qkvw[part * 8 * 128:(part + 1) * 8 * 128, :].rearrange("(i p) d -> p i d", p=128)))

            micro(51)
            t_.wait_ge(semW, w_qkvw)
            asm_store = [0, 0]
            pack_ms = []
            for k in range(NK):
                ab = asm[k % 2]
                if asm_store[k % 2]:
                    t_.wait_ge(semSP, asm_store[k % 2])
                for ip in range(3):
                    pb = pbf[(k * 3 + ip) % 2]
                    if len(pack_ms) >= 2:
                        t_.wait_ge(semDVE, pack_ms[-2])
                    for i2 in range(8):
                        i = ip * 8 + i2
                        mm = t_.transpose(pb[:, i2 * 128:(i2 + 1) * 128],
                                          stage[:, i * D + k * 128: i * D + (k + 1) * 128], identb)
                    mk = mpe(mm)
                    v_.wait_ge(semPE, mk)
                    pack_ms.append(mdve(v_.tensor_copy(ab[:, ip * 1024:(ip + 1) * 1024], pb[:, :])))
                if STOP_MICRO == 52 and k == 0:
                    raise StopEmission()
                sy.wait_ge(semDVE, pack_ms[-1])
                asm_store[k % 2] = wsp(sy.dma_start(out=qkvT_hbm.ap()[k], in_=ab))
                if STOP_MICRO == 53 and k == 0:
                    raise StopEmission()
            micro(6)

            def ln_emit():
                m_sd = {}
                for t in range(NT):
                    xt = xsb[:, t * D:(t + 1) * D]
                    sc4 = stats[:, t * 4: t * 4 + 4]
                    bn = probs[:, (t % 2) * 12:(t % 2) * 12 + 12]
                    v_.bn_stats(bn[:, 0:6], xt[:, 0:512])
                    m_b1 = mdve(v_.bn_stats(bn[:, 6:12], xt[:, 512:1024]))
                    v_.wait_ge(semDVE, m_b1)
                    m_b2 = mdve(v_.bn_aggr(bn[:, 0:2] if False else probs[:, 24:26], bn[:, 0:12]))
                    v_.wait_ge(semDVE, m_b2)
                    v_.tensor_scalar_mul(sc4[:, 0:1], probs[:, 24:25], -1.0)
                    m_var = mdve(v_.tensor_scalar_add(sc4[:, 1:2], probs[:, 25:26], LNEPS))
                    a_.wait_ge(semDVE, m_var)
                    m_sd[t] = mact(a_.activation(sc4[:, 2:3], sc4[:, 1:2], AF.Sqrt))
                v_.wait_ge(semACT, m_sd[NT - 1])
                sdv = stats.rearrange("p (t f) -> p t f", f=4)
                m_rc = mdve(v_.reciprocal(sdv[:, :, 3:4], sdv[:, :, 2:3]))
                v_.wait_ge(semDVE, m_rc)
                pbf_free = [0, 0]
                tp_m = {}
                lnT_done = 0
                for t in range(NT):
                    xt = xsb[:, t * D:(t + 1) * D]
                    sc4 = stats[:, t * 4: t * 4 + 4]
                    if t >= 1:
                        v_.wait_ge(semPE, tp_m[t - 1])
                    m_ln = mdve(v_.tensor_scalar(lnstage, xt, sc4[:, 0:1], sc4[:, 3:4],
                                                 ALU.add, ALU.mult))
                    pb = pbf[t % 2]
                    t_.wait_ge(semDVE, m_ln)
                    if pbf_free[t % 2]:
                        t_.wait_ge(semDVE, pbf_free[t % 2])
                    for k in range(NK):
                        mm = t_.transpose(pb[:, k * 128:(k + 1) * 128],
                                          lnstage[:, k * 128:(k + 1) * 128], identb)
                    tp_m[t] = mpe(mm)
                    t_.matmul(pbig1[:, 0:512], identb, lnstage[:, 0:512], start=True, stop=True)
                    t_.matmul(pbig1[:, 0:512], identb, lnstage[:, 0:512], start=True, stop=True)
                    v_.wait_ge(semPE, tp_m[t])
                    lnT_done = mdve(v_.tensor_copy(
                        lnT.rearrange("p (k s) -> p k s", k=NK)[:, :, t * 128:(t + 1) * 128],
                        pb.rearrange("p (k s) -> p k s", k=NK)))
                    pbf_free[t % 2] = lnT_done
                return lnT_done

            def ffn_emit():
                wall(t_); wall(v_); wall(a_); wall(gp)

                web_loaded = {}
                for e in range(2):
                    web_loaded[e] = ww(gp.dma_start(
                        out=web[e].rearrange("p (k f) -> p k f", k=NK),
                        in_=wed[e * D:(e + 1) * D, :].rearrange("(k p) f -> p k f", p=128)))

                lnT_done = ln_emit()

                t_.wait_ge(semDVE, lnT_done)
                h_acts = []
                for fc in range(2):
                    for nb in range(4):
                        ps = psm[nb % 2]
                        if len(h_acts) >= 2:
                            t_.wait_ge(semACT, h_acts[-2])
                        for k in range(NK):
                            mm = t_.matmul(ps[:, :],
                                           wg1b[:, k * 256 + fc * 128: k * 256 + fc * 128 + 128],
                                           lnT[:, k * T + nb * 512: k * T + (nb + 1) * 512],
                                           start=(k == 0), stop=(k == NK - 1))
                        mk = mpe(mm)
                        a_.wait_ge(semPE, mk)
                        h_acts.append(mact(a_.activation(
                            hT[:, fc * T + nb * 512: fc * T + (nb + 1) * 512], ps[:, :],
                            AF.Gelu, bias=bg1cols[:, fc:fc + 1])))
                h_all = cACT.v

                t_.wait_ge(semACT, h_all)
                for t in range(NT):
                    for fc in range(2):
                        t_.matmul(psm0[:, t * 20:(t + 1) * 20],
                                  hT[:, fc * T + t * 128: fc * T + (t + 1) * 128],
                                  wg2b[:, fc * 40: fc * 40 + 20],
                                  start=(fc == 0), stop=False)
                    mm = t_.matmul(psm0[:, t * 20:(t + 1) * 20], onesbf[0:1, 0:128],
                                   bg2row[0:1, 0:20], start=False, stop=True)
                m_lg = mpe(mm)
                v_.wait_ge(semPE, m_lg)
                m_nm = mdve(v_.tensor_reduce(
                    stats[:, 0:16], psm0[:, 0:320].rearrange("p (t e) -> p t e", e=20),
                    AX.X, ALU.max, negate=True))
                a_.wait_ge(semDVE, m_nm)
                pall = numtot[:, 0:320]
                for t in range(NT):
                    m_exp = mact(a_.activation(pall[:, t * 20:(t + 1) * 20],
                                               psm0[:, t * 20:(t + 1) * 20], AF.Exp,
                                               bias=stats[:, t:t + 1]))
                exp_read = m_exp
                v_.wait_ge(semACT, exp_read)
                m_rs = mdve(v_.tensor_reduce(
                    stats[:, 16:32], pall.rearrange("p (t e) -> p t e", e=20),
                    AX.X, ALU.add))
                v_.wait_ge(semDVE, m_rs)
                m_rc = mdve(v_.reciprocal(stats[:, 32:48], stats[:, 16:32]))
                v_.wait_ge(semDVE, m_rc)
                gT_evac = 0
                m8_read = 0
                for t in range(NT):
                    m_ps = mdve(v_.tensor_scalar(gates_all[:, t * NE:(t + 1) * NE],
                                                 pall[:, t * 20:(t + 1) * 20],
                                                 stats[:, 32 + t:33 + t], 0.5,
                                                 ALU.mult, ALU.mult))
                    v_.wait_ge(semDVE, m_ps)
                    m_m8 = mdve(v_.max(m8sb[:, 0:8], gates_all[:, t * NE:(t + 1) * NE]))
                    v_.wait_ge(semDVE, m_m8)
                    m8_read = mdve(v_.scalar_tensor_tensor(
                        gates_all[:, t * NE:(t + 1) * NE],
                        gates_all[:, t * NE:(t + 1) * NE], m8sb[:, 1:2],
                        gates_all[:, t * NE:(t + 1) * NE], ALU.is_ge, ALU.mult))
                    t_.wait_ge(semDVE, m8_read)
                    if gT_evac:
                        t_.wait_ge(semDVE, gT_evac)
                    m_gt = mpe(t_.transpose(psm1[0:20, 0:128],
                                            gates_all[:, t * NE:(t + 1) * NE], identf))
                    v_.wait_ge(semPE, m_gt)
                    gT_evac = mdve(v_.tensor_copy(gatesT[:, t * 128:(t + 1) * 128],
                                                  psm1[0:20, 0:128]))

                bank_free = [0, 0]
                t_.wait_ge(semDVE, gT_evac)
                for t in range(NT):
                    pb = pbig[t % 2]
                    if bank_free[t % 2]:
                        t_.wait_ge(semDVE, bank_free[t % 2])
                    t_.matmul(pb[:, 0:512], gatesT[:, t * 128:(t + 1) * 128], beb[:, 0:512],
                              start=True, stop=True)
                    m_be = mpe(t_.matmul(pb[:, 512:1024], gatesT[:, t * 128:(t + 1) * 128],
                                         beb[:, 512:1024], start=True, stop=True))
                    v_.wait_ge(semPE, m_be)
                    v_.tensor_add(xsb[:, t * D: t * D + 512], xsb[:, t * D: t * D + 512],
                                  pb[:, 0:512])
                    bank_free[t % 2] = mdve(v_.tensor_add(
                        xsb[:, t * D + 512: t * D + 1024], xsb[:, t * D + 512: t * D + 1024],
                        pb[:, 512:1024]))

                micro(11)
                stt_m = {}
                t_.wait_ge(semACT, exp_read)
                ebanks = [(pbig0[:, 0:512], pbig0[:, 512:1024]),
                          (pbig1[:, 0:512], pbig1[:, 512:1024]),
                          (psm0[:, 0:512], psm1[:, 0:512])]
                for e in range(NE):
                    t_.wait_ge(semW, web_loaded[e])
                    mm_last = 0
                    for t in range(NT):
                        idx = e * NT + t
                        bh = ebanks[idx % 3]
                        if idx - 3 >= 0:
                            t_.wait_ge(semDVE, stt_m[idx - 3])
                        elif bank_free[idx % 2]:
                            t_.wait_ge(semDVE, bank_free[idx % 2])
                        for k in range(NK):
                            for nb in range(2):
                                mm = t_.matmul(bh[nb],
                                               lnT[:, k * T + t * 128: k * T + (t + 1) * 128],
                                               web[e % 2][:, k * D + nb * 512: k * D + (nb + 1) * 512],
                                               start=(k == 0), stop=(k == NK - 1))
                        mm_last = mpe(mm)
                        v_.wait_ge(semPE, mm_last)
                        stt_m[idx] = mdve(v_.scalar_tensor_tensor(
                            xsb[:, t * D: t * D + 512], bh[0],
                            gates_all[:, t * NE + e: t * NE + e + 1],
                            xsb[:, t * D: t * D + 512], ALU.mult, ALU.add))
                        stt_m[idx] = mdve(v_.scalar_tensor_tensor(
                            xsb[:, t * D + 512:(t + 1) * D], bh[1],
                            gates_all[:, t * NE + e: t * NE + e + 1],
                            xsb[:, t * D + 512:(t + 1) * D], ALU.mult, ALU.add))
                    if e + 2 < NE:
                        gp.wait_ge(semPE, mm_last)
                        web_loaded[e + 2] = ww(gp.dma_start(
                            out=web[e % 2].rearrange("p (k f) -> p k f", k=NK),
                            in_=wed[(e + 2) * D:(e + 3) * D, :].rearrange("(k p) f -> p k f", p=128)))

            if STOP_AFTER >= 1:
                ffn_emit()

            if STOP_AFTER < 2:
                raise StopEmission()
            wall(t_); wall(v_); wall(a_); wall(gp); wall(sy)

            lnT_done = ln_emit()

            sy.wait_ge(semDVE, lnT_done)
            sp_x = wsp(sy.dma_start(out=xspill.ap(), in_=xsb))
            w_qwt = wsp(sy.dma_start(out=qwt.rearrange("p (k o) -> p k o", k=NK),
                                     in_=qkvT_hbm.ap().rearrange("k p o -> p k o")))

            t_.wait_ge(semSP, w_qwt)
            v_.wait_ge(semSP, w_qwt)
            mdve(v_.memset(v_aug2.rearrange("p (c s) -> p c s", s=65)[:, :, 64:65], 1.0))

            def qkv_comp(h, comp, dst, with_relu, guard=0):
                biascol = qkvb64[:, comp * 16 + h: comp * 16 + h + 1]
                ms = []
                for nb in range(4):
                    ps = psm[nb % 2]
                    if len(ms) >= 2:
                        t_.wait_ge(semDVE, ms[-2])
                    elif guard:
                        t_.wait_ge(semDVE, guard)
                    for k in range(NK):
                        mm = t_.matmul(ps[0:64, :],
                                       qwt[:, k * 3072 + comp * D + h * DH: k * 3072 + comp * D + (h + 1) * DH],
                                       lnT[:, k * T + nb * 512: k * T + (nb + 1) * 512],
                                       start=(k == 0), stop=(k == NK - 1))
                    mk = mpe(mm)
                    v_.wait_ge(semPE, mk)
                    if with_relu:
                        ms.append(mdve(v_.tensor_scalar(dst[:, nb * 512:(nb + 1) * 512],
                                                        ps[0:64, :], biascol, HEPS,
                                                        ALU.add, ALU.max)))
                    else:
                        ms.append(mdve(v_.tensor_scalar(dst[:, nb * 512:(nb + 1) * 512],
                                                        ps[0:64, :], biascol, None, ALU.add)))
                return ms[-1]

            BUF = [dict(qT=qT_h, kT=kT_h, vT=vT_h, kt=ktok, va=v_aug, so=sown),
                   dict(qT=qT2, kT=kT2, vT=vT2, kt=ktok2, va=v_aug2, so=sown2)]
            vaug_m = {}
            scan_m = {}
            scan_steps = {}

            def make_A_units(h, guard0):
                B = BUF[h % 2]
                st = {"ms": []}
                dsts = [B["qT"], B["kT"], B["vT"]]
                units = []

                def qunit(comp, nb):
                    def go():
                        ps = psm[(comp * 4 + nb) % 2]
                        if len(st["ms"]) >= 2:
                            t_.wait_ge(semDVE, st["ms"][-2])
                        elif guard0:
                            t_.wait_ge(semDVE, guard0)
                        for k in range(NK):
                            mm = t_.matmul(
                                ps[0:64, :],
                                qwt[:, k * 3072 + comp * D + h * DH: k * 3072 + comp * D + (h + 1) * DH],
                                lnT[:, k * T + nb * 512: k * T + (nb + 1) * 512],
                                start=(k == 0), stop=(k == NK - 1))
                        mk = mpe(mm)
                        v_.wait_ge(semPE, mk)
                        biascol = qkvb64[:, comp * 16 + h: comp * 16 + h + 1]
                        if comp < 2:
                            st["ms"].append(mdve(v_.tensor_scalar(
                                dsts[comp][:, nb * 512:(nb + 1) * 512], ps[0:64, :],
                                biascol, HEPS, ALU.add, ALU.max)))
                        else:
                            st["ms"].append(mdve(v_.tensor_scalar(
                                dsts[comp][:, nb * 512:(nb + 1) * 512], ps[0:64, :],
                                biascol, None, ALU.add)))
                    return go

                for comp in range(3):
                    for nb in range(4):
                        units.append(qunit(comp, nb))

                def ktrans():
                    t_.wait_ge(semDVE, st["ms"][-1])
                    for c in range(16):
                        mmk = t_.transpose(pbf0[:, c * 64:(c + 1) * 64],
                                           B["kT"][:, c * 128:(c + 1) * 128], identb[0:64, 0:64])
                    mpe(mmk)

                def vtrans():
                    for c in range(16):
                        mmv = t_.transpose(pbf1[:, c * 64:(c + 1) * 64],
                                           B["vT"][:, c * 128:(c + 1) * 128], identb[0:64, 0:64])
                    m_tv = mpe(mmv)
                    v_.wait_ge(semPE, m_tv)
                    v_.tensor_copy(B["kt"], pbf0[:, :])
                    vaug_m[h] = mdve(v_.tensor_copy(
                        B["va"].rearrange("p (c s) -> p c s", s=65)[:, :, 0:64],
                        pbf1.rearrange("p (c s) -> p c s", s=64)))

                units += [ktrans, vtrans]
                return units

            def stage_U(h, pbig_free_act):
                B = BUF[h % 2]
                if pbig_free_act:
                    t_.wait_ge(semACT, pbig_free_act)
                t_.wait_ge(semDVE, vaug_m[h])
                for c in range(16):
                    pbU = pbig[c // 8]
                    mm = t_.matmul(pbU[0:64, (c % 8) * 128:(c % 8) * 128 + 65],
                                   B["kt"][:, c * 64:(c + 1) * 64],
                                   B["va"][:, c * 65:(c + 1) * 65], start=True, stop=True)
                m_U = mpe(mm)
                v_.wait_ge(semPE, m_U)
                v_.tensor_copy(U_sb[:, 0:1024], pbig0[0:64, :])
                mdve(v_.tensor_copy(U_sb[:, 1024:2048], pbig1[0:64, :]))
                Us = U_sb.rearrange("p (c s) -> p c s", s=128)
                steps = {}
                steps[0] = mdve(v_.memset(B["so"][:, 0:65], 0.0))
                m_sc = mdve(v_.tensor_copy(sacc[:, 0:65], Us[:, 0, 0:65]))
                for c in range(1, 16):
                    v_.wait_ge(semDVE, m_sc)
                    steps[c] = mdve(v_.tensor_copy(B["so"][:, c * 65:(c + 1) * 65],
                                                   sacc[:, 0:65]))
                    if c < 15:
                        m_sc = mdve(v_.tensor_add(sacc[:, 0:65], sacc[:, 0:65],
                                                  Us[:, c, 0:65]))
                scan_m[h] = mdve(v_.tensor_add(contrib[:, h * 65:(h + 1) * 65],
                                               sacc[:, 0:65], Us[:, 15, 0:65]))
                scan_steps[h] = steps

            def stage_D(h, aunits):
                B = BUF[h % 2]
                t_.wait_ge(semDVE, scan_steps[h][1])
                atm_m = {}
                nd_act = {}
                nd_pe = {}
                for c in range(16):
                    atv = pbig0[:, (c % 2) * 512:(c % 2) * 512 + 128]
                    if c >= 2:
                        t_.wait_ge(semDVE, atm_m[c - 2])
                    m_at = mpe(t_.matmul(atv, B["kT"][:, c * 128:(c + 1) * 128],
                                         B["qT"][:, c * 128:(c + 1) * 128],
                                         start=True, stop=True))
                    v_.wait_ge(semPE, m_at)
                    if c >= 2:
                        v_.wait_ge(semPE, nd_pe[c - 2])
                    atm_m[c] = mdve(v_.tensor_mul(atm2[:, (c % 2) * 128:(c % 2) * 128 + 128],
                                                  atv, triu))
                    if aunits:
                        aunits.pop(0)()
                    t_.wait_ge(semDVE, atm_m[c])
                    t_.wait_ge(semDVE, scan_steps[h][c])
                    if c >= 2:
                        t_.wait_ge(semACT, nd_act[c - 2])
                    ndv = pbig1[:, (c % 2) * 512:(c % 2) * 512 + 65]
                    t_.matmul(ndv, atm2[:, (c % 2) * 128:(c % 2) * 128 + 128],
                              B["va"][:, c * 65:(c + 1) * 65], start=True, stop=False)
                    nd_pe[c] = mpe(t_.matmul(ndv, B["qT"][:, c * 128:(c + 1) * 128],
                                             B["so"][:, c * 65:(c + 1) * 65],
                                             start=False, stop=True))
                    a_.wait_ge(semPE, nd_pe[c])
                    a_.copy(attnum[:, c * D + h * DH: c * D + (h + 1) * DH], ndv[:, 0:64])
                    nd_act[c] = mact(a_.copy(den_all[:, h * 16 + c: h * 16 + c + 1],
                                             ndv[:, 64:65]))
                return nd_act[15]

            for u in make_A_units(0, 0):
                u()
            stage_U(0, 0)
            nd_read_act = 0
            aunits = make_A_units(1, vaug_m[0]) if H > 1 else []
            for h in range(H):
                nd_read_act = stage_D(h, aunits)
                if h + 1 < H:
                    for u in aunits:
                        u()
                    aunits = []
                    stage_U(h + 1, nd_read_act)
                    if h + 2 < H:
                        aunits = make_A_units(h + 2, vaug_m[h + 1])
            scan_done = scan_m[H - 1]
            nd_read = cDVE.v

            micro(19)
            gp.wait_ge(semDVE, scan_done)
            c_dma = wgp(gp.dma_start(out=contrib_dram.ap(), in_=contrib))
            gp.wait_ge(semGP, c_dma)
            gp.collective_compute(
                "AllGather", ALU.bypass,
                replica_groups=[list(range(NCORES))],
                ins=[contrib_dram.ap().opt()],
                outs=[gathered_dram.ap().opt()],
            ).then_inc(semCC, 1)
            cCC.inc()
            gp.wait_ge(semCC, cCC.v)
            g_reads = {}
            for r in range(NCORES):
                if r >= 2:
                    gp.wait_ge(semDVE, g_reads[r - 2])
                g_dma = wgp(gp.dma_start(out=gbuf[r % 2],
                                         in_=gathered_dram.ap()[r * 64:(r + 1) * 64, :]))
                v_.wait_ge(semGP, g_dma)
                if r == 0:
                    g_reads[0] = mdve(v_.tensor_scalar_mul(S0, gbuf[0], maskcol[:, 0:1]))
                else:
                    v_.wait_ge(semDVE, g_reads[r - 1])
                    g_reads[r] = mdve(v_.scalar_tensor_tensor(S0, gbuf[r % 2],
                                                              maskcol[:, r:r + 1], S0,
                                                              ALU.mult, ALU.add))
            v_.wait_ge(semDVE, g_reads[NCORES - 1])
            m_s0 = mdve(v_.tensor_copy(S0bf, S0))

            t_.wait_ge(semACT, nd_read_act)
            v_.wait_ge(semACT, nd_read_act)
            CQ = [qT_h, qT2]
            m_q = {0: qkv_comp(0, 0, qT_h, True, guard=nd_read)}
            m_q[1] = qkv_comp(1, 0, qT2, True, guard=m_q[0])
            t_.wait_ge(semDVE, m_s0)
            num_read = 0
            m_d1h = {}
            for h in range(H):
                qb = CQ[h % 2]
                t_.wait_ge(semDVE, m_q[h])
                if h > 0:
                    t_.wait_ge(semDVE, m_d1h[h - 1])
                for c in range(16):
                    mm = t_.matmul(pbig1[:, c:c + 1], qb[:, c * 128:(c + 1) * 128],
                                   S0bf[:, h * 65 + 64: h * 65 + 65], start=True, stop=True)
                m_dc = mpe(mm)
                if num_read:
                    t_.wait_ge(semACT, num_read)
                for c in range(16):
                    mm = t_.matmul(pbig0[:, c * 64:(c + 1) * 64],
                                   qb[:, c * 128:(c + 1) * 128],
                                   S0bf[:, h * 65: h * 65 + 64], start=True, stop=True)
                m_nc = mpe(mm)
                if h + 2 < H:
                    m_q[h + 2] = qkv_comp(h + 2, 0, CQ[(h + 2) % 2], True, guard=m_q[h + 1])
                v_.wait_ge(semPE, m_dc)
                if num_read:
                    v_.wait_ge(semACT, num_read)
                m_d1 = mdve(v_.tensor_add(rec[:, 0:16], den_all[:, h * 16:(h + 1) * 16],
                                          pbig1[:, 0:16]))
                m_d1h[h] = m_d1
                v_.wait_ge(semDVE, m_d1)
                m_d2 = mdve(v_.tensor_scalar_max(rec[:, 16:32], rec[:, 0:16], HEPS))
                v_.wait_ge(semDVE, m_d2)
                m_rec = mdve(v_.reciprocal(rec[:, 0:16], rec[:, 16:32]))
                v_.wait_ge(semDVE, m_rec)
                v_.wait_ge(semPE, m_nc)
                for half in range(2):
                    c0 = half * 8
                    if num_read:
                        v_.wait_ge(semACT, num_read)
                    m_nt = mdve(v_.tensor_add(
                        numtot.rearrange("p (c f) -> p c f", f=64),
                        pbig0[:, half * 512:(half + 1) * 512].rearrange("p (c f) -> p c f", f=64),
                        attnum.rearrange("p (c f) -> p c f", f=D)[:, c0:c0 + 8, h * DH:(h + 1) * DH]))
                    a_.wait_ge(semDVE, m_nt)
                    a_.wait_ge(semDVE, m_rec)
                    for cc in range(8):
                        c = c0 + cc
                        mm2 = a_.activation(
                            attnum[:, c * D + h * DH: c * D + (h + 1) * DH],
                            numtot[:, cc * 64:(cc + 1) * 64], AF.Copy,
                            scale=rec[:, c:c + 1])
                    num_read_act = mact(mm2)
                    num_read = num_read_act
            corr_done_act = num_read
            corr_done = cDVE.v
            qwt_free = cPE.v

            micro(20)
            t_.wait_ge(semDVE, corr_done)
            t_.wait_ge(semACT, corr_done_act)
            sy.wait_ge(semPE, qwt_free)
            sy.wait_ge(semDVE, corr_done)
            xs_dmas = {}
            xs_dmas[0] = wsp(sy.dma_start(out=xstage[0], in_=xspill.ap()[:, 0:D]))
            xs_dmas[1] = wsp(sy.dma_start(out=xstage[1], in_=xspill.ap()[:, D:2 * D]))
            attT_m = {}
            xup = {}
            attT_free = 0
            for t in range(NT):
                pb = pbf[t % 2]
                pbg = pbig[t % 2]
                if t >= 2:
                    t_.wait_ge(semDVE, attT_m[t - 2])
                for k in range(NK):
                    mm = t_.transpose(pb[:, k * 128:(k + 1) * 128],
                                      attnum[:, t * D + k * 128: t * D + (k + 1) * 128], identb)
                m_tt = mpe(mm)
                v_.wait_ge(semPE, m_tt)
                if attT_free:
                    v_.wait_ge(semPE, attT_free)
                m_aT = mdve(v_.tensor_copy(attT, pb[:, :]))
                attT_m[t] = m_aT
                t_.wait_ge(semDVE, m_aT)
                if t >= 2:
                    t_.wait_ge(semDVE, xup[t - 2])
                for nb in range(2):
                    for k in range(NK):
                        t_.matmul(pbg[:, nb * 512:(nb + 1) * 512],
                                  attT[:, k * 128:(k + 1) * 128],
                                  owt[:, k * D + nb * 512: k * D + (nb + 1) * 512],
                                  start=(k == 0), stop=False)
                    mm = t_.matmul(pbg[:, nb * 512:(nb + 1) * 512], onesbf[0:1, 0:128],
                                   outbbf[0:1, nb * 512:(nb + 1) * 512], start=False, stop=True)
                m_op = mpe(mm)
                attT_free = m_op
                v_.wait_ge(semPE, m_op)
                v_.wait_ge(semSP, xs_dmas[t])
                v_.tensor_add(xsb[:, t * D: t * D + 512], xstage[t % 2][:, 0:512],
                              pbg[:, 0:512])
                xup[t] = mdve(v_.tensor_add(xsb[:, t * D + 512: t * D + 1024],
                                            xstage[t % 2][:, 512:1024], pbg[:, 512:1024]))
                if t + 2 < NT:
                    sy.wait_ge(semDVE, xup[t])
                    xs_dmas[t + 2] = wsp(sy.dma_start(
                        out=xstage[t % 2], in_=xspill.ap()[:, (t + 2) * D:(t + 3) * D]))

            if STOP_AFTER >= 3:
                ffn_emit()

          except StopEmission:
            pass
          if DEBUG_DUMPS:
            wall(gp)
            cGP.inc(16); gp.dma_start(out=dbg_lnT, in_=lnT).then_inc(semGP, 16)
            cGP.inc(16); gp.dma_start(out=dbg_hT, in_=hT).then_inc(semGP, 16)
            cGP.inc(16); gp.dma_start(out=dbg_gates, in_=gates_all).then_inc(semGP, 16)
            cGP.inc(16); gp.dma_start(out=dbg_misc[0:64, 0:48], in_=qkvb64).then_inc(semGP, 16)
            cGP.inc(16); gp.dma_start(out=dbg_misc[:, 48:112], in_=probs).then_inc(semGP, 16)
            cGP.inc(16); gp.dma_start(out=dbg_misc[:, 112:120], in_=m8sb).then_inc(semGP, 16)
            cGP.inc(16); gp.dma_start(out=dbg_misc[:, 120:136], in_=stats[:, 0:16]).then_inc(semGP, 16)
            if STOP_AFTER >= 2:
                cGP.inc(16); gp.dma_start(out=dbg_misc[:, 136:168], in_=rec).then_inc(semGP, 16)
                cGP.inc(16); gp.dma_start(out=dbg_misc[:, 168:232], in_=numtot[:, 448:512]).then_inc(semGP, 16)
                cGP.inc(16); gp.dma_start(out=dbg_att, in_=attnum).then_inc(semGP, 16)
                cGP.inc(16); gp.dma_start(out=dbg_den, in_=den_all).then_inc(semGP, 16)
                cGP.inc(16); gp.dma_start(out=dbg_st[:, 0:H * 65], in_=contrib).then_inc(semGP, 16)
                cGP.inc(16); gp.dma_start(out=dbg_st[:, H * 65:2 * H * 65], in_=S0).then_inc(semGP, 16)
                cGP.inc(16); gp.dma_start(out=dbg_st[:, 2 * H * 65:3 * H * 65], in_=S0bf).then_inc(semGP, 16)
                cGP.inc(16); gp.dma_start(out=dbg_st[:, 3 * H * 65:3 * H * 65 + 1040], in_=sown).then_inc(semGP, 16)
            gp.wait_ge(semGP, cGP.v)
          wall(sy)
          m_out = wsp(sy.dma_start(out=ydram.rearrange("(t p) d -> p t d", p=128),
                                   in_=xsb.rearrange("p (t d) -> p t d", d=D)))
          sy.wait_ge(semSP, m_out)

    return nc


def kernel(L0_attn_x, L0_attn_qkv_w, L0_attn_qkv_b, L0_attn_out_w, L0_attn_out_b,
           L0_ffn_wg1, L0_ffn_bg1, L0_ffn_wg2, L0_ffn_bg2, L0_ffn_We, L0_ffn_be):
    x = np.ascontiguousarray(np.asarray(L0_attn_x, dtype=np.float32))
    qkv_w = np.ascontiguousarray(np.asarray(L0_attn_qkv_w, dtype=np.float32))
    qkv_b = np.ascontiguousarray(np.asarray(L0_attn_qkv_b, dtype=np.float32)).reshape(1, 3 * D)
    out_w = np.ascontiguousarray(np.asarray(L0_attn_out_w, dtype=np.float32))
    out_b = np.ascontiguousarray(np.asarray(L0_attn_out_b, dtype=np.float32)).reshape(1, D)
    wg1 = np.ascontiguousarray(np.asarray(L0_ffn_wg1, dtype=np.float32))
    bg1 = np.ascontiguousarray(np.asarray(L0_ffn_bg1, dtype=np.float32)).reshape(1, 256)
    wg2 = np.ascontiguousarray(np.asarray(L0_ffn_wg2, dtype=np.float32))
    bg2 = np.ascontiguousarray(np.asarray(L0_ffn_bg2, dtype=np.float32)).reshape(1, 40)
    We = np.ascontiguousarray(np.asarray(L0_ffn_We, dtype=np.float32)).reshape(NE * D, D)
    be = np.ascontiguousarray(np.asarray(L0_ffn_be, dtype=np.float32))

    identf = np.eye(128, dtype=np.float32)
    identb = np.eye(128, dtype=ml_dtypes.bfloat16)
    triu = np.triu(np.ones((128, 128), dtype=np.float32)).astype(ml_dtypes.bfloat16)

    nc = build_program()
    in_maps = []
    for c in range(NCORES):
        b, half = c // 2, c % 2
        msk = np.zeros((1, 8), dtype=np.float32)
        if half == 1:
            msk[0, c - 1] = 1.0
        in_maps.append({
            "xin": np.ascontiguousarray(x[b, half * T:(half + 1) * T]),
            "qkvw": qkv_w, "qkvb": qkv_b, "outw": out_w, "outb": out_b,
            "wg1": wg1, "bg1": bg1, "wg2": wg2, "bg2": bg2,
            "wed": We, "bed": be,
            "identf": identf, "identb": identb, "triu": triu,
            "maskrow": msk,
        })
    res = run_bass_kernel_spmd(nc, in_maps, core_ids=list(range(NCORES)))
    if DEBUG_DUMPS:
        DBG.clear()
        DBG.update({k: [res.results[c][k] for c in range(NCORES)]
                    for k in res.results[0] if k.startswith("dbg_")})
    out = np.empty((4, 4096, D), dtype=np.float32)
    for c in range(NCORES):
        b, half = c // 2, c % 2
        out[b, half * T:(half + 1) * T] = res.results[c]["y"]
    return out


# version 10
# speedup vs baseline: 1.0455x; 1.0185x over previous
import numpy as np
import ml_dtypes
from contextlib import ExitStack

import concourse.bass as bass
import concourse.mybir as mybir
from concourse.bass_utils import run_bass_kernel_spmd

F32 = mybir.dt.float32
BF16 = mybir.dt.bfloat16
AF = mybir.ActivationFunctionType
ALU = mybir.AluOpType
AX = mybir.AxisListType

NCORES = 8
T = 2048
NT = 16
D = 1024
NK = 8
H = 16
DH = 64
NE = 20
HEPS = 1e-6
LNEPS = 1e-5
STOP_AFTER = 3
STOP_MICRO = None
DEBUG_DUMPS = False
DBG = {}


class StopEmission(Exception):
    pass


class Cnt:
    def __init__(self):
        self.v = 0

    def inc(self, n=1):
        self.v += n
        return self.v


def build_program():
    nc = bass.Bass("TRN2", target_bir_lowering=False, debug=False)

    xin = nc.dram_tensor("xin", [T, D], F32, kind="ExternalInput").ap()
    qkvw = nc.dram_tensor("qkvw", [3 * D, D], F32, kind="ExternalInput").ap()
    qkvb = nc.dram_tensor("qkvb", [1, 3 * D], F32, kind="ExternalInput").ap()
    outw = nc.dram_tensor("outw", [D, D], F32, kind="ExternalInput").ap()
    qkvwT = nc.dram_tensor("qkvwT", [D, 3 * D], F32, kind="ExternalInput").ap()
    outwT = nc.dram_tensor("outwT", [D, D], F32, kind="ExternalInput").ap()
    outb = nc.dram_tensor("outb", [1, D], F32, kind="ExternalInput").ap()
    wg1d = nc.dram_tensor("wg1", [D, 256], F32, kind="ExternalInput").ap()
    bg1d = nc.dram_tensor("bg1", [1, 256], F32, kind="ExternalInput").ap()
    wg2d = nc.dram_tensor("wg2", [256, 40], F32, kind="ExternalInput").ap()
    bg2d = nc.dram_tensor("bg2", [1, 40], F32, kind="ExternalInput").ap()
    wed = nc.dram_tensor("wed", [NE * D, D], F32, kind="ExternalInput").ap()
    bed = nc.dram_tensor("bed", [NE, D], F32, kind="ExternalInput").ap()
    identf_d = nc.dram_tensor("identf", [128, 128], F32, kind="ExternalInput").ap()
    identb_d = nc.dram_tensor("identb", [128, 128], BF16, kind="ExternalInput").ap()
    triu_d = nc.dram_tensor("triu", [128, 128], BF16, kind="ExternalInput").ap()
    maskrow_d = nc.dram_tensor("maskrow", [1, 8], F32, kind="ExternalInput").ap()
    ydram = nc.dram_tensor("y", [T, D], F32, kind="ExternalOutput").ap()
    if DEBUG_DUMPS:
        dbg_lnT = nc.dram_tensor("dbg_lnT", [128, NK * T], F32, kind="ExternalOutput").ap()
        dbg_hT = nc.dram_tensor("dbg_hT", [128, 2 * T], F32, kind="ExternalOutput").ap()
        dbg_gates = nc.dram_tensor("dbg_gates", [128, NT * NE], F32, kind="ExternalOutput").ap()
        dbg_misc = nc.dram_tensor("dbg_misc", [128, 256], F32, kind="ExternalOutput").ap()
        dbg_att = nc.dram_tensor("dbg_att", [128, NT * D], F32, kind="ExternalOutput").ap()
        dbg_den = nc.dram_tensor("dbg_den", [128, 256], F32, kind="ExternalOutput").ap()
        dbg_st = nc.dram_tensor("dbg_st", [64, 4 * H * 65], F32, kind="ExternalOutput").ap()

    qkvT_hbm = nc.dram_tensor("qkvT_hbm", [NK, 128, 3 * D], BF16)
    xspill = nc.dram_tensor("xspill", [128, NT * D], F32)
    contrib_dram = nc.dram_tensor("contrib_dram", [64, H * 65], F32)
    gathered_dram = nc.dram_tensor("gathered_dram", [NCORES * 64, H * 65], F32)

    B0 = (nc.sbuf_base + 31) // 32 * 32
    A = B0
    B = A + 66560
    C = B + 32768
    Dr = C + 49152
    E = Dr + 16384

    at = lambda name, shape, dt, offset: nc.alloc_sbuf_tensor_at(name, shape, dt, offset=offset).ap()

    xsb = at("xsb", [128, NT * D], F32, A)
    attnum = at("attnum", [128, NT * D], BF16, A + 32768)
    U_sb = at("U_sb", [64, 2048], F32, A)
    contrib = at("contrib", [64, H * 65], F32, A + 8192)
    den_all = at("den_all", [128, 256], F32, A + 12416)
    rec = at("rec", [128, 32], F32, A + 13440)
    gbuf = [at(f"gbuf{i}", [64, H * 65], F32, A + i * 4192) for i in range(2)]
    qT2 = at("qT2", [64, T], BF16, A + 13568)
    kT2 = at("kT2", [64, T], BF16, A + 17664)
    vT2 = at("vT2", [64, T], BF16, A + 21760)
    ktok2 = at("ktok2", [128, 16 * 64], BF16, A + 25856)
    v_aug2 = at("v_aug2", [128, 16 * 65], BF16, A + 27904)
    sown2 = at("sown2", [64, 16 * 65], BF16, A + 30016)

    lnT = at("lnT", [128, NK * T], BF16, B)
    asm = [at(f"asm{i}", [128, 3 * D], BF16, B + i * 6144) for i in range(2)]
    qkvb_row = at("qkvb_row", [1, 3 * D], F32, B + 12288)
    bg1row = at("bg1row", [1, 256], F32, B + 24576)

    stage = at("stage", [128, 24 * D], BF16, C)
    web = [at(f"web{i}", [128, NK * D], BF16, C + i * 16384) for i in range(2)]
    hT = at("hT", [128, 2 * T], BF16, C + 32768)
    qwt = at("qwt", [128, NK * 3 * D], BF16, C)
    xstage = [at(f"xstage{i}", [128, D], F32, C + i * 4096) for i in range(2)]
    attT2b = [at(f"attT{i}", [128, D], BF16, C + 8192 + i * 2048) for i in range(2)]

    owt = at("owt", [128, NK * D], BF16, Dr)

    off = [E]

    def tal(name, shape, dt):
        per = int(np.prod(shape[1:])) * mybir.dt.size(dt)
        o = (off[0] + 31) // 32 * 32
        off[0] = o + per
        return at(name, shape, dt, o)

    identf = tal("identfs", [128, 128], F32)
    identb = tal("identbs", [128, 128], BF16)
    triu = tal("trius", [128, 128], BF16)
    wg1b = tal("wg1b", [128, NK * 256], BF16)
    wg2b = tal("wg2b", [128, 2 * 40], BF16)
    beb = tal("beb", [NE, D], BF16)
    qkvb64 = tal("qkvb64", [64, 48], F32)
    outbbf = tal("outbbf", [1, D], BF16)
    bg2row = tal("bg2row", [1, 40], BF16)
    onesbf = tal("onesbf", [1, 512], BF16)
    onesf = tal("onesf", [1, 64], F32)
    maskrow = tal("maskrow_sb", [1, 8], F32)
    maskcol = tal("maskcol", [64, 8], F32)
    gates_all = tal("gates_all", [128, NT * NE], F32)
    gatesT = tal("gatesT", [NE, NT * 128], BF16)
    stats = tal("stats", [128, 64], F32)
    lnstage = tal("lnstage", [128, D], BF16)
    probs = tal("probs", [128, 64], F32)
    m8sb = tal("m8sb", [128, 8], F32)
    bg1cols = tal("bg1cols", [128, 2], F32)
    qT_h = tal("qT_h", [64, T], BF16)
    kT_h = tal("kT_h", [64, T], BF16)
    vT_h = tal("vT_h", [64, T], BF16)
    ktok = tal("ktok", [128, 16 * 64], BF16)
    v_aug = tal("v_aug", [128, 16 * 65], BF16)
    sown = tal("sown", [64, 16 * 65], BF16)
    S0 = tal("S0", [64, H * 65], F32)
    S0bf = tal("S0bf", [64, H * 65], BF16)
    sacc = tal("sacc", [64, 72], F32)
    numtot = tal("numtot", [128, 512], F32)
    epscol = tal("epscol", [128, 1], F32)
    atm2 = tal("atm2", [128, 256], BF16)
    assert off[0] <= 229340, f"tail overflow {off[0]}"

    with ExitStack() as ctx:
        pbig0 = ctx.enter_context(nc.psum_tensor("pbig0", [128, 1024], F32))
        pbig1 = ctx.enter_context(nc.psum_tensor("pbig1", [128, 1024], F32))
        psm0 = ctx.enter_context(nc.psum_tensor("psm0", [128, 512], F32))
        psm1 = ctx.enter_context(nc.psum_tensor("psm1", [128, 512], F32))
        pbf0 = ctx.enter_context(nc.psum_tensor("pbf0", [128, 1024], BF16))
        pbf1 = ctx.enter_context(nc.psum_tensor("pbf1", [128, 1024], BF16))
        pbig = [pbig0, pbig1]
        psm = [psm0, psm1]
        pbf = [pbf0, pbf1]

        semPE = ctx.enter_context(nc.semaphore("semPE"))
        semDVE = ctx.enter_context(nc.semaphore("semDVE"))
        semACT = ctx.enter_context(nc.semaphore("semACT"))
        semIN = ctx.enter_context(nc.semaphore("semIN"))
        semW = ctx.enter_context(nc.semaphore("semW"))
        semSP = ctx.enter_context(nc.semaphore("semSP"))
        semGP = ctx.enter_context(nc.semaphore("semGP"))
        semCC = ctx.enter_context(nc.semaphore("semCC"))
        block = ctx.enter_context(nc.Block())

        cPE, cDVE, cACT = Cnt(), Cnt(), Cnt()
        cIN, cW, cSP, cGP, cCC = Cnt(), Cnt(), Cnt(), Cnt(), Cnt()

        t_, v_, a_, sy, gp = nc.tensor, nc.vector, nc.scalar, nc.sync, nc.gpsimd

        def mpe(i):
            i.then_inc(semPE, 1)
            return cPE.inc()

        def mdve(i):
            i.then_inc(semDVE, 1)
            return cDVE.inc()

        def mact(i):
            i.then_inc(semACT, 1)
            return cACT.inc()

        def win(i):
            i.then_inc(semIN, 16)
            return cIN.inc(16)

        def ww(i):
            i.then_inc(semW, 16)
            return cW.inc(16)

        def wsp(i):
            i.then_inc(semSP, 16)
            return cSP.inc(16)

        def wgp(i):
            i.then_inc(semGP, 16)
            return cGP.inc(16)

        def wall(eng):
            for sem, c in ((semPE, cPE), (semDVE, cDVE), (semACT, cACT), (semIN, cIN),
                           (semW, cW), (semSP, cSP), (semGP, cGP), (semCC, cCC)):
                if c.v:
                    eng.wait_ge(sem, c.v)

        @block.sync
        def _(sync):
          try:
            def micro(n):
                if STOP_MICRO is not None and STOP_MICRO == n:
                    raise StopEmission()

            win(sy.dma_start(out=xsb.rearrange("p (t d) -> p t d", d=D),
                             in_=xin.rearrange("(t p) d -> p t d", p=128)))
            micro(1)
            win(sy.dma_start(out=identf, in_=identf_d))
            win(sy.dma_start(out=identb, in_=identb_d))
            win(sy.dma_start(out=triu, in_=triu_d))
            win(sy.dma_start(out=maskrow, in_=maskrow_d))
            win(sy.dma_start(out=qkvb_row, in_=qkvb))
            win(sy.dma_start(out=bg1row, in_=bg1d))
            in_all = cIN.v

            ww(gp.dma_start(out=wg1b.rearrange("p (k f) -> p k f", k=NK),
                            in_=wg1d.rearrange("(k p) f -> p k f", p=128)))
            ww(gp.dma_start(out=wg2b.rearrange("p (k f) -> p k f", k=2),
                            in_=wg2d.rearrange("(k p) f -> p k f", p=128)))
            ww(gp.dma_start(out=beb, in_=bed))
            ww(gp.dma_start(out=outbbf, in_=outb))
            ww(gp.dma_start(out=bg2row, in_=bg2d))
            ww(gp.dma_start(out=owt.rearrange("p (k o) -> p k o", k=NK),
                            in_=outwT.rearrange("(k p) o -> p k o", p=128)))

            v_.memset(onesbf, 1.0)
            v_.memset(onesf, 1.0)
            m_ones = mdve(v_.memset(v_aug.rearrange("p (c s) -> p c s", s=65)[:, :, 64:65], 1.0))
            micro(3)

            t_.wait_ge(semIN, in_all)
            t_.wait_ge(semDVE, m_ones)
            for c in range(3):
                for h in range(H):
                    j = c * 16 + h
                    t_.matmul(psm0[0:64, j:j + 1],
                              qkvb_row[0:1, c * D + h * DH: c * D + (h + 1) * DH],
                              onesf[0:1, 0:1], start=True, stop=True)
            t_.matmul(psm1[0:128, 0:1], bg1row[0:1, 0:128], onesf[0:1, 0:1],
                      start=True, stop=True)
            t_.matmul(psm1[0:128, 1:2], bg1row[0:1, 128:256], onesf[0:1, 0:1],
                      start=True, stop=True)
            m_cols = mpe(t_.matmul(psm1[0:64, 8:16], onesf[0:1, 0:64], maskrow[0:1, 0:8],
                                   start=True, stop=True))
            v_.wait_ge(semPE, m_cols)
            m_cp1 = mdve(v_.tensor_copy(qkvb64, psm0[0:64, 0:48]))
            v_.tensor_copy(bg1cols, psm1[0:128, 0:2])
            v_.tensor_copy(maskcol, psm1[0:64, 8:16])
            v_.wait_ge(semDVE, m_cp1)
            m_bias = mdve(v_.tensor_scalar(qkvb64[:, 0:32], qkvb64[:, 0:32], HEPS, None, ALU.add))
            micro(6)

            def ln_emit():
                m_sd = {}
                for t in range(NT):
                    xt = xsb[:, t * D:(t + 1) * D]
                    sc4 = stats[:, t * 4: t * 4 + 4]
                    bn = probs[:, (t % 2) * 12:(t % 2) * 12 + 12]
                    v_.bn_stats(bn[:, 0:6], xt[:, 0:512])
                    m_b1 = mdve(v_.bn_stats(bn[:, 6:12], xt[:, 512:1024]))
                    v_.wait_ge(semDVE, m_b1)
                    m_b2 = mdve(v_.bn_aggr(bn[:, 0:2] if False else probs[:, 24:26], bn[:, 0:12]))
                    v_.wait_ge(semDVE, m_b2)
                    v_.tensor_scalar_mul(sc4[:, 0:1], probs[:, 24:25], -1.0)
                    m_var = mdve(v_.tensor_scalar_add(sc4[:, 1:2], probs[:, 25:26], LNEPS))
                    a_.wait_ge(semDVE, m_var)
                    m_sd[t] = mact(a_.activation(sc4[:, 2:3], sc4[:, 1:2], AF.Sqrt))
                v_.wait_ge(semACT, m_sd[NT - 1])
                sdv = stats.rearrange("p (t f) -> p t f", f=4)
                m_rc = mdve(v_.reciprocal(sdv[:, :, 3:4], sdv[:, :, 2:3]))
                v_.wait_ge(semDVE, m_rc)
                pbf_free = [0, 0]
                tp_m = {}
                lnT_done = 0
                for t in range(NT):
                    xt = xsb[:, t * D:(t + 1) * D]
                    sc4 = stats[:, t * 4: t * 4 + 4]
                    if t >= 1:
                        v_.wait_ge(semPE, tp_m[t - 1])
                    m_ln = mdve(v_.tensor_scalar(lnstage, xt, sc4[:, 0:1], sc4[:, 3:4],
                                                 ALU.add, ALU.mult))
                    pb = pbf[t % 2]
                    t_.wait_ge(semDVE, m_ln)
                    if pbf_free[t % 2]:
                        t_.wait_ge(semDVE, pbf_free[t % 2])
                    for k in range(NK):
                        mm = t_.transpose(pb[:, k * 128:(k + 1) * 128],
                                          lnstage[:, k * 128:(k + 1) * 128], identb)
                    tp_m[t] = mpe(mm)
                    t_.matmul(pbig1[:, 0:512], identb, lnstage[:, 0:512], start=True, stop=True)
                    t_.matmul(pbig1[:, 0:512], identb, lnstage[:, 0:512], start=True, stop=True)
                    v_.wait_ge(semPE, tp_m[t])
                    lnT_done = mdve(v_.tensor_copy(
                        lnT.rearrange("p (k s) -> p k s", k=NK)[:, :, t * 128:(t + 1) * 128],
                        pb.rearrange("p (k s) -> p k s", k=NK)))
                    pbf_free[t % 2] = lnT_done
                return lnT_done

            def ffn_emit():
                wall(t_); wall(v_); wall(a_); wall(gp)

                web_loaded = {}
                for e in range(2):
                    web_loaded[e] = ww(gp.dma_start(
                        out=web[e].rearrange("p (k f) -> p k f", k=NK),
                        in_=wed[e * D:(e + 1) * D, :].rearrange("(k p) f -> p k f", p=128)))

                lnT_done = ln_emit()

                t_.wait_ge(semDVE, lnT_done)
                h_acts = []
                for fc in range(2):
                    for nb in range(4):
                        ps = psm[nb % 2]
                        if len(h_acts) >= 2:
                            t_.wait_ge(semACT, h_acts[-2])
                        for k in range(NK):
                            mm = t_.matmul(ps[:, :],
                                           wg1b[:, k * 256 + fc * 128: k * 256 + fc * 128 + 128],
                                           lnT[:, k * T + nb * 512: k * T + (nb + 1) * 512],
                                           start=(k == 0), stop=(k == NK - 1))
                        mk = mpe(mm)
                        a_.wait_ge(semPE, mk)
                        h_acts.append(mact(a_.activation(
                            hT[:, fc * T + nb * 512: fc * T + (nb + 1) * 512], ps[:, :],
                            AF.Gelu, bias=bg1cols[:, fc:fc + 1])))
                h_all = cACT.v

                t_.wait_ge(semACT, h_all)
                for t in range(NT):
                    for fc in range(2):
                        t_.matmul(psm0[:, t * 20:(t + 1) * 20],
                                  hT[:, fc * T + t * 128: fc * T + (t + 1) * 128],
                                  wg2b[:, fc * 40: fc * 40 + 20],
                                  start=(fc == 0), stop=False)
                    mm = t_.matmul(psm0[:, t * 20:(t + 1) * 20], onesbf[0:1, 0:128],
                                   bg2row[0:1, 0:20], start=False, stop=True)
                m_lg = mpe(mm)
                v_.wait_ge(semPE, m_lg)
                m_nm = mdve(v_.tensor_reduce(
                    stats[:, 0:16], psm0[:, 0:320].rearrange("p (t e) -> p t e", e=20),
                    AX.X, ALU.max, negate=True))
                a_.wait_ge(semDVE, m_nm)
                pall = numtot[:, 0:320]
                for t in range(NT):
                    m_exp = mact(a_.activation(pall[:, t * 20:(t + 1) * 20],
                                               psm0[:, t * 20:(t + 1) * 20], AF.Exp,
                                               bias=stats[:, t:t + 1]))
                exp_read = m_exp
                v_.wait_ge(semACT, exp_read)
                m_rs = mdve(v_.tensor_reduce(
                    stats[:, 16:32], pall.rearrange("p (t e) -> p t e", e=20),
                    AX.X, ALU.add))
                v_.wait_ge(semDVE, m_rs)
                m_rc = mdve(v_.reciprocal(stats[:, 32:48], stats[:, 16:32]))
                v_.wait_ge(semDVE, m_rc)
                gT_evac = 0
                m8_read = 0
                for t in range(NT):
                    m_ps = mdve(v_.tensor_scalar(gates_all[:, t * NE:(t + 1) * NE],
                                                 pall[:, t * 20:(t + 1) * 20],
                                                 stats[:, 32 + t:33 + t], 0.5,
                                                 ALU.mult, ALU.mult))
                    v_.wait_ge(semDVE, m_ps)
                    m_m8 = mdve(v_.max(m8sb[:, 0:8], gates_all[:, t * NE:(t + 1) * NE]))
                    v_.wait_ge(semDVE, m_m8)
                    m8_read = mdve(v_.scalar_tensor_tensor(
                        gates_all[:, t * NE:(t + 1) * NE],
                        gates_all[:, t * NE:(t + 1) * NE], m8sb[:, 1:2],
                        gates_all[:, t * NE:(t + 1) * NE], ALU.is_ge, ALU.mult))
                    t_.wait_ge(semDVE, m8_read)
                    if gT_evac:
                        t_.wait_ge(semDVE, gT_evac)
                    m_gt = mpe(t_.transpose(psm1[0:20, 0:128],
                                            gates_all[:, t * NE:(t + 1) * NE], identf))
                    v_.wait_ge(semPE, m_gt)
                    gT_evac = mdve(v_.tensor_copy(gatesT[:, t * 128:(t + 1) * 128],
                                                  psm1[0:20, 0:128]))

                bank_free = [0, 0]
                t_.wait_ge(semDVE, gT_evac)
                for t in range(NT):
                    pb = pbig[t % 2]
                    if bank_free[t % 2]:
                        t_.wait_ge(semDVE, bank_free[t % 2])
                    t_.matmul(pb[:, 0:512], gatesT[:, t * 128:(t + 1) * 128], beb[:, 0:512],
                              start=True, stop=True)
                    m_be = mpe(t_.matmul(pb[:, 512:1024], gatesT[:, t * 128:(t + 1) * 128],
                                         beb[:, 512:1024], start=True, stop=True))
                    v_.wait_ge(semPE, m_be)
                    v_.tensor_add(xsb[:, t * D: t * D + 512], xsb[:, t * D: t * D + 512],
                                  pb[:, 0:512])
                    bank_free[t % 2] = mdve(v_.tensor_add(
                        xsb[:, t * D + 512: t * D + 1024], xsb[:, t * D + 512: t * D + 1024],
                        pb[:, 512:1024]))

                micro(11)
                stt_m = {}
                t_.wait_ge(semACT, exp_read)
                ebanks = [(pbig0[:, 0:512], pbig0[:, 512:1024]),
                          (pbig1[:, 0:512], pbig1[:, 512:1024]),
                          (psm0[:, 0:512], psm1[:, 0:512])]
                for e in range(NE):
                    t_.wait_ge(semW, web_loaded[e])
                    mm_last = 0
                    for t in range(NT):
                        idx = e * NT + t
                        bh = ebanks[idx % 3]
                        if idx - 3 >= 0:
                            t_.wait_ge(semDVE, stt_m[idx - 3])
                        elif bank_free[idx % 2]:
                            t_.wait_ge(semDVE, bank_free[idx % 2])
                        for k in range(NK):
                            for nb in range(2):
                                mm = t_.matmul(bh[nb],
                                               lnT[:, k * T + t * 128: k * T + (t + 1) * 128],
                                               web[e % 2][:, k * D + nb * 512: k * D + (nb + 1) * 512],
                                               start=(k == 0), stop=(k == NK - 1))
                        mm_last = mpe(mm)
                        v_.wait_ge(semPE, mm_last)
                        stt_m[idx] = mdve(v_.scalar_tensor_tensor(
                            xsb[:, t * D: t * D + 512], bh[0],
                            gates_all[:, t * NE + e: t * NE + e + 1],
                            xsb[:, t * D: t * D + 512], ALU.mult, ALU.add))
                        stt_m[idx] = mdve(v_.scalar_tensor_tensor(
                            xsb[:, t * D + 512:(t + 1) * D], bh[1],
                            gates_all[:, t * NE + e: t * NE + e + 1],
                            xsb[:, t * D + 512:(t + 1) * D], ALU.mult, ALU.add))
                    if e + 2 < NE:
                        gp.wait_ge(semPE, mm_last)
                        web_loaded[e + 2] = ww(gp.dma_start(
                            out=web[e % 2].rearrange("p (k f) -> p k f", k=NK),
                            in_=wed[(e + 2) * D:(e + 3) * D, :].rearrange("(k p) f -> p k f", p=128)))

            if STOP_AFTER >= 1:
                ffn_emit()

            if STOP_AFTER < 2:
                raise StopEmission()
            wall(t_); wall(v_); wall(a_); wall(gp); wall(sy)

            lnT_done = ln_emit()

            sy.wait_ge(semDVE, lnT_done)
            sp_x = wsp(sy.dma_start(out=xspill.ap(), in_=xsb))
            for part in range(3):
                w_qwt = ww(gp.dma_start(
                    out=qwt.rearrange("p (k o) -> p k o", k=NK)[:, :, part * D:(part + 1) * D],
                    in_=qkvwT.rearrange("(k p) o -> p k o", p=128)[:, :, part * D:(part + 1) * D]))

            t_.wait_ge(semW, w_qwt)
            t_.wait_ge(semSP, sp_x)
            v_.wait_ge(semW, w_qwt)
            v_.wait_ge(semSP, sp_x)
            a_.wait_ge(semSP, sp_x)
            mdve(v_.memset(v_aug2.rearrange("p (c s) -> p c s", s=65)[:, :, 64:65], 1.0))

            def qkv_comp(h, comp, dst, with_relu, guard=0):
                biascol = qkvb64[:, comp * 16 + h: comp * 16 + h + 1]
                ms = []
                for nb in range(4):
                    ps = psm[nb % 2]
                    if len(ms) >= 2:
                        t_.wait_ge(semDVE, ms[-2])
                    elif guard:
                        t_.wait_ge(semDVE, guard)
                    for k in range(NK):
                        mm = t_.matmul(ps[0:64, :],
                                       qwt[:, k * 3072 + comp * D + h * DH: k * 3072 + comp * D + (h + 1) * DH],
                                       lnT[:, k * T + nb * 512: k * T + (nb + 1) * 512],
                                       start=(k == 0), stop=(k == NK - 1))
                    mk = mpe(mm)
                    v_.wait_ge(semPE, mk)
                    if with_relu:
                        ms.append(mdve(v_.tensor_scalar(dst[:, nb * 512:(nb + 1) * 512],
                                                        ps[0:64, :], biascol, HEPS,
                                                        ALU.add, ALU.max)))
                    else:
                        ms.append(mdve(v_.tensor_scalar(dst[:, nb * 512:(nb + 1) * 512],
                                                        ps[0:64, :], biascol, None, ALU.add)))
                return ms[-1]

            BUF = [dict(qT=qT_h, kT=kT_h, vT=vT_h, kt=ktok, va=v_aug, so=sown),
                   dict(qT=qT2, kT=kT2, vT=vT2, kt=ktok2, va=v_aug2, so=sown2)]
            vaug_m = {}
            scan_m = {}
            scan_steps = {}

            def make_A_units(h, guard0):
                B = BUF[h % 2]
                st = {"ms": []}
                dsts = [B["qT"], B["kT"], B["vT"]]
                units = []

                def qunit(comp, nb):
                    def go():
                        ps = psm[(comp * 4 + nb) % 2]
                        if len(st["ms"]) >= 2:
                            t_.wait_ge(semDVE, st["ms"][-2])
                        elif guard0:
                            t_.wait_ge(semDVE, guard0)
                        for k in range(NK):
                            mm = t_.matmul(
                                ps[0:64, :],
                                qwt[:, k * 3072 + comp * D + h * DH: k * 3072 + comp * D + (h + 1) * DH],
                                lnT[:, k * T + nb * 512: k * T + (nb + 1) * 512],
                                start=(k == 0), stop=(k == NK - 1))
                        mk = mpe(mm)
                        v_.wait_ge(semPE, mk)
                        biascol = qkvb64[:, comp * 16 + h: comp * 16 + h + 1]
                        if comp < 2:
                            st["ms"].append(mdve(v_.tensor_scalar(
                                dsts[comp][:, nb * 512:(nb + 1) * 512], ps[0:64, :],
                                biascol, HEPS, ALU.add, ALU.max)))
                        else:
                            st["ms"].append(mdve(v_.tensor_scalar(
                                dsts[comp][:, nb * 512:(nb + 1) * 512], ps[0:64, :],
                                biascol, None, ALU.add)))
                    return go

                for comp in range(3):
                    for nb in range(4):
                        units.append(qunit(comp, nb))

                def ktrans():
                    t_.wait_ge(semDVE, st["ms"][-1])
                    for c in range(16):
                        mmk = t_.transpose(pbf0[:, c * 64:(c + 1) * 64],
                                           B["kT"][:, c * 128:(c + 1) * 128], identb[0:64, 0:64])
                    mpe(mmk)

                def vtrans():
                    for c in range(16):
                        mmv = t_.transpose(pbf1[:, c * 64:(c + 1) * 64],
                                           B["vT"][:, c * 128:(c + 1) * 128], identb[0:64, 0:64])
                    m_tv = mpe(mmv)
                    v_.wait_ge(semPE, m_tv)
                    v_.tensor_copy(B["kt"], pbf0[:, :])
                    vaug_m[h] = mdve(v_.tensor_copy(
                        B["va"].rearrange("p (c s) -> p c s", s=65)[:, :, 0:64],
                        pbf1.rearrange("p (c s) -> p c s", s=64)))

                units += [ktrans, vtrans]
                return units

            def stage_U(h, pbig_free_act):
                B = BUF[h % 2]
                if pbig_free_act:
                    t_.wait_ge(semACT, pbig_free_act)
                t_.wait_ge(semDVE, vaug_m[h])
                for c in range(16):
                    pbU = pbig[c // 8]
                    mm = t_.matmul(pbU[0:64, (c % 8) * 128:(c % 8) * 128 + 65],
                                   B["kt"][:, c * 64:(c + 1) * 64],
                                   B["va"][:, c * 65:(c + 1) * 65], start=True, stop=True)
                m_U = mpe(mm)
                v_.wait_ge(semPE, m_U)
                v_.tensor_copy(U_sb[:, 0:1024], pbig0[0:64, :])
                mdve(v_.tensor_copy(U_sb[:, 1024:2048], pbig1[0:64, :]))
                Us = U_sb.rearrange("p (c s) -> p c s", s=128)
                steps = {}
                steps[0] = mdve(v_.memset(B["so"][:, 0:65], 0.0))
                m_sc = mdve(v_.tensor_copy(sacc[:, 0:65], Us[:, 0, 0:65]))
                for c in range(1, 16):
                    v_.wait_ge(semDVE, m_sc)
                    steps[c] = mdve(v_.tensor_copy(B["so"][:, c * 65:(c + 1) * 65],
                                                   sacc[:, 0:65]))
                    if c < 15:
                        m_sc = mdve(v_.tensor_add(sacc[:, 0:65], sacc[:, 0:65],
                                                  Us[:, c, 0:65]))
                scan_m[h] = mdve(v_.tensor_add(contrib[:, h * 65:(h + 1) * 65],
                                               sacc[:, 0:65], Us[:, 15, 0:65]))
                scan_steps[h] = steps

            def stage_D(h, aunits):
                B = BUF[h % 2]
                t_.wait_ge(semDVE, scan_steps[h][1])
                atm_m = {}
                nd_act = {}
                nd_pe = {}
                for c in range(16):
                    atv = pbig0[:, (c % 2) * 512:(c % 2) * 512 + 128]
                    if c >= 2:
                        t_.wait_ge(semDVE, atm_m[c - 2])
                    m_at = mpe(t_.matmul(atv, B["kT"][:, c * 128:(c + 1) * 128],
                                         B["qT"][:, c * 128:(c + 1) * 128],
                                         start=True, stop=True))
                    v_.wait_ge(semPE, m_at)
                    if c >= 2:
                        v_.wait_ge(semPE, nd_pe[c - 2])
                    atm_m[c] = mdve(v_.tensor_mul(atm2[:, (c % 2) * 128:(c % 2) * 128 + 128],
                                                  atv, triu))
                    if aunits:
                        aunits.pop(0)()
                    t_.wait_ge(semDVE, atm_m[c])
                    t_.wait_ge(semDVE, scan_steps[h][c])
                    if c >= 2:
                        t_.wait_ge(semACT, nd_act[c - 2])
                    ndv = pbig1[:, (c % 2) * 512:(c % 2) * 512 + 65]
                    t_.matmul(ndv, atm2[:, (c % 2) * 128:(c % 2) * 128 + 128],
                              B["va"][:, c * 65:(c + 1) * 65], start=True, stop=False)
                    nd_pe[c] = mpe(t_.matmul(ndv, B["qT"][:, c * 128:(c + 1) * 128],
                                             B["so"][:, c * 65:(c + 1) * 65],
                                             start=False, stop=True))
                    a_.wait_ge(semPE, nd_pe[c])
                    a_.copy(attnum[:, c * D + h * DH: c * D + (h + 1) * DH], ndv[:, 0:64])
                    nd_act[c] = mact(a_.copy(den_all[:, h * 16 + c: h * 16 + c + 1],
                                             ndv[:, 64:65]))
                return nd_act[15]

            for u in make_A_units(0, 0):
                u()
            stage_U(0, 0)
            nd_read_act = 0
            aunits = make_A_units(1, vaug_m[0]) if H > 1 else []
            for h in range(H):
                nd_read_act = stage_D(h, aunits)
                if h + 1 < H:
                    for u in aunits:
                        u()
                    aunits = []
                    stage_U(h + 1, nd_read_act)
                    if h + 2 < H:
                        aunits = make_A_units(h + 2, vaug_m[h + 1])
            scan_done = scan_m[H - 1]
            nd_read = cDVE.v

            micro(19)
            gp.wait_ge(semDVE, scan_done)
            c_dma = wgp(gp.dma_start(out=contrib_dram.ap(), in_=contrib))
            gp.wait_ge(semGP, c_dma)
            gp.collective_compute(
                "AllGather", ALU.bypass,
                replica_groups=[list(range(NCORES))],
                ins=[contrib_dram.ap().opt()],
                outs=[gathered_dram.ap().opt()],
            ).then_inc(semCC, 1)
            cCC.inc()
            gp.wait_ge(semCC, cCC.v)
            g_reads = {}
            for r in range(NCORES):
                if r >= 2:
                    gp.wait_ge(semDVE, g_reads[r - 2])
                g_dma = wgp(gp.dma_start(out=gbuf[r % 2],
                                         in_=gathered_dram.ap()[r * 64:(r + 1) * 64, :]))
                v_.wait_ge(semGP, g_dma)
                if r == 0:
                    g_reads[0] = mdve(v_.tensor_scalar_mul(S0, gbuf[0], maskcol[:, 0:1]))
                else:
                    v_.wait_ge(semDVE, g_reads[r - 1])
                    g_reads[r] = mdve(v_.scalar_tensor_tensor(S0, gbuf[r % 2],
                                                              maskcol[:, r:r + 1], S0,
                                                              ALU.mult, ALU.add))
            v_.wait_ge(semDVE, g_reads[NCORES - 1])
            m_s0 = mdve(v_.tensor_copy(S0bf, S0))

            t_.wait_ge(semACT, nd_read_act)
            v_.wait_ge(semACT, nd_read_act)
            CQ = [qT_h, qT2]
            m_q = {0: qkv_comp(0, 0, qT_h, True, guard=nd_read)}
            m_q[1] = qkv_comp(1, 0, qT2, True, guard=m_q[0])
            prefilled = 2
            t_.wait_ge(semDVE, m_s0)
            num_read = 0
            m_d1h = {}
            for h in range(H):
                qb = CQ[h % 2]
                t_.wait_ge(semDVE, m_q[h])
                if h > 0:
                    t_.wait_ge(semDVE, m_d1h[h - 1])
                for c in range(16):
                    mm = t_.matmul(pbig1[:, c:c + 1], qb[:, c * 128:(c + 1) * 128],
                                   S0bf[:, h * 65 + 64: h * 65 + 65], start=True, stop=True)
                m_dc = mpe(mm)
                if num_read:
                    t_.wait_ge(semACT, num_read)
                for c in range(16):
                    mm = t_.matmul(pbig0[:, c * 64:(c + 1) * 64],
                                   qb[:, c * 128:(c + 1) * 128],
                                   S0bf[:, h * 65: h * 65 + 64], start=True, stop=True)
                m_nc = mpe(mm)
                if h + prefilled < H:
                    m_q[h + prefilled] = qkv_comp(h + prefilled, 0, CQ[(h + prefilled) % 2],
                                                  True, guard=m_q[h + prefilled - 1])
                v_.wait_ge(semPE, m_dc)
                if num_read:
                    v_.wait_ge(semACT, num_read)
                m_d1 = mdve(v_.tensor_add(rec[:, 0:16], den_all[:, h * 16:(h + 1) * 16],
                                          pbig1[:, 0:16]))
                m_d1h[h] = m_d1
                v_.wait_ge(semDVE, m_d1)
                m_d2 = mdve(v_.tensor_scalar_max(rec[:, 16:32], rec[:, 0:16], HEPS))
                v_.wait_ge(semDVE, m_d2)
                m_rec = mdve(v_.reciprocal(rec[:, 0:16], rec[:, 16:32]))
                v_.wait_ge(semDVE, m_rec)
                v_.wait_ge(semPE, m_nc)
                for half in range(2):
                    c0 = half * 8
                    if num_read:
                        v_.wait_ge(semACT, num_read)
                    m_nt = mdve(v_.tensor_add(
                        numtot.rearrange("p (c f) -> p c f", f=64),
                        pbig0[:, half * 512:(half + 1) * 512].rearrange("p (c f) -> p c f", f=64),
                        attnum.rearrange("p (c f) -> p c f", f=D)[:, c0:c0 + 8, h * DH:(h + 1) * DH]))
                    a_.wait_ge(semDVE, m_nt)
                    a_.wait_ge(semDVE, m_rec)
                    for cc in range(8):
                        c = c0 + cc
                        mm2 = a_.activation(
                            attnum[:, c * D + h * DH: c * D + (h + 1) * DH],
                            numtot[:, cc * 64:(cc + 1) * 64], AF.Copy,
                            scale=rec[:, c:c + 1])
                    num_read_act = mact(mm2)
                    num_read = num_read_act
            corr_done_act = num_read
            corr_done = cDVE.v
            qwt_free = cPE.v

            micro(20)
            t_.wait_ge(semDVE, corr_done)
            t_.wait_ge(semACT, corr_done_act)
            sy.wait_ge(semPE, qwt_free)
            sy.wait_ge(semDVE, corr_done)
            xs_dmas = {}
            xs_dmas[0] = wsp(sy.dma_start(out=xstage[0], in_=xspill.ap()[:, 0:D]))
            xs_dmas[1] = wsp(sy.dma_start(out=xstage[1], in_=xspill.ap()[:, D:2 * D]))
            attT_m = {}
            xup = {}
            attT_pe = {}
            for t in range(NT):
                pb = pbf[t % 2]
                pbg = pbig[t % 2]
                aT = attT2b[t % 2]
                if t >= 2:
                    t_.wait_ge(semDVE, attT_m[t - 2])
                for k in range(NK):
                    mm = t_.transpose(pb[:, k * 128:(k + 1) * 128],
                                      attnum[:, t * D + k * 128: t * D + (k + 1) * 128], identb)
                m_tt = mpe(mm)
                v_.wait_ge(semPE, m_tt)
                if t >= 2:
                    v_.wait_ge(semPE, attT_pe[t - 2])
                m_aT = mdve(v_.tensor_copy(aT, pb[:, :]))
                attT_m[t] = m_aT
                t_.wait_ge(semDVE, m_aT)
                if t >= 2:
                    t_.wait_ge(semDVE, xup[t - 2])
                for nb in range(2):
                    for k in range(NK):
                        t_.matmul(pbg[:, nb * 512:(nb + 1) * 512],
                                  aT[:, k * 128:(k + 1) * 128],
                                  owt[:, k * D + nb * 512: k * D + (nb + 1) * 512],
                                  start=(k == 0), stop=False)
                    mm = t_.matmul(pbg[:, nb * 512:(nb + 1) * 512], onesbf[0:1, 0:128],
                                   outbbf[0:1, nb * 512:(nb + 1) * 512], start=False, stop=True)
                m_op = mpe(mm)
                attT_pe[t] = m_op
                v_.wait_ge(semPE, m_op)
                v_.wait_ge(semSP, xs_dmas[t])
                v_.tensor_add(xsb[:, t * D: t * D + 512], xstage[t % 2][:, 0:512],
                              pbg[:, 0:512])
                xup[t] = mdve(v_.tensor_add(xsb[:, t * D + 512: t * D + 1024],
                                            xstage[t % 2][:, 512:1024], pbg[:, 512:1024]))
                if t + 2 < NT:
                    sy.wait_ge(semDVE, xup[t])
                    xs_dmas[t + 2] = wsp(sy.dma_start(
                        out=xstage[t % 2], in_=xspill.ap()[:, (t + 2) * D:(t + 3) * D]))

            if STOP_AFTER >= 3:
                ffn_emit()

          except StopEmission:
            pass
          if DEBUG_DUMPS:
            wall(gp)
            cGP.inc(16); gp.dma_start(out=dbg_lnT, in_=lnT).then_inc(semGP, 16)
            cGP.inc(16); gp.dma_start(out=dbg_hT, in_=hT).then_inc(semGP, 16)
            cGP.inc(16); gp.dma_start(out=dbg_gates, in_=gates_all).then_inc(semGP, 16)
            cGP.inc(16); gp.dma_start(out=dbg_misc[0:64, 0:48], in_=qkvb64).then_inc(semGP, 16)
            cGP.inc(16); gp.dma_start(out=dbg_misc[:, 48:112], in_=probs).then_inc(semGP, 16)
            cGP.inc(16); gp.dma_start(out=dbg_misc[:, 112:120], in_=m8sb).then_inc(semGP, 16)
            cGP.inc(16); gp.dma_start(out=dbg_misc[:, 120:136], in_=stats[:, 0:16]).then_inc(semGP, 16)
            if STOP_AFTER >= 2:
                cGP.inc(16); gp.dma_start(out=dbg_misc[:, 136:168], in_=rec).then_inc(semGP, 16)
                cGP.inc(16); gp.dma_start(out=dbg_misc[:, 168:232], in_=numtot[:, 448:512]).then_inc(semGP, 16)
                cGP.inc(16); gp.dma_start(out=dbg_att, in_=attnum).then_inc(semGP, 16)
                cGP.inc(16); gp.dma_start(out=dbg_den, in_=den_all).then_inc(semGP, 16)
                cGP.inc(16); gp.dma_start(out=dbg_st[:, 0:H * 65], in_=contrib).then_inc(semGP, 16)
                cGP.inc(16); gp.dma_start(out=dbg_st[:, H * 65:2 * H * 65], in_=S0).then_inc(semGP, 16)
                cGP.inc(16); gp.dma_start(out=dbg_st[:, 2 * H * 65:3 * H * 65], in_=S0bf).then_inc(semGP, 16)
                cGP.inc(16); gp.dma_start(out=dbg_st[:, 3 * H * 65:3 * H * 65 + 1040], in_=sown).then_inc(semGP, 16)
            gp.wait_ge(semGP, cGP.v)
          wall(sy)
          m_out = wsp(sy.dma_start(out=ydram.rearrange("(t p) d -> p t d", p=128),
                                   in_=xsb.rearrange("p (t d) -> p t d", d=D)))
          sy.wait_ge(semSP, m_out)

    return nc


def kernel(L0_attn_x, L0_attn_qkv_w, L0_attn_qkv_b, L0_attn_out_w, L0_attn_out_b,
           L0_ffn_wg1, L0_ffn_bg1, L0_ffn_wg2, L0_ffn_bg2, L0_ffn_We, L0_ffn_be):
    x = np.ascontiguousarray(np.asarray(L0_attn_x, dtype=np.float32))
    qkv_w = np.ascontiguousarray(np.asarray(L0_attn_qkv_w, dtype=np.float32))
    qkv_b = np.ascontiguousarray(np.asarray(L0_attn_qkv_b, dtype=np.float32)).reshape(1, 3 * D)
    out_w = np.ascontiguousarray(np.asarray(L0_attn_out_w, dtype=np.float32))
    out_b = np.ascontiguousarray(np.asarray(L0_attn_out_b, dtype=np.float32)).reshape(1, D)
    wg1 = np.ascontiguousarray(np.asarray(L0_ffn_wg1, dtype=np.float32))
    bg1 = np.ascontiguousarray(np.asarray(L0_ffn_bg1, dtype=np.float32)).reshape(1, 256)
    wg2 = np.ascontiguousarray(np.asarray(L0_ffn_wg2, dtype=np.float32))
    bg2 = np.ascontiguousarray(np.asarray(L0_ffn_bg2, dtype=np.float32)).reshape(1, 40)
    We = np.ascontiguousarray(np.asarray(L0_ffn_We, dtype=np.float32)).reshape(NE * D, D)
    be = np.ascontiguousarray(np.asarray(L0_ffn_be, dtype=np.float32))

    qkv_wT = np.ascontiguousarray(qkv_w.T)
    out_wT = np.ascontiguousarray(out_w.T)
    identf = np.eye(128, dtype=np.float32)
    identb = np.eye(128, dtype=ml_dtypes.bfloat16)
    triu = np.triu(np.ones((128, 128), dtype=np.float32)).astype(ml_dtypes.bfloat16)

    nc = build_program()
    in_maps = []
    for c in range(NCORES):
        b, half = c // 2, c % 2
        msk = np.zeros((1, 8), dtype=np.float32)
        if half == 1:
            msk[0, c - 1] = 1.0
        in_maps.append({
            "xin": np.ascontiguousarray(x[b, half * T:(half + 1) * T]),
            "qkvw": qkv_w, "qkvb": qkv_b, "outw": out_w, "outb": out_b,
            "qkvwT": qkv_wT, "outwT": out_wT,
            "wg1": wg1, "bg1": bg1, "wg2": wg2, "bg2": bg2,
            "wed": We, "bed": be,
            "identf": identf, "identb": identb, "triu": triu,
            "maskrow": msk,
        })
    res = run_bass_kernel_spmd(nc, in_maps, core_ids=list(range(NCORES)))
    if DEBUG_DUMPS:
        DBG.clear()
        DBG.update({k: [res.results[c][k] for c in range(NCORES)]
                    for k in res.results[0] if k.startswith("dbg_")})
    out = np.empty((4, 4096, D), dtype=np.float32)
    for c in range(NCORES):
        b, half = c // 2, c % 2
        out[b, half * T:(half + 1) * T] = res.results[c]["y"]
    return out


# version 11
# speedup vs baseline: 1.0610x; 1.0148x over previous
import numpy as np
import ml_dtypes
from contextlib import ExitStack

import concourse.bass as bass
import concourse.mybir as mybir
from concourse.bass_utils import run_bass_kernel_spmd

F32 = mybir.dt.float32
BF16 = mybir.dt.bfloat16
AF = mybir.ActivationFunctionType
ALU = mybir.AluOpType
AX = mybir.AxisListType

NCORES = 8
T = 2048
NT = 16
D = 1024
NK = 8
H = 16
DH = 64
NE = 20
HEPS = 1e-6
LNEPS = 1e-5
STOP_AFTER = 3
STOP_MICRO = None
DEBUG_DUMPS = False
DBG = {}


class StopEmission(Exception):
    pass


class Cnt:
    def __init__(self):
        self.v = 0

    def inc(self, n=1):
        self.v += n
        return self.v


def build_program():
    nc = bass.Bass("TRN2", target_bir_lowering=False, debug=False)

    xin = nc.dram_tensor("xin", [T, D], F32, kind="ExternalInput").ap()
    qkvw = nc.dram_tensor("qkvw", [3 * D, D], F32, kind="ExternalInput").ap()
    qkvb = nc.dram_tensor("qkvb", [1, 3 * D], F32, kind="ExternalInput").ap()
    outw = nc.dram_tensor("outw", [D, D], F32, kind="ExternalInput").ap()
    qkvwT = nc.dram_tensor("qkvwT", [D, 3 * D], F32, kind="ExternalInput").ap()
    outwT = nc.dram_tensor("outwT", [D, D], F32, kind="ExternalInput").ap()
    outb = nc.dram_tensor("outb", [1, D], F32, kind="ExternalInput").ap()
    wg1d = nc.dram_tensor("wg1", [D, 256], F32, kind="ExternalInput").ap()
    bg1d = nc.dram_tensor("bg1", [1, 256], F32, kind="ExternalInput").ap()
    wg2d = nc.dram_tensor("wg2", [256, 40], F32, kind="ExternalInput").ap()
    bg2d = nc.dram_tensor("bg2", [1, 40], F32, kind="ExternalInput").ap()
    wed = nc.dram_tensor("wed", [NE * D, D], F32, kind="ExternalInput").ap()
    bed = nc.dram_tensor("bed", [NE, D], F32, kind="ExternalInput").ap()
    identf_d = nc.dram_tensor("identf", [128, 128], F32, kind="ExternalInput").ap()
    identb_d = nc.dram_tensor("identb", [128, 128], BF16, kind="ExternalInput").ap()
    triu_d = nc.dram_tensor("triu", [128, 128], BF16, kind="ExternalInput").ap()
    maskrow_d = nc.dram_tensor("maskrow", [1, 8], F32, kind="ExternalInput").ap()
    ydram = nc.dram_tensor("y", [T, D], F32, kind="ExternalOutput").ap()
    if DEBUG_DUMPS:
        dbg_lnT = nc.dram_tensor("dbg_lnT", [128, NK * T], F32, kind="ExternalOutput").ap()
        dbg_hT = nc.dram_tensor("dbg_hT", [128, 2 * T], F32, kind="ExternalOutput").ap()
        dbg_gates = nc.dram_tensor("dbg_gates", [128, NT * NE], F32, kind="ExternalOutput").ap()
        dbg_misc = nc.dram_tensor("dbg_misc", [128, 256], F32, kind="ExternalOutput").ap()
        dbg_att = nc.dram_tensor("dbg_att", [128, NT * D], F32, kind="ExternalOutput").ap()
        dbg_den = nc.dram_tensor("dbg_den", [128, 256], F32, kind="ExternalOutput").ap()
        dbg_st = nc.dram_tensor("dbg_st", [64, 4 * H * 65], F32, kind="ExternalOutput").ap()

    qkvT_hbm = nc.dram_tensor("qkvT_hbm", [NK, 128, 3 * D], BF16)
    xspill = nc.dram_tensor("xspill", [128, NT * D], F32)
    contrib_dram = nc.dram_tensor("contrib_dram", [64, H * 65], F32)
    gathered_dram = nc.dram_tensor("gathered_dram", [NCORES * 64, H * 65], F32)

    B0 = (nc.sbuf_base + 31) // 32 * 32
    A = B0
    B = A + 66560
    C = B + 32768
    Dr = C + 49152
    E = Dr + 16384

    at = lambda name, shape, dt, offset: nc.alloc_sbuf_tensor_at(name, shape, dt, offset=offset).ap()

    xsb = at("xsb", [128, NT * D], F32, A)
    attnum = at("attnum", [128, NT * D], BF16, A + 32768)
    U_sb = at("U_sb", [128, 2048], F32, A)
    contrib = at("contrib", [128, 8 * 65], F32, A + 8192)
    den_all = at("den_all", [128, 256], F32, A + 12416)
    rec = at("rec", [128, 32], F32, A + 13440)
    gbuf = [at(f"gbuf{i}", [64, H * 65], F32, A + i * 4192) for i in range(2)]
    qT2 = at("qT2", [128, T], BF16, A + 13568)
    kT2 = at("kT2", [128, T], BF16, A + 17664)
    vT2 = at("vT2", [128, T], BF16, A + 21760)
    ktok2 = at("ktok2", [128, 16 * 64], BF16, A + 25856)
    v_aug2 = at("v_aug2", [128, 16 * 65], BF16, A + 27904)
    sown2 = at("sown2", [128, 16 * 65], BF16, A + 30016)

    lnT = at("lnT", [128, NK * T], BF16, B)
    asm = [at(f"asm{i}", [128, 3 * D], BF16, B + i * 6144) for i in range(2)]
    qkvb_row = at("qkvb_row", [1, 3 * D], F32, B + 12288)
    bg1row = at("bg1row", [1, 256], F32, B + 24576)

    stage = at("stage", [128, 24 * D], BF16, C)
    web = [at(f"web{i}", [128, NK * D], BF16, C + i * 16384) for i in range(2)]
    hT = at("hT", [128, 2 * T], BF16, C + 32768)
    qwt = at("qwt", [128, NK * 3 * D], BF16, C)
    xstage = [at(f"xstage{i}", [128, D], F32, C + i * 4096) for i in range(2)]
    attT2b = [at(f"attT{i}", [128, D], BF16, C + 8192 + i * 2048) for i in range(2)]

    owt = at("owt", [128, NK * D], BF16, Dr)

    off = [E]

    def tal(name, shape, dt):
        per = int(np.prod(shape[1:])) * mybir.dt.size(dt)
        o = (off[0] + 31) // 32 * 32
        off[0] = o + per
        return at(name, shape, dt, o)

    identf = tal("identfs", [128, 128], F32)
    identb = tal("identbs", [128, 128], BF16)
    triu = tal("trius", [128, 128], BF16)
    wg1b = tal("wg1b", [128, NK * 256], BF16)
    wg2b = tal("wg2b", [128, 2 * 40], BF16)
    beb = tal("beb", [NE, D], BF16)
    qkvb64 = tal("qkvb64", [64, 48], F32)
    outbbf = tal("outbbf", [1, D], BF16)
    bg2row = tal("bg2row", [1, 40], BF16)
    onesbf = tal("onesbf", [1, 512], BF16)
    onesf = tal("onesf", [1, 64], F32)
    maskrow = tal("maskrow_sb", [1, 8], F32)
    maskcol = tal("maskcol", [64, 8], F32)
    gates_all = tal("gates_all", [128, NT * NE], F32)
    gatesT = tal("gatesT", [NE, NT * 128], BF16)
    stats = tal("stats", [128, 64], F32)
    lnstage = tal("lnstage", [128, D], BF16)
    probs = tal("probs", [128, 64], F32)
    m8sb = tal("m8sb", [128, 8], F32)
    bg1cols = tal("bg1cols", [128, 2], F32)
    qT_h = tal("qT_h", [128, T], BF16)
    kT_h = tal("kT_h", [128, T], BF16)
    vT_h = tal("vT_h", [128, T], BF16)
    ktok = tal("ktok", [128, 16 * 64], BF16)
    v_aug = tal("v_aug", [128, 16 * 65], BF16)
    sown = tal("sown", [128, 16 * 65], BF16)
    S0 = tal("S0", [64, H * 65], F32)
    S0bf = tal("S0bf", [64, H * 65], BF16)
    sacc = tal("sacc", [128, 72], F32)
    qkvb128 = tal("qkvb128", [128, 24], F32)
    numtot = tal("numtot", [128, 512], F32)
    epscol = tal("epscol", [128, 1], F32)
    atm2 = tal("atm2", [128, 256], BF16)
    assert off[0] <= 229340, f"tail overflow {off[0]}"

    with ExitStack() as ctx:
        pbig0 = ctx.enter_context(nc.psum_tensor("pbig0", [128, 1024], F32))
        pbig1 = ctx.enter_context(nc.psum_tensor("pbig1", [128, 1024], F32))
        psm0 = ctx.enter_context(nc.psum_tensor("psm0", [128, 512], F32))
        psm1 = ctx.enter_context(nc.psum_tensor("psm1", [128, 512], F32))
        pbf0 = ctx.enter_context(nc.psum_tensor("pbf0", [128, 1024], BF16))
        pbf1 = ctx.enter_context(nc.psum_tensor("pbf1", [128, 1024], BF16))
        pbig = [pbig0, pbig1]
        psm = [psm0, psm1]
        pbf = [pbf0, pbf1]

        semPE = ctx.enter_context(nc.semaphore("semPE"))
        semDVE = ctx.enter_context(nc.semaphore("semDVE"))
        semACT = ctx.enter_context(nc.semaphore("semACT"))
        semIN = ctx.enter_context(nc.semaphore("semIN"))
        semW = ctx.enter_context(nc.semaphore("semW"))
        semSP = ctx.enter_context(nc.semaphore("semSP"))
        semGP = ctx.enter_context(nc.semaphore("semGP"))
        semCC = ctx.enter_context(nc.semaphore("semCC"))
        block = ctx.enter_context(nc.Block())

        cPE, cDVE, cACT = Cnt(), Cnt(), Cnt()
        cIN, cW, cSP, cGP, cCC = Cnt(), Cnt(), Cnt(), Cnt(), Cnt()

        t_, v_, a_, sy, gp = nc.tensor, nc.vector, nc.scalar, nc.sync, nc.gpsimd

        def mpe(i):
            i.then_inc(semPE, 1)
            return cPE.inc()

        def mdve(i):
            i.then_inc(semDVE, 1)
            return cDVE.inc()

        def mact(i):
            i.then_inc(semACT, 1)
            return cACT.inc()

        def win(i):
            i.then_inc(semIN, 16)
            return cIN.inc(16)

        def ww(i):
            i.then_inc(semW, 16)
            return cW.inc(16)

        def wsp(i):
            i.then_inc(semSP, 16)
            return cSP.inc(16)

        def wgp(i):
            i.then_inc(semGP, 16)
            return cGP.inc(16)

        def wall(eng):
            for sem, c in ((semPE, cPE), (semDVE, cDVE), (semACT, cACT), (semIN, cIN),
                           (semW, cW), (semSP, cSP), (semGP, cGP), (semCC, cCC)):
                if c.v:
                    eng.wait_ge(sem, c.v)

        @block.sync
        def _(sync):
          try:
            def micro(n):
                if STOP_MICRO is not None and STOP_MICRO == n:
                    raise StopEmission()

            win(sy.dma_start(out=xsb.rearrange("p (t d) -> p t d", d=D),
                             in_=xin.rearrange("(t p) d -> p t d", p=128)))
            micro(1)
            win(sy.dma_start(out=identf, in_=identf_d))
            win(sy.dma_start(out=identb, in_=identb_d))
            win(sy.dma_start(out=triu, in_=triu_d))
            win(sy.dma_start(out=maskrow, in_=maskrow_d))
            win(sy.dma_start(out=qkvb_row, in_=qkvb))
            win(sy.dma_start(out=bg1row, in_=bg1d))
            in_all = cIN.v

            ww(gp.dma_start(out=wg1b.rearrange("p (k f) -> p k f", k=NK),
                            in_=wg1d.rearrange("(k p) f -> p k f", p=128)))
            ww(gp.dma_start(out=wg2b.rearrange("p (k f) -> p k f", k=2),
                            in_=wg2d.rearrange("(k p) f -> p k f", p=128)))
            ww(gp.dma_start(out=beb, in_=bed))
            ww(gp.dma_start(out=outbbf, in_=outb))
            ww(gp.dma_start(out=bg2row, in_=bg2d))
            ww(gp.dma_start(out=owt.rearrange("p (k o) -> p k o", k=NK),
                            in_=outwT.rearrange("(k p) o -> p k o", p=128)))

            v_.memset(onesbf, 1.0)
            v_.memset(onesf, 1.0)
            m_ones = mdve(v_.memset(v_aug.rearrange("p (c s) -> p c s", s=65)[:, :, 64:65], 1.0))
            micro(3)

            t_.wait_ge(semIN, in_all)
            t_.wait_ge(semDVE, m_ones)
            for c in range(3):
                for h in range(H):
                    j = c * 16 + h
                    t_.matmul(psm0[0:64, j:j + 1],
                              qkvb_row[0:1, c * D + h * DH: c * D + (h + 1) * DH],
                              onesf[0:1, 0:1], start=True, stop=True)
            t_.matmul(psm1[0:128, 0:1], bg1row[0:1, 0:128], onesf[0:1, 0:1],
                      start=True, stop=True)
            t_.matmul(psm1[0:128, 1:2], bg1row[0:1, 128:256], onesf[0:1, 0:1],
                      start=True, stop=True)
            for c in range(3):
                for p in range(8):
                    j = c * 8 + p
                    t_.matmul(psm1[0:128, 16 + j:17 + j],
                              qkvb_row[0:1, c * D + p * 128: c * D + (p + 1) * 128],
                              onesf[0:1, 0:1], start=True, stop=True)
            m_cols = mpe(t_.matmul(psm1[0:64, 8:16], onesf[0:1, 0:64], maskrow[0:1, 0:8],
                                   start=True, stop=True))
            v_.wait_ge(semPE, m_cols)
            m_cp1 = mdve(v_.tensor_copy(qkvb64, psm0[0:64, 0:48]))
            v_.tensor_copy(bg1cols, psm1[0:128, 0:2])
            v_.tensor_copy(maskcol, psm1[0:64, 8:16])
            v_.tensor_copy(qkvb128, psm1[0:128, 16:40])
            v_.wait_ge(semDVE, m_cp1)
            v_.tensor_scalar(qkvb64[:, 0:32], qkvb64[:, 0:32], HEPS, None, ALU.add)
            m_bias = mdve(v_.tensor_scalar(qkvb128[:, 0:16], qkvb128[:, 0:16], HEPS, None, ALU.add))
            micro(6)

            def ln_emit():
                m_sd = {}
                for t in range(NT):
                    xt = xsb[:, t * D:(t + 1) * D]
                    sc4 = stats[:, t * 4: t * 4 + 4]
                    bn = probs[:, (t % 2) * 12:(t % 2) * 12 + 12]
                    v_.bn_stats(bn[:, 0:6], xt[:, 0:512])
                    m_b1 = mdve(v_.bn_stats(bn[:, 6:12], xt[:, 512:1024]))
                    v_.wait_ge(semDVE, m_b1)
                    m_b2 = mdve(v_.bn_aggr(bn[:, 0:2] if False else probs[:, 24:26], bn[:, 0:12]))
                    v_.wait_ge(semDVE, m_b2)
                    v_.tensor_scalar_mul(sc4[:, 0:1], probs[:, 24:25], -1.0)
                    m_var = mdve(v_.tensor_scalar_add(sc4[:, 1:2], probs[:, 25:26], LNEPS))
                    a_.wait_ge(semDVE, m_var)
                    m_sd[t] = mact(a_.activation(sc4[:, 2:3], sc4[:, 1:2], AF.Sqrt))
                v_.wait_ge(semACT, m_sd[NT - 1])
                sdv = stats.rearrange("p (t f) -> p t f", f=4)
                m_rc = mdve(v_.reciprocal(sdv[:, :, 3:4], sdv[:, :, 2:3]))
                v_.wait_ge(semDVE, m_rc)
                pbf_free = [0, 0]
                tp_m = {}
                lnT_done = 0
                for t in range(NT):
                    xt = xsb[:, t * D:(t + 1) * D]
                    sc4 = stats[:, t * 4: t * 4 + 4]
                    if t >= 1:
                        v_.wait_ge(semPE, tp_m[t - 1])
                    m_ln = mdve(v_.tensor_scalar(lnstage, xt, sc4[:, 0:1], sc4[:, 3:4],
                                                 ALU.add, ALU.mult))
                    pb = pbf[t % 2]
                    t_.wait_ge(semDVE, m_ln)
                    if pbf_free[t % 2]:
                        t_.wait_ge(semDVE, pbf_free[t % 2])
                    for k in range(NK):
                        mm = t_.transpose(pb[:, k * 128:(k + 1) * 128],
                                          lnstage[:, k * 128:(k + 1) * 128], identb)
                    tp_m[t] = mpe(mm)
                    t_.matmul(pbig1[:, 0:512], identb, lnstage[:, 0:512], start=True, stop=True)
                    t_.matmul(pbig1[:, 0:512], identb, lnstage[:, 0:512], start=True, stop=True)
                    v_.wait_ge(semPE, tp_m[t])
                    lnT_done = mdve(v_.tensor_copy(
                        lnT.rearrange("p (k s) -> p k s", k=NK)[:, :, t * 128:(t + 1) * 128],
                        pb.rearrange("p (k s) -> p k s", k=NK)))
                    pbf_free[t % 2] = lnT_done
                return lnT_done

            def ffn_emit():
                wall(t_); wall(v_); wall(a_); wall(gp)

                web_loaded = {}
                for e in range(2):
                    web_loaded[e] = ww(gp.dma_start(
                        out=web[e].rearrange("p (k f) -> p k f", k=NK),
                        in_=wed[e * D:(e + 1) * D, :].rearrange("(k p) f -> p k f", p=128)))

                lnT_done = ln_emit()

                t_.wait_ge(semDVE, lnT_done)
                h_acts = []
                for fc in range(2):
                    for nb in range(4):
                        ps = psm[nb % 2]
                        if len(h_acts) >= 2:
                            t_.wait_ge(semACT, h_acts[-2])
                        for k in range(NK):
                            mm = t_.matmul(ps[:, :],
                                           wg1b[:, k * 256 + fc * 128: k * 256 + fc * 128 + 128],
                                           lnT[:, k * T + nb * 512: k * T + (nb + 1) * 512],
                                           start=(k == 0), stop=(k == NK - 1))
                        mk = mpe(mm)
                        a_.wait_ge(semPE, mk)
                        h_acts.append(mact(a_.activation(
                            hT[:, fc * T + nb * 512: fc * T + (nb + 1) * 512], ps[:, :],
                            AF.Gelu, bias=bg1cols[:, fc:fc + 1])))
                h_all = cACT.v

                t_.wait_ge(semACT, h_all)
                for t in range(NT):
                    for fc in range(2):
                        t_.matmul(psm0[:, t * 20:(t + 1) * 20],
                                  hT[:, fc * T + t * 128: fc * T + (t + 1) * 128],
                                  wg2b[:, fc * 40: fc * 40 + 20],
                                  start=(fc == 0), stop=False)
                    mm = t_.matmul(psm0[:, t * 20:(t + 1) * 20], onesbf[0:1, 0:128],
                                   bg2row[0:1, 0:20], start=False, stop=True)
                m_lg = mpe(mm)
                v_.wait_ge(semPE, m_lg)
                m_nm = mdve(v_.tensor_reduce(
                    stats[:, 0:16], psm0[:, 0:320].rearrange("p (t e) -> p t e", e=20),
                    AX.X, ALU.max, negate=True))
                a_.wait_ge(semDVE, m_nm)
                pall = numtot[:, 0:320]
                for t in range(NT):
                    m_exp = mact(a_.activation(pall[:, t * 20:(t + 1) * 20],
                                               psm0[:, t * 20:(t + 1) * 20], AF.Exp,
                                               bias=stats[:, t:t + 1]))
                exp_read = m_exp
                v_.wait_ge(semACT, exp_read)
                m_rs = mdve(v_.tensor_reduce(
                    stats[:, 16:32], pall.rearrange("p (t e) -> p t e", e=20),
                    AX.X, ALU.add))
                v_.wait_ge(semDVE, m_rs)
                m_rc = mdve(v_.reciprocal(stats[:, 32:48], stats[:, 16:32]))
                v_.wait_ge(semDVE, m_rc)
                gT_evac = 0
                m8_read = 0
                for t in range(NT):
                    m_ps = mdve(v_.tensor_scalar(gates_all[:, t * NE:(t + 1) * NE],
                                                 pall[:, t * 20:(t + 1) * 20],
                                                 stats[:, 32 + t:33 + t], 0.5,
                                                 ALU.mult, ALU.mult))
                    v_.wait_ge(semDVE, m_ps)
                    m_m8 = mdve(v_.max(m8sb[:, 0:8], gates_all[:, t * NE:(t + 1) * NE]))
                    v_.wait_ge(semDVE, m_m8)
                    m8_read = mdve(v_.scalar_tensor_tensor(
                        gates_all[:, t * NE:(t + 1) * NE],
                        gates_all[:, t * NE:(t + 1) * NE], m8sb[:, 1:2],
                        gates_all[:, t * NE:(t + 1) * NE], ALU.is_ge, ALU.mult))
                    t_.wait_ge(semDVE, m8_read)
                    if gT_evac:
                        t_.wait_ge(semDVE, gT_evac)
                    m_gt = mpe(t_.transpose(psm1[0:20, 0:128],
                                            gates_all[:, t * NE:(t + 1) * NE], identf))
                    v_.wait_ge(semPE, m_gt)
                    gT_evac = mdve(v_.tensor_copy(gatesT[:, t * 128:(t + 1) * 128],
                                                  psm1[0:20, 0:128]))

                bank_free = [0, 0]
                t_.wait_ge(semDVE, gT_evac)
                for t in range(NT):
                    pb = pbig[t % 2]
                    if bank_free[t % 2]:
                        t_.wait_ge(semDVE, bank_free[t % 2])
                    t_.matmul(pb[:, 0:512], gatesT[:, t * 128:(t + 1) * 128], beb[:, 0:512],
                              start=True, stop=True)
                    m_be = mpe(t_.matmul(pb[:, 512:1024], gatesT[:, t * 128:(t + 1) * 128],
                                         beb[:, 512:1024], start=True, stop=True))
                    v_.wait_ge(semPE, m_be)
                    v_.tensor_add(xsb[:, t * D: t * D + 512], xsb[:, t * D: t * D + 512],
                                  pb[:, 0:512])
                    bank_free[t % 2] = mdve(v_.tensor_add(
                        xsb[:, t * D + 512: t * D + 1024], xsb[:, t * D + 512: t * D + 1024],
                        pb[:, 512:1024]))

                micro(11)
                stt_m = {}
                t_.wait_ge(semACT, exp_read)
                ebanks = [(pbig0[:, 0:512], pbig0[:, 512:1024]),
                          (pbig1[:, 0:512], pbig1[:, 512:1024]),
                          (psm0[:, 0:512], psm1[:, 0:512])]
                for e in range(NE):
                    t_.wait_ge(semW, web_loaded[e])
                    mm_last = 0
                    for t in range(NT):
                        idx = e * NT + t
                        bh = ebanks[idx % 3]
                        if idx - 3 >= 0:
                            t_.wait_ge(semDVE, stt_m[idx - 3])
                        elif bank_free[idx % 2]:
                            t_.wait_ge(semDVE, bank_free[idx % 2])
                        for k in range(NK):
                            for nb in range(2):
                                mm = t_.matmul(bh[nb],
                                               lnT[:, k * T + t * 128: k * T + (t + 1) * 128],
                                               web[e % 2][:, k * D + nb * 512: k * D + (nb + 1) * 512],
                                               start=(k == 0), stop=(k == NK - 1))
                        mm_last = mpe(mm)
                        v_.wait_ge(semPE, mm_last)
                        stt_m[idx] = mdve(v_.scalar_tensor_tensor(
                            xsb[:, t * D: t * D + 512], bh[0],
                            gates_all[:, t * NE + e: t * NE + e + 1],
                            xsb[:, t * D: t * D + 512], ALU.mult, ALU.add))
                        stt_m[idx] = mdve(v_.scalar_tensor_tensor(
                            xsb[:, t * D + 512:(t + 1) * D], bh[1],
                            gates_all[:, t * NE + e: t * NE + e + 1],
                            xsb[:, t * D + 512:(t + 1) * D], ALU.mult, ALU.add))
                    if e + 2 < NE:
                        gp.wait_ge(semPE, mm_last)
                        web_loaded[e + 2] = ww(gp.dma_start(
                            out=web[e % 2].rearrange("p (k f) -> p k f", k=NK),
                            in_=wed[(e + 2) * D:(e + 3) * D, :].rearrange("(k p) f -> p k f", p=128)))

            if STOP_AFTER >= 1:
                ffn_emit()

            if STOP_AFTER < 2:
                raise StopEmission()
            wall(t_); wall(v_); wall(a_); wall(gp); wall(sy)

            lnT_done = ln_emit()

            sy.wait_ge(semDVE, lnT_done)
            sp_x = wsp(sy.dma_start(out=xspill.ap(), in_=xsb))
            for part in range(3):
                w_qwt = ww(gp.dma_start(
                    out=qwt.rearrange("p (k o) -> p k o", k=NK)[:, :, part * D:(part + 1) * D],
                    in_=qkvwT.rearrange("(k p) o -> p k o", p=128)[:, :, part * D:(part + 1) * D]))

            t_.wait_ge(semW, w_qwt)
            t_.wait_ge(semSP, sp_x)
            v_.wait_ge(semW, w_qwt)
            v_.wait_ge(semSP, sp_x)
            a_.wait_ge(semSP, sp_x)
            mdve(v_.memset(v_aug2.rearrange("p (c s) -> p c s", s=65)[:, :, 64:65], 1.0))

            def qkv_comp(h, comp, dst, with_relu, guard=0):
                biascol = qkvb64[:, comp * 16 + h: comp * 16 + h + 1]
                ms = []
                for nb in range(4):
                    ps = psm[nb % 2]
                    if len(ms) >= 2:
                        t_.wait_ge(semDVE, ms[-2])
                    elif guard:
                        t_.wait_ge(semDVE, guard)
                    for k in range(NK):
                        mm = t_.matmul(ps[0:64, :],
                                       qwt[:, k * 3072 + comp * D + h * DH: k * 3072 + comp * D + (h + 1) * DH],
                                       lnT[:, k * T + nb * 512: k * T + (nb + 1) * 512],
                                       start=(k == 0), stop=(k == NK - 1))
                    mk = mpe(mm)
                    v_.wait_ge(semPE, mk)
                    if with_relu:
                        ms.append(mdve(v_.tensor_scalar(dst[:, nb * 512:(nb + 1) * 512],
                                                        ps[0:64, :], biascol, HEPS,
                                                        ALU.add, ALU.max)))
                    else:
                        ms.append(mdve(v_.tensor_scalar(dst[:, nb * 512:(nb + 1) * 512],
                                                        ps[0:64, :], biascol, None, ALU.add)))
                return ms[-1]

            PBUF = [dict(qT=qT_h, kT=kT_h, vT=vT_h), dict(qT=qT2, kT=kT2, vT=vT2)]
            HBUF = [dict(kt=ktok, va=v_aug, so=sown), dict(kt=ktok2, va=v_aug2, so=sown2)]
            vaug_m = {}
            scan_steps = {}
            scan_m = {}
            pair_evac = {}

            def make_A_units(p, guard0):
                B = PBUF[p % 2]
                st = {"ms": []}
                dsts = [B["qT"], B["kT"], B["vT"]]
                units = []

                def qunit(comp, nb):
                    def go():
                        ps = psm[(comp * 4 + nb) % 2]
                        if len(st["ms"]) >= 2:
                            t_.wait_ge(semDVE, st["ms"][-2])
                        elif guard0:
                            t_.wait_ge(semDVE, guard0)
                        for k in range(NK):
                            mm = t_.matmul(
                                ps[:, :],
                                qwt[:, k * 3072 + comp * D + p * 128: k * 3072 + comp * D + (p + 1) * 128],
                                lnT[:, k * T + nb * 512: k * T + (nb + 1) * 512],
                                start=(k == 0), stop=(k == NK - 1))
                        mk = mpe(mm)
                        v_.wait_ge(semPE, mk)
                        biascol = qkvb128[:, comp * 8 + p: comp * 8 + p + 1]
                        if comp < 2:
                            st["ms"].append(mdve(v_.tensor_scalar(
                                dsts[comp][:, nb * 512:(nb + 1) * 512], ps[:, :],
                                biascol, HEPS, ALU.add, ALU.max)))
                        else:
                            st["ms"].append(mdve(v_.tensor_scalar(
                                dsts[comp][:, nb * 512:(nb + 1) * 512], ps[:, :],
                                biascol, None, ALU.add)))
                        pair_evac[p] = st["ms"][-1]
                    return go

                for comp in range(3):
                    for nb in range(4):
                        units.append(qunit(comp, nb))
                return units

            def stage_B(h):
                B = PBUF[(h // 2) % 2]
                HB = HBUF[h % 2]
                rb = (h % 2) * 64
                t_.wait_ge(semDVE, pair_evac[h // 2])
                if h >= 1:
                    t_.wait_ge(semDVE, vaug_m[h - 1])
                idd = identb[rb:rb + 64, rb:rb + 64]
                for c in range(16):
                    mmk = t_.transpose(pbf0[:, c * 64:(c + 1) * 64],
                                       B["kT"][rb:rb + 64, c * 128:(c + 1) * 128], idd)
                mpe(mmk)
                for c in range(16):
                    mmv = t_.transpose(pbf1[:, c * 64:(c + 1) * 64],
                                       B["vT"][rb:rb + 64, c * 128:(c + 1) * 128], idd)
                m_tv = mpe(mmv)
                v_.wait_ge(semPE, m_tv)
                v_.tensor_copy(HB["kt"], pbf0[:, :])
                vaug_m[h] = mdve(v_.tensor_copy(
                    HB["va"].rearrange("p (c s) -> p c s", s=65)[:, :, 0:64],
                    pbf1.rearrange("p (c s) -> p c s", s=64)))

            def stage_U_pair(p, pbig_free_act):
                if pbig_free_act:
                    t_.wait_ge(semACT, pbig_free_act)
                t_.wait_ge(semDVE, vaug_m[2 * p + 1])
                for hb in range(2):
                    HB = HBUF[(2 * p + hb) % 2]
                    rb = hb * 64
                    tp = (0, 64) if hb else None
                    for c in range(16):
                        pbU = pbig[c // 8]
                        mm = t_.matmul(pbU[rb:rb + 64, (c % 8) * 128:(c % 8) * 128 + 65],
                                       HB["kt"][:, c * 64:(c + 1) * 64],
                                       HB["va"][:, c * 65:(c + 1) * 65], start=True, stop=True,
                                       tile_position=tp)
                m_U = mpe(mm)
                v_.wait_ge(semPE, m_U)
                v_.tensor_copy(U_sb[:, 0:1024], pbig0[:, :])
                mdve(v_.tensor_copy(U_sb[:, 1024:2048], pbig1[:, :]))
                Us = U_sb.rearrange("p (c s) -> p c s", s=128)
                soA = HBUF[0]["so"]
                soB = HBUF[1]["so"]
                so_even = HBUF[(2 * p) % 2]["so"]
                so_odd = HBUF[(2 * p + 1) % 2]["so"]
                steps = {}
                v_.memset(so_even[0:64, 0:65], 0.0)
                steps[0] = mdve(v_.memset(so_odd[64:128, 0:65], 0.0))
                m_sc = mdve(v_.tensor_copy(sacc[:, 0:65], Us[:, 0, 0:65]))
                for c in range(1, 16):
                    v_.wait_ge(semDVE, m_sc)
                    v_.tensor_copy(so_even[0:64, c * 65:(c + 1) * 65], sacc[0:64, 0:65])
                    steps[c] = mdve(v_.tensor_copy(so_odd[64:128, c * 65:(c + 1) * 65],
                                                   sacc[64:128, 0:65]))
                    if c < 15:
                        m_sc = mdve(v_.tensor_add(sacc[:, 0:65], sacc[:, 0:65],
                                                  Us[:, c, 0:65]))
                m_ct = mdve(v_.tensor_add(contrib[:, p * 65:(p + 1) * 65],
                                          sacc[:, 0:65], Us[:, 15, 0:65]))
                scan_m[2 * p] = m_ct
                scan_m[2 * p + 1] = m_ct
                scan_steps[2 * p] = steps
                scan_steps[2 * p + 1] = steps

            def stage_D(h, aunits):
                B = PBUF[(h // 2) % 2]
                HB = HBUF[h % 2]
                rb = (h % 2) * 64
                t_.wait_ge(semDVE, scan_steps[h][1])
                atm_m = {}
                nd_act = {}
                nd_pe = {}
                for c in range(16):
                    atv = pbig0[:, (c % 2) * 512:(c % 2) * 512 + 128]
                    if c >= 2:
                        t_.wait_ge(semDVE, atm_m[c - 2])
                    m_at = mpe(t_.matmul(atv, B["kT"][rb:rb + 64, c * 128:(c + 1) * 128],
                                         B["qT"][rb:rb + 64, c * 128:(c + 1) * 128],
                                         start=True, stop=True))
                    v_.wait_ge(semPE, m_at)
                    if c >= 2:
                        v_.wait_ge(semPE, nd_pe[c - 2])
                    atm_m[c] = mdve(v_.tensor_mul(atm2[:, (c % 2) * 128:(c % 2) * 128 + 128],
                                                  atv, triu))
                    if aunits and (c % 2 == 1):
                        aunits.pop(0)()
                    t_.wait_ge(semDVE, atm_m[c])
                    t_.wait_ge(semDVE, scan_steps[h][c])
                    if c >= 2:
                        t_.wait_ge(semACT, nd_act[c - 2])
                    ndv = pbig1[:, (c % 2) * 512:(c % 2) * 512 + 65]
                    t_.matmul(ndv, atm2[:, (c % 2) * 128:(c % 2) * 128 + 128],
                              HB["va"][:, c * 65:(c + 1) * 65], start=True, stop=False)
                    nd_pe[c] = mpe(t_.matmul(ndv, B["qT"][rb:rb + 64, c * 128:(c + 1) * 128],
                                             HB["so"][rb:rb + 64, c * 65:(c + 1) * 65],
                                             start=False, stop=True))
                    a_.wait_ge(semPE, nd_pe[c])
                    a_.copy(attnum[:, c * D + h * DH: c * D + (h + 1) * DH], ndv[:, 0:64])
                    nd_act[c] = mact(a_.copy(den_all[:, h * 16 + c: h * 16 + c + 1],
                                             ndv[:, 64:65]))
                return nd_act[15]

            for u in make_A_units(0, 0):
                u()
            stage_B(0)
            stage_B(1)
            stage_U_pair(0, 0)
            aunits = make_A_units(1, pair_evac[0])
            nd_read_act = 0
            for h in range(H):
                nd_read_act = stage_D(h, aunits)
                if h + 1 >= H:
                    break
                if (h + 1) % 2 == 0:
                    p_next = (h + 1) // 2
                    for u in aunits:
                        u()
                    stage_B(2 * p_next)
                    stage_B(2 * p_next + 1)
                    stage_U_pair(p_next, nd_read_act)
                    aunits = (make_A_units(p_next + 1, pair_evac[p_next])
                              if p_next + 1 < H // 2 else [])
            scan_done = scan_m[H - 1]
            nd_read = cDVE.v

            micro(19)
            gp.wait_ge(semDVE, scan_done)
            dv = contrib_dram.ap().rearrange("p (hp two c) -> p hp two c", two=2, c=65)
            wgp(gp.dma_start(out=dv[:, :, 0, :],
                             in_=contrib[0:64, :].rearrange("p (hp c) -> p hp c", c=65)))
            c_dma = wgp(gp.dma_start(out=dv[:, :, 1, :],
                                     in_=contrib[64:128, :].rearrange("p (hp c) -> p hp c", c=65)))
            gp.wait_ge(semGP, c_dma)
            gp.collective_compute(
                "AllGather", ALU.bypass,
                replica_groups=[list(range(NCORES))],
                ins=[contrib_dram.ap().opt()],
                outs=[gathered_dram.ap().opt()],
            ).then_inc(semCC, 1)
            cCC.inc()
            gp.wait_ge(semCC, cCC.v)
            g_reads = {}
            for r in range(NCORES):
                if r >= 2:
                    gp.wait_ge(semDVE, g_reads[r - 2])
                g_dma = wgp(gp.dma_start(out=gbuf[r % 2],
                                         in_=gathered_dram.ap()[r * 64:(r + 1) * 64, :]))
                v_.wait_ge(semGP, g_dma)
                if r == 0:
                    g_reads[0] = mdve(v_.tensor_scalar_mul(S0, gbuf[0], maskcol[:, 0:1]))
                else:
                    v_.wait_ge(semDVE, g_reads[r - 1])
                    g_reads[r] = mdve(v_.scalar_tensor_tensor(S0, gbuf[r % 2],
                                                              maskcol[:, r:r + 1], S0,
                                                              ALU.mult, ALU.add))
            v_.wait_ge(semDVE, g_reads[NCORES - 1])
            m_s0 = mdve(v_.tensor_copy(S0bf, S0))

            t_.wait_ge(semACT, nd_read_act)
            v_.wait_ge(semACT, nd_read_act)
            CQ = [qT_h[0:64, :], qT2[0:64, :]]
            m_q = {0: qkv_comp(0, 0, CQ[0], True, guard=nd_read)}
            m_q[1] = qkv_comp(1, 0, CQ[1], True, guard=m_q[0])
            prefilled = 2
            t_.wait_ge(semDVE, m_s0)
            num_read = 0
            m_d1h = {}
            for h in range(H):
                qb = CQ[h % 2]
                t_.wait_ge(semDVE, m_q[h])
                if h > 0:
                    t_.wait_ge(semDVE, m_d1h[h - 1])
                for c in range(16):
                    mm = t_.matmul(pbig1[:, c:c + 1], qb[:, c * 128:(c + 1) * 128],
                                   S0bf[:, h * 65 + 64: h * 65 + 65], start=True, stop=True)
                m_dc = mpe(mm)
                if num_read:
                    t_.wait_ge(semACT, num_read)
                for c in range(16):
                    mm = t_.matmul(pbig0[:, c * 64:(c + 1) * 64],
                                   qb[:, c * 128:(c + 1) * 128],
                                   S0bf[:, h * 65: h * 65 + 64], start=True, stop=True)
                m_nc = mpe(mm)
                if h + prefilled < H:
                    m_q[h + prefilled] = qkv_comp(h + prefilled, 0, CQ[(h + prefilled) % 2],
                                                  True, guard=m_q[h + prefilled - 1])
                v_.wait_ge(semPE, m_dc)
                if num_read:
                    v_.wait_ge(semACT, num_read)
                m_d1 = mdve(v_.tensor_add(rec[:, 0:16], den_all[:, h * 16:(h + 1) * 16],
                                          pbig1[:, 0:16]))
                m_d1h[h] = m_d1
                v_.wait_ge(semDVE, m_d1)
                m_d2 = mdve(v_.tensor_scalar_max(rec[:, 16:32], rec[:, 0:16], HEPS))
                v_.wait_ge(semDVE, m_d2)
                m_rec = mdve(v_.reciprocal(rec[:, 0:16], rec[:, 16:32]))
                v_.wait_ge(semDVE, m_rec)
                v_.wait_ge(semPE, m_nc)
                for half in range(2):
                    c0 = half * 8
                    if num_read:
                        v_.wait_ge(semACT, num_read)
                    m_nt = mdve(v_.tensor_add(
                        numtot.rearrange("p (c f) -> p c f", f=64),
                        pbig0[:, half * 512:(half + 1) * 512].rearrange("p (c f) -> p c f", f=64),
                        attnum.rearrange("p (c f) -> p c f", f=D)[:, c0:c0 + 8, h * DH:(h + 1) * DH]))
                    a_.wait_ge(semDVE, m_nt)
                    a_.wait_ge(semDVE, m_rec)
                    for cc in range(8):
                        c = c0 + cc
                        mm2 = a_.activation(
                            attnum[:, c * D + h * DH: c * D + (h + 1) * DH],
                            numtot[:, cc * 64:(cc + 1) * 64], AF.Copy,
                            scale=rec[:, c:c + 1])
                    num_read_act = mact(mm2)
                    num_read = num_read_act
            corr_done_act = num_read
            corr_done = cDVE.v
            qwt_free = cPE.v

            micro(20)
            t_.wait_ge(semDVE, corr_done)
            t_.wait_ge(semACT, corr_done_act)
            sy.wait_ge(semPE, qwt_free)
            sy.wait_ge(semDVE, corr_done)
            xs_dmas = {}
            xs_dmas[0] = wsp(sy.dma_start(out=xstage[0], in_=xspill.ap()[:, 0:D]))
            xs_dmas[1] = wsp(sy.dma_start(out=xstage[1], in_=xspill.ap()[:, D:2 * D]))
            attT_m = {}
            xup = {}
            attT_pe = {}
            for t in range(NT):
                pb = pbf[t % 2]
                pbg = pbig[t % 2]
                aT = attT2b[t % 2]
                if t >= 2:
                    t_.wait_ge(semDVE, attT_m[t - 2])
                for k in range(NK):
                    mm = t_.transpose(pb[:, k * 128:(k + 1) * 128],
                                      attnum[:, t * D + k * 128: t * D + (k + 1) * 128], identb)
                m_tt = mpe(mm)
                v_.wait_ge(semPE, m_tt)
                if t >= 2:
                    v_.wait_ge(semPE, attT_pe[t - 2])
                m_aT = mdve(v_.tensor_copy(aT, pb[:, :]))
                attT_m[t] = m_aT
                t_.wait_ge(semDVE, m_aT)
                if t >= 2:
                    t_.wait_ge(semDVE, xup[t - 2])
                for nb in range(2):
                    for k in range(NK):
                        t_.matmul(pbg[:, nb * 512:(nb + 1) * 512],
                                  aT[:, k * 128:(k + 1) * 128],
                                  owt[:, k * D + nb * 512: k * D + (nb + 1) * 512],
                                  start=(k == 0), stop=False)
                    mm = t_.matmul(pbg[:, nb * 512:(nb + 1) * 512], onesbf[0:1, 0:128],
                                   outbbf[0:1, nb * 512:(nb + 1) * 512], start=False, stop=True)
                m_op = mpe(mm)
                attT_pe[t] = m_op
                v_.wait_ge(semPE, m_op)
                v_.wait_ge(semSP, xs_dmas[t])
                v_.tensor_add(xsb[:, t * D: t * D + 512], xstage[t % 2][:, 0:512],
                              pbg[:, 0:512])
                xup[t] = mdve(v_.tensor_add(xsb[:, t * D + 512: t * D + 1024],
                                            xstage[t % 2][:, 512:1024], pbg[:, 512:1024]))
                if t + 2 < NT:
                    sy.wait_ge(semDVE, xup[t])
                    xs_dmas[t + 2] = wsp(sy.dma_start(
                        out=xstage[t % 2], in_=xspill.ap()[:, (t + 2) * D:(t + 3) * D]))

            if STOP_AFTER >= 3:
                ffn_emit()

          except StopEmission:
            pass
          if DEBUG_DUMPS:
            wall(gp)
            cGP.inc(16); gp.dma_start(out=dbg_lnT, in_=lnT).then_inc(semGP, 16)
            cGP.inc(16); gp.dma_start(out=dbg_hT, in_=hT).then_inc(semGP, 16)
            cGP.inc(16); gp.dma_start(out=dbg_gates, in_=gates_all).then_inc(semGP, 16)
            cGP.inc(16); gp.dma_start(out=dbg_misc[0:64, 0:48], in_=qkvb64).then_inc(semGP, 16)
            cGP.inc(16); gp.dma_start(out=dbg_misc[:, 48:112], in_=probs).then_inc(semGP, 16)
            cGP.inc(16); gp.dma_start(out=dbg_misc[:, 112:120], in_=m8sb).then_inc(semGP, 16)
            cGP.inc(16); gp.dma_start(out=dbg_misc[:, 120:136], in_=stats[:, 0:16]).then_inc(semGP, 16)
            if STOP_AFTER >= 2:
                cGP.inc(16); gp.dma_start(out=dbg_misc[:, 136:168], in_=rec).then_inc(semGP, 16)
                cGP.inc(16); gp.dma_start(out=dbg_misc[:, 168:232], in_=numtot[:, 448:512]).then_inc(semGP, 16)
                cGP.inc(16); gp.dma_start(out=dbg_att, in_=attnum).then_inc(semGP, 16)
                cGP.inc(16); gp.dma_start(out=dbg_den, in_=den_all).then_inc(semGP, 16)
                cGP.inc(16); gp.dma_start(out=dbg_st[:, 0:H * 65], in_=contrib).then_inc(semGP, 16)
                cGP.inc(16); gp.dma_start(out=dbg_st[:, H * 65:2 * H * 65], in_=S0).then_inc(semGP, 16)
                cGP.inc(16); gp.dma_start(out=dbg_st[:, 2 * H * 65:3 * H * 65], in_=S0bf).then_inc(semGP, 16)
                cGP.inc(16); gp.dma_start(out=dbg_st[:, 3 * H * 65:3 * H * 65 + 1040], in_=sown).then_inc(semGP, 16)
            gp.wait_ge(semGP, cGP.v)
          wall(sy)
          m_out = wsp(sy.dma_start(out=ydram.rearrange("(t p) d -> p t d", p=128),
                                   in_=xsb.rearrange("p (t d) -> p t d", d=D)))
          sy.wait_ge(semSP, m_out)

    return nc


def kernel(L0_attn_x, L0_attn_qkv_w, L0_attn_qkv_b, L0_attn_out_w, L0_attn_out_b,
           L0_ffn_wg1, L0_ffn_bg1, L0_ffn_wg2, L0_ffn_bg2, L0_ffn_We, L0_ffn_be):
    x = np.ascontiguousarray(np.asarray(L0_attn_x, dtype=np.float32))
    qkv_w = np.ascontiguousarray(np.asarray(L0_attn_qkv_w, dtype=np.float32))
    qkv_b = np.ascontiguousarray(np.asarray(L0_attn_qkv_b, dtype=np.float32)).reshape(1, 3 * D)
    out_w = np.ascontiguousarray(np.asarray(L0_attn_out_w, dtype=np.float32))
    out_b = np.ascontiguousarray(np.asarray(L0_attn_out_b, dtype=np.float32)).reshape(1, D)
    wg1 = np.ascontiguousarray(np.asarray(L0_ffn_wg1, dtype=np.float32))
    bg1 = np.ascontiguousarray(np.asarray(L0_ffn_bg1, dtype=np.float32)).reshape(1, 256)
    wg2 = np.ascontiguousarray(np.asarray(L0_ffn_wg2, dtype=np.float32))
    bg2 = np.ascontiguousarray(np.asarray(L0_ffn_bg2, dtype=np.float32)).reshape(1, 40)
    We = np.ascontiguousarray(np.asarray(L0_ffn_We, dtype=np.float32)).reshape(NE * D, D)
    be = np.ascontiguousarray(np.asarray(L0_ffn_be, dtype=np.float32))

    qkv_wT = np.ascontiguousarray(qkv_w.T)
    out_wT = np.ascontiguousarray(out_w.T)
    identf = np.eye(128, dtype=np.float32)
    identb = np.eye(128, dtype=ml_dtypes.bfloat16)
    triu = np.triu(np.ones((128, 128), dtype=np.float32)).astype(ml_dtypes.bfloat16)

    nc = build_program()
    in_maps = []
    for c in range(NCORES):
        b, half = c // 2, c % 2
        msk = np.zeros((1, 8), dtype=np.float32)
        if half == 1:
            msk[0, c - 1] = 1.0
        in_maps.append({
            "xin": np.ascontiguousarray(x[b, half * T:(half + 1) * T]),
            "qkvw": qkv_w, "qkvb": qkv_b, "outw": out_w, "outb": out_b,
            "qkvwT": qkv_wT, "outwT": out_wT,
            "wg1": wg1, "bg1": bg1, "wg2": wg2, "bg2": bg2,
            "wed": We, "bed": be,
            "identf": identf, "identb": identb, "triu": triu,
            "maskrow": msk,
        })
    res = run_bass_kernel_spmd(nc, in_maps, core_ids=list(range(NCORES)))
    if DEBUG_DUMPS:
        DBG.clear()
        DBG.update({k: [res.results[c][k] for c in range(NCORES)]
                    for k in res.results[0] if k.startswith("dbg_")})
    out = np.empty((4, 4096, D), dtype=np.float32)
    for c in range(NCORES):
        b, half = c // 2, c % 2
        out[b, half * T:(half + 1) * T] = res.results[c]["y"]
    return out


# version 12
# speedup vs baseline: 1.0666x; 1.0053x over previous
import numpy as np
import ml_dtypes
from contextlib import ExitStack

import concourse.bass as bass
import concourse.mybir as mybir
from concourse.bass_utils import run_bass_kernel_spmd

F32 = mybir.dt.float32
BF16 = mybir.dt.bfloat16
AF = mybir.ActivationFunctionType
ALU = mybir.AluOpType
AX = mybir.AxisListType

NCORES = 8
T = 2048
NT = 16
D = 1024
NK = 8
H = 16
DH = 64
NE = 20
HEPS = 1e-6
LNEPS = 1e-5
STOP_AFTER = 3
STOP_MICRO = None
DEBUG_DUMPS = False
DBG = {}


class StopEmission(Exception):
    pass


class Cnt:
    def __init__(self):
        self.v = 0

    def inc(self, n=1):
        self.v += n
        return self.v


def build_program():
    nc = bass.Bass("TRN2", target_bir_lowering=False, debug=False)

    xin = nc.dram_tensor("xin", [T, D], F32, kind="ExternalInput").ap()
    qkvw = nc.dram_tensor("qkvw", [3 * D, D], F32, kind="ExternalInput").ap()
    qkvb = nc.dram_tensor("qkvb", [1, 3 * D], F32, kind="ExternalInput").ap()
    outw = nc.dram_tensor("outw", [D, D], F32, kind="ExternalInput").ap()
    qkvwT = nc.dram_tensor("qkvwT", [D, 3 * D], F32, kind="ExternalInput").ap()
    outwT = nc.dram_tensor("outwT", [D, D], F32, kind="ExternalInput").ap()
    outb = nc.dram_tensor("outb", [1, D], F32, kind="ExternalInput").ap()
    wg1d = nc.dram_tensor("wg1", [D, 256], F32, kind="ExternalInput").ap()
    bg1d = nc.dram_tensor("bg1", [1, 256], F32, kind="ExternalInput").ap()
    wg2d = nc.dram_tensor("wg2", [256, 40], F32, kind="ExternalInput").ap()
    bg2d = nc.dram_tensor("bg2", [1, 40], F32, kind="ExternalInput").ap()
    wed = nc.dram_tensor("wed", [NE * D, D], F32, kind="ExternalInput").ap()
    bed = nc.dram_tensor("bed", [NE, D], F32, kind="ExternalInput").ap()
    identf_d = nc.dram_tensor("identf", [128, 128], F32, kind="ExternalInput").ap()
    identb_d = nc.dram_tensor("identb", [128, 128], BF16, kind="ExternalInput").ap()
    triu_d = nc.dram_tensor("triu", [128, 128], BF16, kind="ExternalInput").ap()
    maskrow_d = nc.dram_tensor("maskrow", [1, 8], F32, kind="ExternalInput").ap()
    ydram = nc.dram_tensor("y", [T, D], F32, kind="ExternalOutput").ap()
    if DEBUG_DUMPS:
        dbg_lnT = nc.dram_tensor("dbg_lnT", [128, NK * T], F32, kind="ExternalOutput").ap()
        dbg_hT = nc.dram_tensor("dbg_hT", [128, 2 * T], F32, kind="ExternalOutput").ap()
        dbg_gates = nc.dram_tensor("dbg_gates", [128, NT * NE], F32, kind="ExternalOutput").ap()
        dbg_misc = nc.dram_tensor("dbg_misc", [128, 256], F32, kind="ExternalOutput").ap()
        dbg_att = nc.dram_tensor("dbg_att", [128, NT * D], F32, kind="ExternalOutput").ap()
        dbg_den = nc.dram_tensor("dbg_den", [128, 256], F32, kind="ExternalOutput").ap()
        dbg_st = nc.dram_tensor("dbg_st", [64, 4 * H * 65], F32, kind="ExternalOutput").ap()

    qkvT_hbm = nc.dram_tensor("qkvT_hbm", [NK, 128, 3 * D], BF16)
    xspill = nc.dram_tensor("xspill", [128, NT * D], F32)
    contrib_dram = nc.dram_tensor("contrib_dram", [128, 8 * 65], F32)
    gathered_dram = nc.dram_tensor("gathered_dram", [NCORES * 128, 8 * 65], F32)

    B0 = (nc.sbuf_base + 31) // 32 * 32
    A = B0
    B = A + 66560
    C = B + 32768
    Dr = C + 49152
    E = Dr + 16384

    at = lambda name, shape, dt, offset: nc.alloc_sbuf_tensor_at(name, shape, dt, offset=offset).ap()

    xsb = at("xsb", [128, NT * D], F32, A)
    attnum = at("attnum", [128, NT * D], BF16, A + 32768)
    U_sb = at("U_sb", [128, 2048], F32, A)
    contrib = at("contrib", [128, 8 * 65], F32, A + 8192)
    den_all = at("den_all", [128, 256], F32, A + 12416)
    rec = at("rec", [128, 32], F32, A + 13440)
    gbuf = [at(f"gbuf{i}", [128, 8 * 65], F32, A + i * 2112) for i in range(2)]
    qT2 = at("qT2", [128, T], BF16, A + 13568)
    kT2 = at("kT2", [128, T], BF16, A + 17664)
    vT2 = at("vT2", [128, T], BF16, A + 21760)
    ktok2 = at("ktok2", [128, 16 * 64], BF16, A + 25856)
    v_aug2 = at("v_aug2", [128, 16 * 65], BF16, A + 27904)
    sown2 = at("sown2", [128, 16 * 65], BF16, A + 30016)

    lnT = at("lnT", [128, NK * T], BF16, B)
    asm = [at(f"asm{i}", [128, 3 * D], BF16, B + i * 6144) for i in range(2)]
    qkvb_row = at("qkvb_row", [1, 3 * D], F32, B + 12288)
    bg1row = at("bg1row", [1, 256], F32, B + 24576)

    stage = at("stage", [128, 24 * D], BF16, C)
    web = [at(f"web{i}", [128, NK * D], BF16, C + i * 16384) for i in range(2)]
    hT = at("hT", [128, 2 * T], BF16, C + 32768)
    qwt = at("qwt", [128, NK * 3 * D], BF16, C)
    xstage = [at(f"xstage{i}", [128, D], F32, C + i * 4096) for i in range(2)]
    attT2b = [at(f"attT{i}", [128, D], BF16, C + 8192 + i * 2048) for i in range(2)]

    owt = at("owt", [128, NK * D], BF16, Dr)

    off = [E]

    def tal(name, shape, dt):
        per = int(np.prod(shape[1:])) * mybir.dt.size(dt)
        o = (off[0] + 31) // 32 * 32
        off[0] = o + per
        return at(name, shape, dt, o)

    identf = tal("identfs", [128, 128], F32)
    identb = tal("identbs", [128, 128], BF16)
    triu = tal("trius", [128, 128], BF16)
    wg1b = tal("wg1b", [128, NK * 256], BF16)
    wg2b = tal("wg2b", [128, 2 * 40], BF16)
    beb = tal("beb", [NE, D], BF16)
    qkvb64 = tal("qkvb64", [64, 48], F32)
    outbbf = tal("outbbf", [1, D], BF16)
    bg2row = tal("bg2row", [1, 40], BF16)
    onesbf = tal("onesbf", [1, 512], BF16)
    onesf = tal("onesf", [1, 128], F32)
    maskrow = tal("maskrow_sb", [1, 8], F32)
    maskcol = tal("maskcol", [128, 8], F32)
    gates_all = tal("gates_all", [128, NT * NE], F32)
    gatesT = tal("gatesT", [NE, NT * 128], BF16)
    stats = tal("stats", [128, 64], F32)
    lnstage = tal("lnstage", [128, D], BF16)
    probs = tal("probs", [128, 64], F32)
    m8sb = tal("m8sb", [128, 8], F32)
    bg1cols = tal("bg1cols", [128, 2], F32)
    qT_h = tal("qT_h", [128, T], BF16)
    kT_h = tal("kT_h", [128, T], BF16)
    vT_h = tal("vT_h", [128, T], BF16)
    ktok = tal("ktok", [128, 16 * 64], BF16)
    v_aug = tal("v_aug", [128, 16 * 65], BF16)
    sown = tal("sown", [128, 16 * 65], BF16)
    S0 = tal("S0", [128, 8 * 65], F32)
    S0bf = tal("S0bf", [128, 8 * 65], BF16)
    S0obf = tal("S0obf", [64, 8 * 65], BF16)
    sacc = tal("sacc", [128, 72], F32)
    qkvb128 = tal("qkvb128", [128, 24], F32)
    numtot = tal("numtot", [128, 512], F32)
    epscol = tal("epscol", [128, 1], F32)
    atm2 = tal("atm2", [128, 256], BF16)
    assert off[0] <= 229340, f"tail overflow {off[0]}"

    with ExitStack() as ctx:
        pbig0 = ctx.enter_context(nc.psum_tensor("pbig0", [128, 1024], F32))
        pbig1 = ctx.enter_context(nc.psum_tensor("pbig1", [128, 1024], F32))
        psm0 = ctx.enter_context(nc.psum_tensor("psm0", [128, 512], F32))
        psm1 = ctx.enter_context(nc.psum_tensor("psm1", [128, 512], F32))
        pbf0 = ctx.enter_context(nc.psum_tensor("pbf0", [128, 1024], BF16))
        pbf1 = ctx.enter_context(nc.psum_tensor("pbf1", [128, 1024], BF16))
        pbig = [pbig0, pbig1]
        psm = [psm0, psm1]
        pbf = [pbf0, pbf1]

        semPE = ctx.enter_context(nc.semaphore("semPE"))
        semDVE = ctx.enter_context(nc.semaphore("semDVE"))
        semACT = ctx.enter_context(nc.semaphore("semACT"))
        semIN = ctx.enter_context(nc.semaphore("semIN"))
        semW = ctx.enter_context(nc.semaphore("semW"))
        semSP = ctx.enter_context(nc.semaphore("semSP"))
        semGP = ctx.enter_context(nc.semaphore("semGP"))
        semCC = ctx.enter_context(nc.semaphore("semCC"))
        block = ctx.enter_context(nc.Block())

        cPE, cDVE, cACT = Cnt(), Cnt(), Cnt()
        cIN, cW, cSP, cGP, cCC = Cnt(), Cnt(), Cnt(), Cnt(), Cnt()

        t_, v_, a_, sy, gp = nc.tensor, nc.vector, nc.scalar, nc.sync, nc.gpsimd

        def mpe(i):
            i.then_inc(semPE, 1)
            return cPE.inc()

        def mdve(i):
            i.then_inc(semDVE, 1)
            return cDVE.inc()

        def mact(i):
            i.then_inc(semACT, 1)
            return cACT.inc()

        def win(i):
            i.then_inc(semIN, 16)
            return cIN.inc(16)

        def ww(i):
            i.then_inc(semW, 16)
            return cW.inc(16)

        def wsp(i):
            i.then_inc(semSP, 16)
            return cSP.inc(16)

        def wgp(i):
            i.then_inc(semGP, 16)
            return cGP.inc(16)

        def wall(eng):
            for sem, c in ((semPE, cPE), (semDVE, cDVE), (semACT, cACT), (semIN, cIN),
                           (semW, cW), (semSP, cSP), (semGP, cGP), (semCC, cCC)):
                if c.v:
                    eng.wait_ge(sem, c.v)

        @block.sync
        def _(sync):
          try:
            def micro(n):
                if STOP_MICRO is not None and STOP_MICRO == n:
                    raise StopEmission()

            win(sy.dma_start(out=xsb.rearrange("p (t d) -> p t d", d=D),
                             in_=xin.rearrange("(t p) d -> p t d", p=128)))
            micro(1)
            win(sy.dma_start(out=identf, in_=identf_d))
            win(sy.dma_start(out=identb, in_=identb_d))
            win(sy.dma_start(out=triu, in_=triu_d))
            win(sy.dma_start(out=maskrow, in_=maskrow_d))
            win(sy.dma_start(out=qkvb_row, in_=qkvb))
            win(sy.dma_start(out=bg1row, in_=bg1d))
            in_all = cIN.v

            ww(gp.dma_start(out=wg1b.rearrange("p (k f) -> p k f", k=NK),
                            in_=wg1d.rearrange("(k p) f -> p k f", p=128)))
            ww(gp.dma_start(out=wg2b.rearrange("p (k f) -> p k f", k=2),
                            in_=wg2d.rearrange("(k p) f -> p k f", p=128)))
            ww(gp.dma_start(out=beb, in_=bed))
            ww(gp.dma_start(out=outbbf, in_=outb))
            ww(gp.dma_start(out=bg2row, in_=bg2d))
            ww(gp.dma_start(out=owt.rearrange("p (k o) -> p k o", k=NK),
                            in_=outwT.rearrange("(k p) o -> p k o", p=128)))

            v_.memset(onesbf, 1.0)
            v_.memset(onesf, 1.0)
            m_ones = mdve(v_.memset(v_aug.rearrange("p (c s) -> p c s", s=65)[:, :, 64:65], 1.0))
            micro(3)

            t_.wait_ge(semIN, in_all)
            t_.wait_ge(semDVE, m_ones)
            for c in range(3):
                for h in range(H):
                    j = c * 16 + h
                    t_.matmul(psm0[0:64, j:j + 1],
                              qkvb_row[0:1, c * D + h * DH: c * D + (h + 1) * DH],
                              onesf[0:1, 0:1], start=True, stop=True)
            t_.matmul(psm1[0:128, 0:1], bg1row[0:1, 0:128], onesf[0:1, 0:1],
                      start=True, stop=True)
            t_.matmul(psm1[0:128, 1:2], bg1row[0:1, 128:256], onesf[0:1, 0:1],
                      start=True, stop=True)
            for c in range(3):
                for p in range(8):
                    j = c * 8 + p
                    t_.matmul(psm1[0:128, 16 + j:17 + j],
                              qkvb_row[0:1, c * D + p * 128: c * D + (p + 1) * 128],
                              onesf[0:1, 0:1], start=True, stop=True)
            m_cols = mpe(t_.matmul(psm1[0:128, 8:16], onesf[0:1, 0:128], maskrow[0:1, 0:8],
                                   start=True, stop=True))
            v_.wait_ge(semPE, m_cols)
            m_cp1 = mdve(v_.tensor_copy(qkvb64, psm0[0:64, 0:48]))
            v_.tensor_copy(bg1cols, psm1[0:128, 0:2])
            v_.tensor_copy(maskcol, psm1[0:128, 8:16])
            v_.tensor_copy(qkvb128, psm1[0:128, 16:40])
            v_.wait_ge(semDVE, m_cp1)
            v_.tensor_scalar(qkvb64[:, 0:32], qkvb64[:, 0:32], HEPS, None, ALU.add)
            m_bias = mdve(v_.tensor_scalar(qkvb128[:, 0:16], qkvb128[:, 0:16], HEPS, None, ALU.add))
            micro(6)

            def ln_emit():
                m_sd = {}
                for t in range(NT):
                    xt = xsb[:, t * D:(t + 1) * D]
                    sc4 = stats[:, t * 4: t * 4 + 4]
                    bn = probs[:, (t % 2) * 12:(t % 2) * 12 + 12]
                    v_.bn_stats(bn[:, 0:6], xt[:, 0:512])
                    m_b1 = mdve(v_.bn_stats(bn[:, 6:12], xt[:, 512:1024]))
                    v_.wait_ge(semDVE, m_b1)
                    m_b2 = mdve(v_.bn_aggr(bn[:, 0:2] if False else probs[:, 24:26], bn[:, 0:12]))
                    v_.wait_ge(semDVE, m_b2)
                    v_.tensor_scalar_mul(sc4[:, 0:1], probs[:, 24:25], -1.0)
                    m_var = mdve(v_.tensor_scalar_add(sc4[:, 1:2], probs[:, 25:26], LNEPS))
                    a_.wait_ge(semDVE, m_var)
                    m_sd[t] = mact(a_.activation(sc4[:, 2:3], sc4[:, 1:2], AF.Sqrt))
                v_.wait_ge(semACT, m_sd[NT - 1])
                sdv = stats.rearrange("p (t f) -> p t f", f=4)
                m_rc = mdve(v_.reciprocal(sdv[:, :, 3:4], sdv[:, :, 2:3]))
                v_.wait_ge(semDVE, m_rc)
                pbf_free = [0, 0]
                tp_m = {}
                lnT_done = 0
                for t in range(NT):
                    xt = xsb[:, t * D:(t + 1) * D]
                    sc4 = stats[:, t * 4: t * 4 + 4]
                    if t >= 1:
                        v_.wait_ge(semPE, tp_m[t - 1])
                    m_ln = mdve(v_.tensor_scalar(lnstage, xt, sc4[:, 0:1], sc4[:, 3:4],
                                                 ALU.add, ALU.mult))
                    pb = pbf[t % 2]
                    t_.wait_ge(semDVE, m_ln)
                    if pbf_free[t % 2]:
                        t_.wait_ge(semDVE, pbf_free[t % 2])
                    for k in range(NK):
                        mm = t_.transpose(pb[:, k * 128:(k + 1) * 128],
                                          lnstage[:, k * 128:(k + 1) * 128], identb)
                    tp_m[t] = mpe(mm)
                    t_.matmul(pbig1[:, 0:512], identb, lnstage[:, 0:512], start=True, stop=True)
                    t_.matmul(pbig1[:, 0:512], identb, lnstage[:, 0:512], start=True, stop=True)
                    v_.wait_ge(semPE, tp_m[t])
                    lnT_done = mdve(v_.tensor_copy(
                        lnT.rearrange("p (k s) -> p k s", k=NK)[:, :, t * 128:(t + 1) * 128],
                        pb.rearrange("p (k s) -> p k s", k=NK)))
                    pbf_free[t % 2] = lnT_done
                return lnT_done

            def ffn_emit():
                wall(t_); wall(v_); wall(a_); wall(gp)

                web_loaded = {}
                for e in range(2):
                    web_loaded[e] = ww(gp.dma_start(
                        out=web[e].rearrange("p (k f) -> p k f", k=NK),
                        in_=wed[e * D:(e + 1) * D, :].rearrange("(k p) f -> p k f", p=128)))

                lnT_done = ln_emit()

                t_.wait_ge(semDVE, lnT_done)
                h_acts = []
                for fc in range(2):
                    for nb in range(4):
                        ps = psm[nb % 2]
                        if len(h_acts) >= 2:
                            t_.wait_ge(semACT, h_acts[-2])
                        for k in range(NK):
                            mm = t_.matmul(ps[:, :],
                                           wg1b[:, k * 256 + fc * 128: k * 256 + fc * 128 + 128],
                                           lnT[:, k * T + nb * 512: k * T + (nb + 1) * 512],
                                           start=(k == 0), stop=(k == NK - 1))
                        mk = mpe(mm)
                        a_.wait_ge(semPE, mk)
                        h_acts.append(mact(a_.activation(
                            hT[:, fc * T + nb * 512: fc * T + (nb + 1) * 512], ps[:, :],
                            AF.Gelu, bias=bg1cols[:, fc:fc + 1])))
                h_all = cACT.v

                t_.wait_ge(semACT, h_all)
                for t in range(NT):
                    for fc in range(2):
                        t_.matmul(psm0[:, t * 20:(t + 1) * 20],
                                  hT[:, fc * T + t * 128: fc * T + (t + 1) * 128],
                                  wg2b[:, fc * 40: fc * 40 + 20],
                                  start=(fc == 0), stop=False)
                    mm = t_.matmul(psm0[:, t * 20:(t + 1) * 20], onesbf[0:1, 0:128],
                                   bg2row[0:1, 0:20], start=False, stop=True)
                m_lg = mpe(mm)
                v_.wait_ge(semPE, m_lg)
                m_nm = mdve(v_.tensor_reduce(
                    stats[:, 0:16], psm0[:, 0:320].rearrange("p (t e) -> p t e", e=20),
                    AX.X, ALU.max, negate=True))
                a_.wait_ge(semDVE, m_nm)
                pall = numtot[:, 0:320]
                for t in range(NT):
                    m_exp = mact(a_.activation(pall[:, t * 20:(t + 1) * 20],
                                               psm0[:, t * 20:(t + 1) * 20], AF.Exp,
                                               bias=stats[:, t:t + 1]))
                exp_read = m_exp
                v_.wait_ge(semACT, exp_read)
                m_rs = mdve(v_.tensor_reduce(
                    stats[:, 16:32], pall.rearrange("p (t e) -> p t e", e=20),
                    AX.X, ALU.add))
                v_.wait_ge(semDVE, m_rs)
                m_rc = mdve(v_.reciprocal(stats[:, 32:48], stats[:, 16:32]))
                v_.wait_ge(semDVE, m_rc)
                gT_evac = 0
                m8_read = 0
                for t in range(NT):
                    m_ps = mdve(v_.tensor_scalar(gates_all[:, t * NE:(t + 1) * NE],
                                                 pall[:, t * 20:(t + 1) * 20],
                                                 stats[:, 32 + t:33 + t], 0.5,
                                                 ALU.mult, ALU.mult))
                    v_.wait_ge(semDVE, m_ps)
                    m_m8 = mdve(v_.max(m8sb[:, 0:8], gates_all[:, t * NE:(t + 1) * NE]))
                    v_.wait_ge(semDVE, m_m8)
                    m8_read = mdve(v_.scalar_tensor_tensor(
                        gates_all[:, t * NE:(t + 1) * NE],
                        gates_all[:, t * NE:(t + 1) * NE], m8sb[:, 1:2],
                        gates_all[:, t * NE:(t + 1) * NE], ALU.is_ge, ALU.mult))
                    t_.wait_ge(semDVE, m8_read)
                    if gT_evac:
                        t_.wait_ge(semDVE, gT_evac)
                    m_gt = mpe(t_.transpose(psm1[0:20, 0:128],
                                            gates_all[:, t * NE:(t + 1) * NE], identf))
                    v_.wait_ge(semPE, m_gt)
                    gT_evac = mdve(v_.tensor_copy(gatesT[:, t * 128:(t + 1) * 128],
                                                  psm1[0:20, 0:128]))

                bank_free = [0, 0]
                t_.wait_ge(semDVE, gT_evac)
                for t in range(NT):
                    pb = pbig[t % 2]
                    if bank_free[t % 2]:
                        t_.wait_ge(semDVE, bank_free[t % 2])
                    t_.matmul(pb[:, 0:512], gatesT[:, t * 128:(t + 1) * 128], beb[:, 0:512],
                              start=True, stop=True)
                    m_be = mpe(t_.matmul(pb[:, 512:1024], gatesT[:, t * 128:(t + 1) * 128],
                                         beb[:, 512:1024], start=True, stop=True))
                    v_.wait_ge(semPE, m_be)
                    v_.tensor_add(xsb[:, t * D: t * D + 512], xsb[:, t * D: t * D + 512],
                                  pb[:, 0:512])
                    bank_free[t % 2] = mdve(v_.tensor_add(
                        xsb[:, t * D + 512: t * D + 1024], xsb[:, t * D + 512: t * D + 1024],
                        pb[:, 512:1024]))

                micro(11)
                stt_m = {}
                t_.wait_ge(semACT, exp_read)
                ebanks = [(pbig0[:, 0:512], pbig0[:, 512:1024]),
                          (pbig1[:, 0:512], pbig1[:, 512:1024]),
                          (psm0[:, 0:512], psm1[:, 0:512])]
                for e in range(NE):
                    t_.wait_ge(semW, web_loaded[e])
                    mm_last = 0
                    for t in range(NT):
                        idx = e * NT + t
                        bh = ebanks[idx % 3]
                        if idx - 3 >= 0:
                            t_.wait_ge(semDVE, stt_m[idx - 3])
                        elif bank_free[idx % 2]:
                            t_.wait_ge(semDVE, bank_free[idx % 2])
                        for k in range(NK):
                            for nb in range(2):
                                mm = t_.matmul(bh[nb],
                                               lnT[:, k * T + t * 128: k * T + (t + 1) * 128],
                                               web[e % 2][:, k * D + nb * 512: k * D + (nb + 1) * 512],
                                               start=(k == 0), stop=(k == NK - 1))
                        mm_last = mpe(mm)
                        v_.wait_ge(semPE, mm_last)
                        stt_m[idx] = mdve(v_.scalar_tensor_tensor(
                            xsb[:, t * D: t * D + 512], bh[0],
                            gates_all[:, t * NE + e: t * NE + e + 1],
                            xsb[:, t * D: t * D + 512], ALU.mult, ALU.add))
                        stt_m[idx] = mdve(v_.scalar_tensor_tensor(
                            xsb[:, t * D + 512:(t + 1) * D], bh[1],
                            gates_all[:, t * NE + e: t * NE + e + 1],
                            xsb[:, t * D + 512:(t + 1) * D], ALU.mult, ALU.add))
                    if e + 2 < NE:
                        gp.wait_ge(semPE, mm_last)
                        web_loaded[e + 2] = ww(gp.dma_start(
                            out=web[e % 2].rearrange("p (k f) -> p k f", k=NK),
                            in_=wed[(e + 2) * D:(e + 3) * D, :].rearrange("(k p) f -> p k f", p=128)))

            if STOP_AFTER >= 1:
                ffn_emit()

            if STOP_AFTER < 2:
                raise StopEmission()
            wall(t_); wall(v_); wall(a_); wall(gp); wall(sy)

            lnT_done = ln_emit()

            sy.wait_ge(semDVE, lnT_done)
            sp_x = wsp(sy.dma_start(out=xspill.ap(), in_=xsb))
            for part in range(3):
                w_qwt = ww(gp.dma_start(
                    out=qwt.rearrange("p (k o) -> p k o", k=NK)[:, :, part * D:(part + 1) * D],
                    in_=qkvwT.rearrange("(k p) o -> p k o", p=128)[:, :, part * D:(part + 1) * D]))

            t_.wait_ge(semW, w_qwt)
            t_.wait_ge(semSP, sp_x)
            v_.wait_ge(semW, w_qwt)
            v_.wait_ge(semSP, sp_x)
            a_.wait_ge(semSP, sp_x)
            mdve(v_.memset(v_aug2.rearrange("p (c s) -> p c s", s=65)[:, :, 64:65], 1.0))

            def qkv_comp(h, comp, dst, with_relu, guard=0):
                biascol = qkvb64[:, comp * 16 + h: comp * 16 + h + 1]
                ms = []
                for nb in range(4):
                    ps = psm[nb % 2]
                    if len(ms) >= 2:
                        t_.wait_ge(semDVE, ms[-2])
                    elif guard:
                        t_.wait_ge(semDVE, guard)
                    for k in range(NK):
                        mm = t_.matmul(ps[0:64, :],
                                       qwt[:, k * 3072 + comp * D + h * DH: k * 3072 + comp * D + (h + 1) * DH],
                                       lnT[:, k * T + nb * 512: k * T + (nb + 1) * 512],
                                       start=(k == 0), stop=(k == NK - 1))
                    mk = mpe(mm)
                    v_.wait_ge(semPE, mk)
                    if with_relu:
                        ms.append(mdve(v_.tensor_scalar(dst[:, nb * 512:(nb + 1) * 512],
                                                        ps[0:64, :], biascol, HEPS,
                                                        ALU.add, ALU.max)))
                    else:
                        ms.append(mdve(v_.tensor_scalar(dst[:, nb * 512:(nb + 1) * 512],
                                                        ps[0:64, :], biascol, None, ALU.add)))
                return ms[-1]

            PBUF = [dict(qT=qT_h, kT=kT_h, vT=vT_h), dict(qT=qT2, kT=kT2, vT=vT2)]
            HBUF = [dict(kt=ktok, va=v_aug, so=sown), dict(kt=ktok2, va=v_aug2, so=sown2)]
            vaug_m = {}
            scan_steps = {}
            scan_m = {}
            pair_evac = {}

            def make_A_units(p, guard0):
                B = PBUF[p % 2]
                st = {"ms": []}
                dsts = [B["qT"], B["kT"], B["vT"]]
                units = []

                def qunit(comp, nb):
                    def go():
                        ps = psm[(comp * 4 + nb) % 2]
                        if len(st["ms"]) >= 2:
                            t_.wait_ge(semDVE, st["ms"][-2])
                        elif guard0:
                            t_.wait_ge(semDVE, guard0)
                        for k in range(NK):
                            mm = t_.matmul(
                                ps[:, :],
                                qwt[:, k * 3072 + comp * D + p * 128: k * 3072 + comp * D + (p + 1) * 128],
                                lnT[:, k * T + nb * 512: k * T + (nb + 1) * 512],
                                start=(k == 0), stop=(k == NK - 1))
                        mk = mpe(mm)
                        v_.wait_ge(semPE, mk)
                        biascol = qkvb128[:, comp * 8 + p: comp * 8 + p + 1]
                        if comp < 2:
                            st["ms"].append(mdve(v_.tensor_scalar(
                                dsts[comp][:, nb * 512:(nb + 1) * 512], ps[:, :],
                                biascol, HEPS, ALU.add, ALU.max)))
                        else:
                            st["ms"].append(mdve(v_.tensor_scalar(
                                dsts[comp][:, nb * 512:(nb + 1) * 512], ps[:, :],
                                biascol, None, ALU.add)))
                        pair_evac[p] = st["ms"][-1]
                    return go

                for comp in range(3):
                    for nb in range(4):
                        units.append(qunit(comp, nb))
                return units

            def stage_B(h):
                B = PBUF[(h // 2) % 2]
                HB = HBUF[h % 2]
                rb = (h % 2) * 64
                t_.wait_ge(semDVE, pair_evac[h // 2])
                if h >= 1:
                    t_.wait_ge(semDVE, vaug_m[h - 1])
                idd = identb[rb:rb + 64, rb:rb + 64]
                for c in range(16):
                    mmk = t_.transpose(pbf0[:, c * 64:(c + 1) * 64],
                                       B["kT"][rb:rb + 64, c * 128:(c + 1) * 128], idd)
                mpe(mmk)
                for c in range(16):
                    mmv = t_.transpose(pbf1[:, c * 64:(c + 1) * 64],
                                       B["vT"][rb:rb + 64, c * 128:(c + 1) * 128], idd)
                m_tv = mpe(mmv)
                v_.wait_ge(semPE, m_tv)
                v_.tensor_copy(HB["kt"], pbf0[:, :])
                vaug_m[h] = mdve(v_.tensor_copy(
                    HB["va"].rearrange("p (c s) -> p c s", s=65)[:, :, 0:64],
                    pbf1.rearrange("p (c s) -> p c s", s=64)))

            def stage_U_pair(p, pbig_free_act):
                if pbig_free_act:
                    t_.wait_ge(semACT, pbig_free_act)
                t_.wait_ge(semDVE, vaug_m[2 * p + 1])
                for hb in range(2):
                    HB = HBUF[(2 * p + hb) % 2]
                    rb = hb * 64
                    tp = (0, 64) if hb else None
                    for c in range(16):
                        pbU = pbig[c // 8]
                        mm = t_.matmul(pbU[rb:rb + 64, (c % 8) * 128:(c % 8) * 128 + 65],
                                       HB["kt"][:, c * 64:(c + 1) * 64],
                                       HB["va"][:, c * 65:(c + 1) * 65], start=True, stop=True,
                                       tile_position=tp)
                m_U = mpe(mm)
                v_.wait_ge(semPE, m_U)
                v_.tensor_copy(U_sb[:, 0:1024], pbig0[:, :])
                mdve(v_.tensor_copy(U_sb[:, 1024:2048], pbig1[:, :]))
                Us = U_sb.rearrange("p (c s) -> p c s", s=128)
                soA = HBUF[0]["so"]
                soB = HBUF[1]["so"]
                so_even = HBUF[(2 * p) % 2]["so"]
                so_odd = HBUF[(2 * p + 1) % 2]["so"]
                steps = {}
                v_.memset(so_even[0:64, 0:65], 0.0)
                steps[0] = mdve(v_.memset(so_odd[64:128, 0:65], 0.0))
                m_sc = mdve(v_.tensor_copy(sacc[:, 0:65], Us[:, 0, 0:65]))
                for c in range(1, 16):
                    v_.wait_ge(semDVE, m_sc)
                    v_.tensor_copy(so_even[0:64, c * 65:(c + 1) * 65], sacc[0:64, 0:65])
                    steps[c] = mdve(v_.tensor_copy(so_odd[64:128, c * 65:(c + 1) * 65],
                                                   sacc[64:128, 0:65]))
                    if c < 15:
                        m_sc = mdve(v_.tensor_add(sacc[:, 0:65], sacc[:, 0:65],
                                                  Us[:, c, 0:65]))
                m_ct = mdve(v_.tensor_add(contrib[:, p * 65:(p + 1) * 65],
                                          sacc[:, 0:65], Us[:, 15, 0:65]))
                scan_m[2 * p] = m_ct
                scan_m[2 * p + 1] = m_ct
                scan_steps[2 * p] = steps
                scan_steps[2 * p + 1] = steps

            def stage_D(h, aunits):
                B = PBUF[(h // 2) % 2]
                HB = HBUF[h % 2]
                rb = (h % 2) * 64
                t_.wait_ge(semDVE, scan_steps[h][1])
                atm_m = {}
                nd_act = {}
                nd_pe = {}
                for c in range(16):
                    atv = pbig0[:, (c % 2) * 512:(c % 2) * 512 + 128]
                    if c >= 2:
                        t_.wait_ge(semDVE, atm_m[c - 2])
                    m_at = mpe(t_.matmul(atv, B["kT"][rb:rb + 64, c * 128:(c + 1) * 128],
                                         B["qT"][rb:rb + 64, c * 128:(c + 1) * 128],
                                         start=True, stop=True))
                    v_.wait_ge(semPE, m_at)
                    if c >= 2:
                        v_.wait_ge(semPE, nd_pe[c - 2])
                    atm_m[c] = mdve(v_.tensor_mul(atm2[:, (c % 2) * 128:(c % 2) * 128 + 128],
                                                  atv, triu))
                    if aunits and (c % 2 == 1):
                        aunits.pop(0)()
                    t_.wait_ge(semDVE, atm_m[c])
                    t_.wait_ge(semDVE, scan_steps[h][c])
                    if c >= 2:
                        t_.wait_ge(semACT, nd_act[c - 2])
                    ndv = pbig1[:, (c % 2) * 512:(c % 2) * 512 + 65]
                    t_.matmul(ndv, atm2[:, (c % 2) * 128:(c % 2) * 128 + 128],
                              HB["va"][:, c * 65:(c + 1) * 65], start=True, stop=False)
                    nd_pe[c] = mpe(t_.matmul(ndv, B["qT"][rb:rb + 64, c * 128:(c + 1) * 128],
                                             HB["so"][rb:rb + 64, c * 65:(c + 1) * 65],
                                             start=False, stop=True))
                    a_.wait_ge(semPE, nd_pe[c])
                    a_.copy(attnum[:, c * D + h * DH: c * D + (h + 1) * DH], ndv[:, 0:64])
                    nd_act[c] = mact(a_.copy(den_all[:, h * 16 + c: h * 16 + c + 1],
                                             ndv[:, 64:65]))
                return nd_act[15]

            for u in make_A_units(0, 0):
                u()
            stage_B(0)
            stage_B(1)
            stage_U_pair(0, 0)
            aunits = make_A_units(1, pair_evac[0])
            nd_read_act = 0
            for h in range(H):
                nd_read_act = stage_D(h, aunits)
                if h + 1 >= H:
                    break
                if (h + 1) % 2 == 0:
                    p_next = (h + 1) // 2
                    for u in aunits:
                        u()
                    stage_B(2 * p_next)
                    stage_B(2 * p_next + 1)
                    stage_U_pair(p_next, nd_read_act)
                    aunits = (make_A_units(p_next + 1, pair_evac[p_next])
                              if p_next + 1 < H // 2 else [])
            scan_done = scan_m[H - 1]
            nd_read = cDVE.v

            micro(19)
            gp.wait_ge(semDVE, scan_done)
            c_dma = wgp(gp.dma_start(out=contrib_dram.ap(), in_=contrib))
            gp.wait_ge(semGP, c_dma)
            gp.collective_compute(
                "AllGather", ALU.bypass,
                replica_groups=[list(range(NCORES))],
                ins=[contrib_dram.ap().opt()],
                outs=[gathered_dram.ap().opt()],
            ).then_inc(semCC, 1)
            cCC.inc()
            gp.wait_ge(semCC, cCC.v)
            g_reads = {}
            for r in range(NCORES):
                if r >= 2:
                    gp.wait_ge(semDVE, g_reads[r - 2])
                g_dma = wgp(gp.dma_start(out=gbuf[r % 2],
                                         in_=gathered_dram.ap()[r * 128:(r + 1) * 128, :]))
                v_.wait_ge(semGP, g_dma)
                if r == 0:
                    g_reads[0] = mdve(v_.tensor_scalar_mul(S0, gbuf[0], maskcol[:, 0:1]))
                else:
                    v_.wait_ge(semDVE, g_reads[r - 1])
                    g_reads[r] = mdve(v_.scalar_tensor_tensor(S0, gbuf[r % 2],
                                                              maskcol[:, r:r + 1], S0,
                                                              ALU.mult, ALU.add))
            v_.wait_ge(semDVE, g_reads[NCORES - 1])
            m_s0 = mdve(v_.tensor_copy(S0bf, S0))
            gp.wait_ge(semDVE, m_s0)
            s0sh = wgp(gp.dma_start(out=S0obf, in_=S0bf[64:128, :]))

            t_.wait_ge(semACT, nd_read_act)
            v_.wait_ge(semACT, nd_read_act)
            CQ = [qT_h[0:64, :], qT2[0:64, :]]
            m_q = {0: qkv_comp(0, 0, CQ[0], True, guard=nd_read)}
            m_q[1] = qkv_comp(1, 0, CQ[1], True, guard=m_q[0])
            prefilled = 2
            t_.wait_ge(semDVE, m_s0)
            t_.wait_ge(semGP, s0sh)
            num_read = 0
            m_d1h = {}
            for h in range(H):
                qb = CQ[h % 2]
                t_.wait_ge(semDVE, m_q[h])
                if h > 0:
                    t_.wait_ge(semDVE, m_d1h[h - 1])
                s0src = S0bf if h % 2 == 0 else S0obf
                s0col = (h // 2) * 65
                for c in range(16):
                    mm = t_.matmul(pbig1[:, c:c + 1], qb[:, c * 128:(c + 1) * 128],
                                   s0src[0:64, s0col + 64: s0col + 65], start=True, stop=True)
                m_dc = mpe(mm)
                if num_read:
                    t_.wait_ge(semACT, num_read)
                for c in range(16):
                    mm = t_.matmul(pbig0[:, c * 64:(c + 1) * 64],
                                   qb[:, c * 128:(c + 1) * 128],
                                   s0src[0:64, s0col: s0col + 64], start=True, stop=True)
                m_nc = mpe(mm)
                if h + prefilled < H:
                    m_q[h + prefilled] = qkv_comp(h + prefilled, 0, CQ[(h + prefilled) % 2],
                                                  True, guard=m_q[h + prefilled - 1])
                v_.wait_ge(semPE, m_dc)
                if num_read:
                    v_.wait_ge(semACT, num_read)
                m_d1 = mdve(v_.tensor_add(rec[:, 0:16], den_all[:, h * 16:(h + 1) * 16],
                                          pbig1[:, 0:16]))
                m_d1h[h] = m_d1
                v_.wait_ge(semDVE, m_d1)
                m_d2 = mdve(v_.tensor_scalar_max(rec[:, 16:32], rec[:, 0:16], HEPS))
                v_.wait_ge(semDVE, m_d2)
                m_rec = mdve(v_.reciprocal(rec[:, 0:16], rec[:, 16:32]))
                v_.wait_ge(semDVE, m_rec)
                v_.wait_ge(semPE, m_nc)
                for half in range(2):
                    c0 = half * 8
                    if num_read:
                        v_.wait_ge(semACT, num_read)
                    m_nt = mdve(v_.tensor_add(
                        numtot.rearrange("p (c f) -> p c f", f=64),
                        pbig0[:, half * 512:(half + 1) * 512].rearrange("p (c f) -> p c f", f=64),
                        attnum.rearrange("p (c f) -> p c f", f=D)[:, c0:c0 + 8, h * DH:(h + 1) * DH]))
                    a_.wait_ge(semDVE, m_nt)
                    a_.wait_ge(semDVE, m_rec)
                    for cc in range(8):
                        c = c0 + cc
                        mm2 = a_.activation(
                            attnum[:, c * D + h * DH: c * D + (h + 1) * DH],
                            numtot[:, cc * 64:(cc + 1) * 64], AF.Copy,
                            scale=rec[:, c:c + 1])
                    num_read_act = mact(mm2)
                    num_read = num_read_act
            corr_done_act = num_read
            corr_done = cDVE.v
            qwt_free = cPE.v

            micro(20)
            t_.wait_ge(semDVE, corr_done)
            t_.wait_ge(semACT, corr_done_act)
            sy.wait_ge(semPE, qwt_free)
            sy.wait_ge(semDVE, corr_done)
            xs_dmas = {}
            xs_dmas[0] = wsp(sy.dma_start(out=xstage[0], in_=xspill.ap()[:, 0:D]))
            xs_dmas[1] = wsp(sy.dma_start(out=xstage[1], in_=xspill.ap()[:, D:2 * D]))
            attT_m = {}
            xup = {}
            attT_pe = {}
            for t in range(NT):
                pb = pbf[t % 2]
                pbg = pbig[t % 2]
                aT = attT2b[t % 2]
                if t >= 2:
                    t_.wait_ge(semDVE, attT_m[t - 2])
                for k in range(NK):
                    mm = t_.transpose(pb[:, k * 128:(k + 1) * 128],
                                      attnum[:, t * D + k * 128: t * D + (k + 1) * 128], identb)
                m_tt = mpe(mm)
                v_.wait_ge(semPE, m_tt)
                if t >= 2:
                    v_.wait_ge(semPE, attT_pe[t - 2])
                m_aT = mdve(v_.tensor_copy(aT, pb[:, :]))
                attT_m[t] = m_aT
                t_.wait_ge(semDVE, m_aT)
                if t >= 2:
                    t_.wait_ge(semDVE, xup[t - 2])
                for nb in range(2):
                    for k in range(NK):
                        t_.matmul(pbg[:, nb * 512:(nb + 1) * 512],
                                  aT[:, k * 128:(k + 1) * 128],
                                  owt[:, k * D + nb * 512: k * D + (nb + 1) * 512],
                                  start=(k == 0), stop=False)
                    mm = t_.matmul(pbg[:, nb * 512:(nb + 1) * 512], onesbf[0:1, 0:128],
                                   outbbf[0:1, nb * 512:(nb + 1) * 512], start=False, stop=True)
                m_op = mpe(mm)
                attT_pe[t] = m_op
                v_.wait_ge(semPE, m_op)
                v_.wait_ge(semSP, xs_dmas[t])
                v_.tensor_add(xsb[:, t * D: t * D + 512], xstage[t % 2][:, 0:512],
                              pbg[:, 0:512])
                xup[t] = mdve(v_.tensor_add(xsb[:, t * D + 512: t * D + 1024],
                                            xstage[t % 2][:, 512:1024], pbg[:, 512:1024]))
                if t + 2 < NT:
                    sy.wait_ge(semDVE, xup[t])
                    xs_dmas[t + 2] = wsp(sy.dma_start(
                        out=xstage[t % 2], in_=xspill.ap()[:, (t + 2) * D:(t + 3) * D]))

            if STOP_AFTER >= 3:
                ffn_emit()

          except StopEmission:
            pass
          if DEBUG_DUMPS:
            wall(gp)
            cGP.inc(16); gp.dma_start(out=dbg_lnT, in_=lnT).then_inc(semGP, 16)
            cGP.inc(16); gp.dma_start(out=dbg_hT, in_=hT).then_inc(semGP, 16)
            cGP.inc(16); gp.dma_start(out=dbg_gates, in_=gates_all).then_inc(semGP, 16)
            cGP.inc(16); gp.dma_start(out=dbg_misc[0:64, 0:48], in_=qkvb64).then_inc(semGP, 16)
            cGP.inc(16); gp.dma_start(out=dbg_misc[:, 48:112], in_=probs).then_inc(semGP, 16)
            cGP.inc(16); gp.dma_start(out=dbg_misc[:, 112:120], in_=m8sb).then_inc(semGP, 16)
            cGP.inc(16); gp.dma_start(out=dbg_misc[:, 120:136], in_=stats[:, 0:16]).then_inc(semGP, 16)
            if STOP_AFTER >= 2:
                cGP.inc(16); gp.dma_start(out=dbg_misc[:, 136:168], in_=rec).then_inc(semGP, 16)
                cGP.inc(16); gp.dma_start(out=dbg_misc[:, 168:232], in_=numtot[:, 448:512]).then_inc(semGP, 16)
                cGP.inc(16); gp.dma_start(out=dbg_att, in_=attnum).then_inc(semGP, 16)
                cGP.inc(16); gp.dma_start(out=dbg_den, in_=den_all).then_inc(semGP, 16)
                cGP.inc(16); gp.dma_start(out=dbg_st[:, 0:H * 65], in_=contrib).then_inc(semGP, 16)
                cGP.inc(16); gp.dma_start(out=dbg_st[:, H * 65:2 * H * 65], in_=S0).then_inc(semGP, 16)
                cGP.inc(16); gp.dma_start(out=dbg_st[:, 2 * H * 65:3 * H * 65], in_=S0bf).then_inc(semGP, 16)
                cGP.inc(16); gp.dma_start(out=dbg_st[:, 3 * H * 65:3 * H * 65 + 1040], in_=sown).then_inc(semGP, 16)
            gp.wait_ge(semGP, cGP.v)
          wall(sy)
          m_out = wsp(sy.dma_start(out=ydram.rearrange("(t p) d -> p t d", p=128),
                                   in_=xsb.rearrange("p (t d) -> p t d", d=D)))
          sy.wait_ge(semSP, m_out)

    return nc


def kernel(L0_attn_x, L0_attn_qkv_w, L0_attn_qkv_b, L0_attn_out_w, L0_attn_out_b,
           L0_ffn_wg1, L0_ffn_bg1, L0_ffn_wg2, L0_ffn_bg2, L0_ffn_We, L0_ffn_be):
    x = np.ascontiguousarray(np.asarray(L0_attn_x, dtype=np.float32))
    qkv_w = np.ascontiguousarray(np.asarray(L0_attn_qkv_w, dtype=np.float32))
    qkv_b = np.ascontiguousarray(np.asarray(L0_attn_qkv_b, dtype=np.float32)).reshape(1, 3 * D)
    out_w = np.ascontiguousarray(np.asarray(L0_attn_out_w, dtype=np.float32))
    out_b = np.ascontiguousarray(np.asarray(L0_attn_out_b, dtype=np.float32)).reshape(1, D)
    wg1 = np.ascontiguousarray(np.asarray(L0_ffn_wg1, dtype=np.float32))
    bg1 = np.ascontiguousarray(np.asarray(L0_ffn_bg1, dtype=np.float32)).reshape(1, 256)
    wg2 = np.ascontiguousarray(np.asarray(L0_ffn_wg2, dtype=np.float32))
    bg2 = np.ascontiguousarray(np.asarray(L0_ffn_bg2, dtype=np.float32)).reshape(1, 40)
    We = np.ascontiguousarray(np.asarray(L0_ffn_We, dtype=np.float32)).reshape(NE * D, D)
    be = np.ascontiguousarray(np.asarray(L0_ffn_be, dtype=np.float32))

    qkv_wT = np.ascontiguousarray(qkv_w.T)
    out_wT = np.ascontiguousarray(out_w.T)
    identf = np.eye(128, dtype=np.float32)
    identb = np.eye(128, dtype=ml_dtypes.bfloat16)
    triu = np.triu(np.ones((128, 128), dtype=np.float32)).astype(ml_dtypes.bfloat16)

    nc = build_program()
    in_maps = []
    for c in range(NCORES):
        b, half = c // 2, c % 2
        msk = np.zeros((1, 8), dtype=np.float32)
        if half == 1:
            msk[0, c - 1] = 1.0
        in_maps.append({
            "xin": np.ascontiguousarray(x[b, half * T:(half + 1) * T]),
            "qkvw": qkv_w, "qkvb": qkv_b, "outw": out_w, "outb": out_b,
            "qkvwT": qkv_wT, "outwT": out_wT,
            "wg1": wg1, "bg1": bg1, "wg2": wg2, "bg2": bg2,
            "wed": We, "bed": be,
            "identf": identf, "identb": identb, "triu": triu,
            "maskrow": msk,
        })
    res = run_bass_kernel_spmd(nc, in_maps, core_ids=list(range(NCORES)))
    if DEBUG_DUMPS:
        DBG.clear()
        DBG.update({k: [res.results[c][k] for c in range(NCORES)]
                    for k in res.results[0] if k.startswith("dbg_")})
    out = np.empty((4, 4096, D), dtype=np.float32)
    for c in range(NCORES):
        b, half = c // 2, c % 2
        out[b, half * T:(half + 1) * T] = res.results[c]["y"]
    return out
